# version 07
# baseline (speedup 1.0000x reference)
import math
import numpy as np
import ml_dtypes
import concourse.bass as bass
import concourse.mybir as mybir
from concourse.bass_utils import run_bass_kernel_spmd
from concourse.bass_core_rust import IndirectOffsetOnAxis
from contextlib import ExitStack

F32 = mybir.dt.float32
I32 = mybir.dt.int32
BF16 = mybir.dt.bfloat16
AF = mybir.ActivationFunctionType
ALU = mybir.AluOpType
AX = mybir.AxisListType

BF = ml_dtypes.bfloat16
NEGM = -1.0e30
D = 2048
T = 2048
RQ = 1024
RK = 512
NE = 9
NCORE = 8
SEQ = 2
NK = 16
EPS = 1e-5
GRP = 256
NT = 23
NTG = NT * GRP

ENGS = ("sync", "gpsimd", "tensor", "vector", "scalar")


class Ctr:
    __slots__ = ("name", "n", "h")

    def __init__(self, name):
        self.name = name
        self.n = 0
        self.h = None


class Kb:
    def __init__(self):
        self.ctrs = []
        self.items = {e: [] for e in ENGS}
        self.ecnt = {e: self.ctr("c_" + e) for e in ("tensor", "vector", "scalar")}
        self._bar = {e: () for e in ENGS}

    def ctr(self, name):
        c = Ctr(name)
        self.ctrs.append(c)
        return c

    def op(self, eng, fn, waits=(), inc=None, incn=1):
        ws = tuple((c, v) for (c, v) in waits if v > 0)
        if self._bar[eng]:
            ws = self._bar[eng] + ws
            self._bar[eng] = ()
        self.items[eng].append((fn, ws, inc, incn))
        if inc is not None:
            inc.n += incn

    def cop(self, eng, fn, waits=(), count=True):
        c = self.ecnt[eng]
        self.op(eng, fn, waits, inc=c if count else None, incn=1)
        return c.n

    def barrier(self):
        snap = tuple((c, c.n) for c in self.ctrs if c.n > 0)
        for e in ENGS:
            self._bar[e] = snap

    def emit(self, nc, block, es):
        for c in self.ctrs:
            c.h = es.enter_context(nc.semaphore(c.name))
        for e in ENGS:
            items = self.items[e]

            def make(lst):
                def run(eng):
                    for fn, ws, inc, incn in lst:
                        for (c, v) in ws:
                            eng.wait_ge(c.h, v)
                        inst = fn(eng) if fn is not None else None
                        if inc is not None:
                            tgt = inst if inst is not None else eng.nop()
                            tgt.then_inc(inc.h, incn)
                return run

            getattr(block, e)(make(items))


class DStream:

    def __init__(self, kb, name, nbuf):
        self.kb = kb
        self.sems = [kb.ctr(f"{name}{i}") for i in range(nbuf)]

    def load(self, eng, par, fn, waits=()):
        s = self.sems[par]
        self.kb.op(eng, fn, waits, inc=s, incn=16)
        return (s, s.n)

    def totals(self):
        return tuple((s, s.n) for s in self.sems if s.n > 0)


class Bank:
    __slots__ = ("ap", "free")

    def __init__(self, ap):
        self.ap = ap
        self.free = ()


def build_program(stage=2):
    nc = bass.Bass("TRN2", target_bir_lowering=False)

    x_in = nc.dram_tensor("x", [SEQ, T, D], F32, kind="ExternalInput").ap()
    wqdT = nc.dram_tensor("wqdT", [128, NK, RQ], F32, kind="ExternalInput").ap()
    wkvdT = nc.dram_tensor("wkvdT", [128, NK, RK], F32, kind="ExternalInput").ap()
    woT = nc.dram_tensor("woT", [128, NK, D], F32, kind="ExternalInput").ap()
    wqu = nc.dram_tensor("wqu", [D, RQ], F32, kind="ExternalInput").ap()
    wku = nc.dram_tensor("wku", [D, RK], F32, kind="ExternalInput").ap()
    wvu = nc.dram_tensor("wvu", [D, RK], F32, kind="ExternalInput").ap()
    bo_p = nc.dram_tensor("bo_p", [128, NK], F32, kind="ExternalInput").ap()
    gateT_p = nc.dram_tensor("gateT_p", [128, NK, 8], F32, kind="ExternalInput").ap()
    eb_p = nc.dram_tensor("eb_p", [128, 8], F32, kind="ExternalInput").ap()
    w13p = nc.dram_tensor("w13p", [NE, 8, 128, 2, NK, 128], F32, kind="ExternalInput").ap()
    w2p = nc.dram_tensor("w2p", [NK, 128, NE, 8, 128], F32, kind="ExternalInput").ap()
    idf_p = nc.dram_tensor("idf", [128, 128], F32, kind="ExternalInput").ap()
    idb_p = nc.dram_tensor("idb", [128, 128], BF16, kind="ExternalInput").ap()
    maskb_p = nc.dram_tensor("maskb", [128, 128], BF16, kind="ExternalInput").ap()
    iota_p = nc.dram_tensor("iota_in", [128, 1], F32, kind="ExternalInput").ap()
    grprow_p = nc.dram_tensor("grprow", [1, 32], F32, kind="ExternalInput").ap()
    y_out = nc.dram_tensor("y", [SEQ, T, D], F32, kind="ExternalOutput").ap()

    w13s = nc.dram_tensor("w13s", [NE, 8, 128, 2, NK, 128], BF16, kind="Internal").ap()
    w2s = nc.dram_tensor("w2s", [NK, 128, NE, 8, 128], BF16, kind="Internal").ap()
    x1s = nc.dram_tensor("x1s", [SEQ, T, D], F32, kind="Internal").ap()
    xn2s_t = [nc.dram_tensor(f"xn2s{i}", [T, D], BF16, kind="Internal").ap()
              for i in range(SEQ)]
    rtg_t = [nc.dram_tensor(f"rtg{i}", [NTG, 2], F32, kind="Internal").ap()
             for i in range(SEQ)]

    kb = Kb()
    tc, vc, sc = kb.ecnt["tensor"], kb.ecnt["vector"], kb.ecnt["scalar"]

    es_outer = ExitStack()
    sb = lambda name, shape, dt: es_outer.enter_context(nc.sbuf_tensor(name, shape, dt))
    ps = lambda name, shape, dt: es_outer.enter_context(nc.psum_tensor(name, shape, dt))

    idf = sb("idf_sb", [128, 128], F32)
    idb = sb("idb_sb", [128, 128], BF16)
    maskb = sb("maskb_sb", [128, 128], BF16)
    bo_sb = sb("bo_sb", [128, NK], F32)
    eb_sb = sb("eb_sb", [128, 8], F32)
    gateT_sb = sb("gateT_sb", [128, NK, 8], BF16)
    ones_sb = sb("ones_sb", [1, 128], BF16)
    onesf_sb = sb("onesf_sb", [1, 128], F32)
    eps_sb = sb("eps_sb", [128, 1], F32)
    iota_sb = sb("iota_sb", [128, 1], F32)
    iota9_sb = sb("iota9_sb", [128, 1], F32)
    grprow_sb = sb("grprow_sb", [1, 32], F32)

    bkA = Bank(ps("psA", [128, 512], F32).ap())
    bkB = Bank(ps("psB", [128, 512], F32).ap())
    bkC = Bank(ps("psC", [128, 512], F32).ap())
    bkD = Bank(ps("psD", [128, 512], F32).ap())
    bkE = Bank(ps("psE", [128, 512], F32).ap())
    bkF = Bank(ps("psF", [128, 512], F32).ap())
    bkT0 = Bank(ps("psT0", [128, 1024], BF16).ap())
    bkT1 = Bank(ps("psT1", [128, 1024], BF16).ap())

    ld1 = DStream(kb, "ld1", 1)
    for dst, src in ((idf, idf_p), (idb, idb_p), (maskb, maskb_p),
                     (bo_sb, bo_p), (eb_sb, eb_p), (iota_sb, iota_p),
                     (grprow_sb, grprow_p)):
        ld1.load("sync", 0, lambda e, d=dst, s=src: e.dma_start(out=d[:], in_=s))
    ld1.load("gpsimd", 0, lambda e: e.dma_start(out=gateT_sb[:], in_=gateT_p))
    bregs = {}

    def _setup_regs(e):
        bregs["tok"] = e.alloc_register("bnd_tok")
        bregs["tab"] = e.alloc_register("bnd_tab")
        bregs["w13"] = e.alloc_register("bnd_w13")
        bregs["w2"] = e.alloc_register("bnd_w2")
        e.reg_alu(bregs["tok"], T - 1, 0, ALU.add)
        e.reg_alu(bregs["w13"], NE * 8 * 128 - 1, 0, ALU.add)
        e.reg_alu(bregs["w2"], NK * 128 * NE - 1, 0, ALU.add)
        return e.reg_alu(bregs["tab"], NTG - 1, 0, ALU.add)

    kb.op("gpsimd", _setup_regs)
    kb.cop("vector", lambda e: e.memset(ones_sb[:], 1.0))
    kb.cop("vector", lambda e: e.memset(onesf_sb[:], 1.0))
    kb.cop("vector", lambda e: e.memset(eps_sb[:], EPS))
    kb.cop("vector", lambda e: e.tensor_scalar(
        iota9_sb[:], iota_sb[:], 9.0, None, op0=ALU.mult), list(ld1.totals()))
    kb.barrier()

    es_prep = ExitStack()
    prep_buf = es_prep.enter_context(nc.sbuf_tensor("prep_buf", [128, 2, 4608], BF16))
    prep_ld = DStream(kb, "pld", 2)
    prep_st = DStream(kb, "pst", 2)
    w13p_f = w13p.rearrange("e h p u k f -> (e h) p (u k f)")
    w13s_f = w13s.rearrange("e h p u k f -> (e h) p (u k f)")
    w2p_f = w2p.rearrange("d p e h f -> (d p) (e h f)")
    w2s_f = w2s.rearrange("d p e h f -> (d p) (e h f)")
    prep_jobs = [(w13p_f[i], w13s_f[i], 4096) for i in range(72)]
    for i in range(16):
        for hf in range(2):
            prep_jobs.append((w2p_f[128 * i:128 * (i + 1), 4608 * hf:4608 * (hf + 1)],
                              w2s_f[128 * i:128 * (i + 1), 4608 * hf:4608 * (hf + 1)],
                              4608))
    prep_state = {"i": 0}

    def pump(n=1):
        for _ in range(n):
            i = prep_state["i"]
            if i >= len(prep_jobs):
                return
            prep_state["i"] = i + 1
            par = i % 2
            src, dst, width = prep_jobs[i]
            waits = []
            if i >= 2:
                waits.append((prep_st.sems[par], prep_st.sems[par].n))
            ltok = prep_ld.load("gpsimd", par,
                                lambda e, s=src, p=par, w=width: e.dma_start(
                                    out=prep_buf[:, p, 0:w], in_=s), waits)
            prep_st.load("sync", par,
                         lambda e, d=dst, p=par, w=width: e.dma_start(
                             out=d, in_=prep_buf[:, p, 0:w]), [ltok])

    def phase_A(s, QlatT, CkvT):
        esA = ExitStack()
        sbA = lambda nm, shape, dt: esA.enter_context(nc.sbuf_tensor(f"{nm}_a{s}", shape, dt))
        xtile = sbA("xtile", [128, 2, D], F32)
        xn_bf = sbA("xn_bf", [128, 2, D], BF16)
        xnT = sbA("xnT", [128, NK, 512], BF16)
        stats = sbA("astat", [128, 2, 8], F32)
        wqd_sb = sbA("wqd_sb", [128, NK, RQ], BF16)
        wkvd_sb = sbA("wkvd_sb", [128, NK, RK], BF16)

        xA = DStream(kb, f"xA{s}", 2)
        wA = DStream(kb, f"wA{s}", 1)
        wA.load("gpsimd", 0, lambda e: e.dma_start(out=wqd_sb[:], in_=wqdT))
        tok_w = wA.load("gpsimd", 0, lambda e: e.dma_start(out=wkvd_sb[:], in_=wkvdT))
        xt_free = [(), ()]
        qkv_banks = [bkA, bkB]
        tp_banks = [bkT0, bkT1]
        rot = [0, 0]

        for cg in range(4):
            for t in range(4):
                i = 4 * cg + t
                par = i % 2
                tok = xA.load("sync", par, lambda e, p=par, r=512 * cg + 128 * t:
                              e.dma_start(out=xtile[:, p], in_=x_in[s, r:r + 128, :]),
                              xt_free[par])
                _ln(kb, xtile[:, par], xn_bf[:, par], stats[:, par], xn_bf[:, par], eps_sb[:, 0:1], tok,
                    [(tc, tc.n)])
                xt_free[par] = [(vc, vc.n)]
                xn_done = vc.n
                for k in range(NK):
                    bk = tp_banks[rot[0] % 2]
                    rot[0] += 1
                    cap = kb.cop("tensor", lambda e, b=bk, p=par, kk=k: e.transpose(
                        b.ap[:, 0:128], xn_bf[:, p, 128 * kk:128 * (kk + 1)], idb[:]),
                        [(vc, xn_done)] + list(bk.free))
                    kb.cop("vector", lambda e, b=bk, kk=k, tt=t: e.tensor_copy(
                        xnT[:, kk, 128 * tt:128 * (tt + 1)], b.ap[:, 0:128]), [(tc, cap)])
                    bk.free = ((vc, vc.n),)
            xnT_ready = vc.n
            pump(4)
            for rt in range(12):
                q_side = rt < 8
                wsel = wqd_sb if q_side else wkvd_sb
                col = rt if q_side else rt - 8
                bk = qkv_banks[rot[1] % 2]
                rot[1] += 1
                for k in range(NK):
                    waits = ([(vc, xnT_ready), tok_w] + list(bk.free)) if k == 0 else []
                    kb.cop("tensor", lambda e, b=bk, w=wsel, c=col, kk=k:
                           e.matmul(b.ap[:], w[:, kk, 128 * c:128 * (c + 1)],
                                    xnT[:, kk, :], start=(kk == 0), stop=(kk == NK - 1)),
                           waits, count=(k == NK - 1))
                cap = tc.n
                dst = QlatT if q_side else CkvT
                kb.cop("vector", lambda e, b=bk, d_=dst, c=col, g=cg: e.tensor_copy(
                    d_[:, c, 512 * g:512 * (g + 1)], b.ap[:]), [(tc, cap)])
                bk.free = ((vc, vc.n),)
        esA.close()

    def phase_B(s, QlatT, CkvT, ctxT):
        esB = ExitStack()
        sbB = lambda nm, shape, dt: esB.enter_context(nc.sbuf_tensor(f"{nm}_b{s}", shape, dt))
        wqh = sbB("wqh", [128, 2, RQ], BF16)
        wkh = sbB("wkh", [128, 2, RK], BF16)
        wvh = sbB("wvh", [128, 2, RK], BF16)
        uvT = sbB("uvT", [128, 4, 128], BF16)
        mm_sb = sbB("mm_sb", [128, 8, 512], BF16)
        qhT = sbB("qhT", [128, 4, T], BF16)
        vup = sbB("vup", [128, NK, 128], BF16)
        p_sb = sbB("p_sb", [128, 2, T], BF16)
        pt4 = sbB("pt4", [128, NK, 512], BF16)
        mst = sbB("mst", [128, 8], F32)
        sstat = sbB("sstat", [128, 4], F32)

        whl = DStream(kb, f"wh{s}", 2)
        wh_free = [(), ()]
        lg_banks = [bkA, bkB, bkC, bkD, bkE, bkF]
        tp_banks = [bkT0, bkT1]
        rot = {"lg": 0, "tp": 0}
        prev_pv = [0]

        for h in range(16):
            par = h % 2
            whl.load("gpsimd", par, lambda e, p=par, hh=h: e.dma_start(
                out=wqh[:, p], in_=wqu[128 * hh:128 * (hh + 1), :]), wh_free[par])
            whl.load("gpsimd", par, lambda e, p=par, hh=h: e.dma_start(
                out=wkh[:, p], in_=wku[128 * hh:128 * (hh + 1), :]))
            tok_h = whl.load("gpsimd", par, lambda e, p=par, hh=h: e.dma_start(
                out=wvh[:, p], in_=wvu[128 * hh:128 * (hh + 1), :]))
            pump(6)
            for c in range(4):
                bk = tp_banks[rot["tp"] % 2]
                rot["tp"] += 1
                cap = kb.cop("tensor", lambda e, b=bk, p=par, cc=c: e.transpose(
                    b.ap[:, 0:128], wvh[:, p, 128 * cc:128 * (cc + 1)], idb[:]),
                    [tok_h] + list(bk.free))
                kb.cop("vector", lambda e, b=bk, cc=c: e.tensor_copy(
                    uvT[:, cc, :], b.ap[:, 0:128]), [(tc, cap)])
                bk.free = ((vc, vc.n),)
            for j in range(8):
                bk = lg_banks[rot["lg"] % 6]
                rot["lg"] += 1
                cap = kb.cop("tensor", lambda e, b=bk, p=par, jj=j: e.matmul(
                    b.ap[:], wqh[:, p, 128 * jj:128 * (jj + 1)], wkh[:, p],
                    start=True, stop=True), [tok_h] + list(bk.free))
                kb.cop("vector", lambda e, b=bk, jj=j: e.tensor_scalar(
                    mm_sb[:, jj, :], b.ap[:], 1.0 / math.sqrt(128.0), None,
                    op0=ALU.mult), [(tc, cap)])
                bk.free = ((vc, vc.n),)
            mm_done = vc.n
            wh_free[par] = [(tc, tc.n)]
            for rt in range(4):
                for g4 in range(4):
                    bk = lg_banks[rot["lg"] % 6]
                    rot["lg"] += 1
                    for j in range(8):
                        waits = ([(vc, mm_done)] + list(bk.free)) if j == 0 else []
                        kb.cop("tensor", lambda e, b=bk, r=rt, g=g4, jj=j, Q=QlatT:
                               e.matmul(b.ap[:], mm_sb[:, jj, 128 * r:128 * (r + 1)],
                                        Q[:, jj, 512 * g:512 * (g + 1)],
                                        start=(jj == 0), stop=(jj == 7)),
                               waits, count=(j == 7))
                    cap = tc.n
                    kb.cop("vector", lambda e, b=bk, r=rt, g=g4: e.tensor_copy(
                        qhT[:, r, 512 * g:512 * (g + 1)], b.ap[:]), [(tc, cap)])
                    bk.free = ((vc, vc.n),)
            uv_done = vc.n
            for st in range(NK):
                bk = lg_banks[rot["lg"] % 6]
                rot["lg"] += 1
                for c in range(4):
                    waits = ([(vc, uv_done)] + list(bk.free)) if c == 0 else []
                    kb.cop("tensor", lambda e, b=bk, s_=st, cc=c, C=CkvT: e.matmul(
                        b.ap[:, 0:128], C[:, cc, 128 * s_:128 * (s_ + 1)],
                        uvT[:, cc, :], start=(cc == 0), stop=(cc == 3)),
                        waits, count=(c == 3))
                cap = tc.n
                kb.cop("vector", lambda e, b=bk, s_=st: e.tensor_copy(
                    vup[:, s_, :], b.ap[:, 0:128]), [(tc, cap)])
                bk.free = ((vc, vc.n),)
            qh_done = vc.n

            def emit_tr(tq, p_done):
                par = tq % 2
                g = tq // 4
                m = tq - 4 * g
                for st in range(tq + 1):
                    bk = tp_banks[rot["tp"] % 2]
                    rot["tp"] += 1
                    cap = kb.cop("tensor", lambda e, b=bk, s_=st, p=par: e.transpose(
                        b.ap[:, 0:128], p_sb[:, p, 128 * s_:128 * (s_ + 1)], idb[:]),
                        [(vc, p_done)] + list(bk.free))
                    kb.cop("vector", lambda e, b=bk, s_=st, mm=m: e.tensor_copy(
                        pt4[:, s_, 128 * mm:128 * (mm + 1)], b.ap[:, 0:128]),
                        [(tc, cap)])
                    bk.free = ((vc, vc.n),)
                for st in range(tq + 1, 4 * g + 4):
                    kb.cop("vector", lambda e, s_=st, mm=m: e.memset(
                        pt4[:, s_, 128 * mm:128 * (mm + 1)], 0.0),
                        [(tc, prev_pv[0])])
                return vc.n

            def emit_pv(g):
                bk = lg_banks[rot["lg"] % 6]
                rot["lg"] += 1
                nst = 4 * g + 4
                for st in range(nst):
                    waits = ([(vc, pt_done[0])] + list(bk.free)) if st == 0 else []
                    kb.cop("tensor", lambda e, b=bk, s_=st, n=nst: e.matmul(
                        b.ap[:], vup[:, s_, :], pt4[:, s_, :],
                        start=(s_ == 0), stop=(s_ == n - 1)),
                        waits, count=(st == nst - 1))
                prev_pv[0] = tc.n
                kb.cop("vector", lambda e, b=bk, hh=h, gg=g, X=ctxT: e.tensor_copy(
                    X[:, hh, 512 * gg:512 * (gg + 1)], b.ap[:]), [(tc, prev_pv[0])])
                bk.free = ((vc, vc.n),)

            pend = None
            pt_done = [0]
            for tq in range(16):
                g = tq // 4
                m = tq - 4 * g
                par = tq % 2
                slen = 128 * (tq + 1)
                ng = (slen + 511) // 512
                exp_banks = []
                for scg in range(ng):
                    w = min(512, slen - 512 * scg)
                    bk = lg_banks[rot["lg"] % 6]
                    rot["lg"] += 1
                    diag = (scg == ng - 1)
                    for c in range(4):
                        waits = ([(vc, qh_done)] + list(bk.free)) if c == 0 else []
                        kb.cop("tensor", lambda e, b=bk, cc=c, q=tq, sg=scg, ww=w,
                               C=CkvT, dg=diag:
                               e.matmul(b.ap[:, 0:ww],
                                        qhT[:, cc, 128 * q:128 * (q + 1)],
                                        C[:, cc, 512 * sg:512 * sg + ww],
                                        start=(cc == 0), stop=(cc == 3 and not dg)),
                               waits, count=(c == 3 and not diag))
                    if diag:
                        kb.cop("tensor", lambda e, b=bk, ww=w: e.matmul(
                            b.ap[:, ww - 128:ww], idb[:], maskb[:],
                            start=False, stop=True), [], count=True)
                    cap = tc.n
                    kb.cop("vector", lambda e, b=bk, sg=scg, ww=w: e.reduce_max(
                        mst[:, sg:sg + 1], b.ap[:, 0:ww], axis=AX.X), [(tc, cap)])
                    exp_banks.append((bk, w))
                kb.cop("vector", lambda e, n=ng: e.reduce_max(
                    sstat[:, 0:1], mst[:, 0:n], axis=AX.X, negate=True),
                    [(vc, vc.n), (sc, sc.n)])
                negm_done = vc.n
                if pend is not None:
                    pt_done[0] = emit_tr(*pend)
                if m == 0 and g > 0:
                    emit_pv(g - 1)
                for scg, (bk, w) in enumerate(exp_banks):
                    kb.cop("scalar", lambda e, b=bk, sg=scg, ww=w, p=par:
                           e.activation(
                               p_sb[:, p, 512 * sg:512 * sg + ww], b.ap[:, 0:ww],
                               AF.Exp, bias=sstat[:, 0:1],
                               accum_out=mst[:, 4 + sg:5 + sg]),
                           [(vc, negm_done)])
                    bk.free = ((sc, sc.n),)
                exp_done = sc.n
                kb.cop("vector", lambda e, n=ng: e.reduce_sum(
                    sstat[:, 1:2], mst[:, 4:4 + n], axis=AX.X), [(sc, exp_done)])
                kb.cop("vector", lambda e: e.reciprocal(sstat[:, 2:3], sstat[:, 1:2]),
                       [(vc, vc.n)])
                kb.cop("vector", lambda e, L=slen, p=par: e.tensor_scalar(
                    p_sb[:, p, 0:L], p_sb[:, p, 0:L], sstat[:, 2:3], None,
                    op0=ALU.mult), [(vc, vc.n)])
                pend = (tq, vc.n)
            pt_done[0] = emit_tr(*pend)
            emit_pv(3)
        esB.close()

    def phase_C(s, ctxT):
        esC = ExitStack()
        sbC = lambda nm, shape, dt: esC.enter_context(nc.sbuf_tensor(f"{nm}_c{s}", shape, dt))
        wo_sb = sbC("wo_sb", [128, NK, D], BF16)
        xcg = sbC("xcg", [128, 4, D], F32)
        wsb = sbC("wsbC", [128, 2, 512], F32)
        piece = sbC("pieceC", [128, 2, 128], F32)

        wC = DStream(kb, f"wC{s}", 2)
        tok_wo1 = wC.load("gpsimd", 0, lambda e: e.dma_start(
            out=wo_sb[:, :, 0:1024], in_=woT[:, :, 0:1024]))
        tok_wo2 = wC.load("gpsimd", 1, lambda e: e.dma_start(
            out=wo_sb[:, :, 1024:2048], in_=woT[:, :, 1024:2048]))
        xC = DStream(kb, f"xC{s}", 1)
        stC = DStream(kb, f"stC{s}", 2)
        acc_banks = [bkA, bkB]
        tr_banks = [bkC, bkD]
        xc_free = []
        piece_free = [(), ()]
        wsb_free = [(), ()]
        rot = {"a": 0, "t": 0, "p": 0}
        dest = x1s if stage >= 2 else y_out

        for cg in range(4):
            tokx = xC.load("sync", 0, lambda e, c=cg: e.dma_start(
                out=xcg[:],
                in_=x_in[s, 512 * c:512 * (c + 1), :].rearrange("(t p) d -> p t d", p=128)),
                list(xc_free))
            pump(2)
            for dc in range(NK):
                bk = acc_banks[rot["a"] % 2]
                rot["a"] += 1
                for k in range(NK):
                    tw = tok_wo1 if dc < 8 else tok_wo2
                    waits = ([tw] + list(bk.free)) if k == 0 else []
                    kb.cop("tensor", lambda e, b=bk, d_=dc, kk=k, c=cg, X=ctxT: e.matmul(
                        b.ap[:], wo_sb[:, kk, 128 * d_:128 * (d_ + 1)],
                        X[:, kk, 512 * c:512 * (c + 1)],
                        start=(kk == 0), stop=(kk == NK - 1)),
                        waits, count=(k == NK - 1))
                cap = tc.n
                wp = dc % 2
                kb.cop("vector", lambda e, b=bk, d_=dc, p=wp: e.tensor_scalar(
                    wsb[:, p], b.ap[:], bo_sb[:, d_:d_ + 1], None, op0=ALU.add),
                    [(tc, cap)] + list(wsb_free[wp]))
                bk.free = ((vc, vc.n),)
                wsb_done = vc.n
                for t in range(4):
                    tb = tr_banks[rot["t"] % 2]
                    rot["t"] += 1
                    capt = kb.cop("tensor", lambda e, b=tb, p=wp, tt=t: e.transpose(
                        b.ap[:, 0:128], wsb[:, p, 128 * tt:128 * (tt + 1)], idf[:]),
                        [(vc, wsb_done)] + list(tb.free))
                    pp = rot["p"] % 2
                    rot["p"] += 1
                    kb.cop("vector", lambda e, b=tb, tt=t, d_=dc, q=pp: e.tensor_add(
                        piece[:, q], b.ap[:, 0:128], xcg[:, tt, 128 * d_:128 * (d_ + 1)]),
                        [(tc, capt), tokx] + list(piece_free[pp]))
                    tb.free = ((vc, vc.n),)
                    addcap = vc.n
                    stC.load("gpsimd", pp, lambda e, q=pp, r=512 * cg + 128 * t, d_=dc:
                             e.dma_start(
                                 out=dest[s, r:r + 128, 128 * d_:128 * (d_ + 1)],
                                 in_=piece[:, q]), [(vc, addcap)])
                    piece_free[pp] = [(stC.sems[pp], stC.sems[pp].n)]
                wsb_free[wp] = [(tc, tc.n)]
            xc_free = [(vc, vc.n)]
        esC.close()
        return stC.totals()

    def phase_E(s, stC_tot, prep_tot):
        esE = ExitStack()
        sbE = lambda nm, shape, dt: esE.enter_context(nc.sbuf_tensor(f"{nm}_e{s}", shape, dt))
        xn2T = sbE("xn2T", [128, NK, 512], BF16)
        h_sb = sbE("h_sb", [128, 8, 512], BF16)
        w13buf = sbE("w13buf", [128, 3, 2 * NK * 128], BF16)
        w2buf = sbE("w2buf", [128, 3, 1024], BF16)
        scores = sbE("scores", [128, 4, 8], F32)
        gtok = sbE("gtok", [128, 8], F32)
        gtmp = sbE("gtmp", [128, 3, 8], F32)
        gm = sbE("gm", [128, 2], F32)
        xcg1 = sbE("xcg1", [128, 4, D], F32)
        xn2_bf = sbE("xn2_bf", [128, 2, D], BF16)
        stats = sbE("estat", [128, 8], F32)
        tmp1 = sbE("tmp1", [128, 2, 512], BF16)
        wsb = sbE("wsbE", [128, 2, 512], F32)
        piece = sbE("pieceE", [128, 2, 128], F32)
        crow = sbE("crow", [1, 16], F32)
        brow = sbE("brow", [1, 16], F32)
        etab = sbE("etab", [1, 64], F32)
        esT = ExitStack()
        sbT = lambda nm, shape, dt: esT.enter_context(nc.sbuf_tensor(f"{nm}_e{s}", shape, dt))
        grow = sbT("grow", [1, 8, T], BF16)
        mrow = sbT("mrow", [1, T], BF16)
        zrow = sbT("zrow", [1, T], BF16)
        prow = sbT("prow", [1, T], F32)
        fill2 = sbT("fill2", [128, NTG // 128, 2], F32)
        hugec = sbT("hugec", [128, 1], F32)
        sdata = sbT("sdata", [128, 2, 2], F32)
        scol = sbT("scol", [128, 2, 4], F32)
        sidx = sbT("sidx", [128, 2, 1], I32)
        e4b = sbT("e4b", [128, 1], F32)

        x1L = DStream(kb, f"x1L{s}", 1)
        xnS = DStream(kb, f"xnS{s}", 2)
        w13L = DStream(kb, f"w13L{s}", 2)
        w2L = DStream(kb, f"w2L{s}", 2)
        stE = DStream(kb, f"stE{s}", 2)
        tfill = DStream(kb, f"tf{s}", 1)
        tscat = DStream(kb, f"ts{s}", 2)
        rload = DStream(kb, f"rl{s}", 4)
        wg13 = DStream(kb, f"wg13{s}", 3)
        wg2 = DStream(kb, f"wg2{s}", 3)
        xgL = DStream(kb, f"xgL{s}", 4)
        stR = DStream(kb, f"stR{s}", 1)
        u1_banks = [bkA, bkC]
        u3_banks = [bkB, bkD]
        acc_banks = [bkA, bkB]
        tr_banks = [bkC, bkD]
        x1_free = []
        w13_free = [(), (), ()]
        w2_free = [(), (), ()]
        piece_free = [(), ()]
        wsb_free = [(), ()]
        xn_free = [(), ()]
        tscat_free = [(), ()]
        rot = {"u": 0, "a": 0, "t": 0, "p": 0, "ih": 0}
        w13s_rows = w13s.rearrange("e h p u k f -> (e h p) (u k f)")
        w2s_rows = w2s.rearrange("d p e h f -> (d p e) (h f)")
        xn2s_rows = xn2s_t[s]
        rtg_s = rtg_t[s]
        y_flat = y_out.rearrange("q t d -> (q t) d")

        e4_steps = []
        e4refs = {}

        def _e4_fill():
            kb.cop("vector", lambda e: e.memset(zrow[:], 0.0), [(vc, vc.n)])
            kb.cop("vector", lambda e: e.memset(fill2[:, :, 0:1], float(1 << 23)), [])
            kb.cop("vector", lambda e: e.memset(fill2[:, :, 1:2], 0.0), [])
            fill_done = vc.n
            e4refs["ftok"] = tfill.load("gpsimd", 0, lambda e: e.dma_start(
                out=rtg_s.rearrange("(a p) c -> p a c", p=128), in_=fill2[:]),
                [(vc, fill_done)])
        e4_steps.append(_e4_fill)

        def _e4_scanA(ee):
            def f():
                kb.cop("vector", lambda e, x=ee: e.tensor_scalar(
                    mrow[:], grow[0:1, x, :], 0.0, None, op0=ALU.is_gt), [(vc, vc.n)])
                kb.cop("vector", lambda e: e.tensor_tensor_scan(
                    prow[:], mrow[:], zrow[:], 0.0, op0=ALU.add, op1=ALU.add),
                    [(vc, vc.n)])
                kb.cop("vector", lambda e, x=ee: e.tensor_copy(
                    crow[0:1, x:x + 1], prow[0:1, T - 1:T]), [(vc, vc.n)])
            return f
        for _ee in range(8):
            e4_steps.append(_e4_scanA(_ee))

        def _e4_bases():
            kb.cop("vector", lambda e: e.tensor_scalar(
                crow[0:1, 8:16], crow[0:1, 0:8], float(GRP - 1), 1.0 / GRP,
                op0=ALU.add, op1=ALU.mult), [(vc, vc.n)])
            kb.cop("vector", lambda e: e.tensor_scalar(
                crow[0:1, 8:16], crow[0:1, 8:16], -0.498046875, 12582912.0,
                op0=ALU.add, op1=ALU.add), [(vc, vc.n)])
            kb.cop("vector", lambda e: e.tensor_scalar(
                crow[0:1, 8:16], crow[0:1, 8:16], -12582912.0, float(GRP),
                op0=ALU.add, op1=ALU.mult), [(vc, vc.n)])
            kb.cop("vector", lambda e: e.tensor_tensor_scan(
                brow[0:1, 1:9], crow[0:1, 8:16], zrow[0:1, 0:8], 0.0,
                op0=ALU.add, op1=ALU.add), [(vc, vc.n)])
            kb.cop("vector", lambda e: e.memset(brow[0:1, 0:1], 0.0), [(vc, vc.n)])
            kb.cop("vector", lambda e: e.memset(etab[:], 0.0), [(vc, vc.n)])
            for x in range(1, 9):
                kb.cop("vector", lambda e, x_=x: e.tensor_scalar(
                    etab[0:1, 32:32 + NT], grprow_sb[0:1, 0:NT], brow[0:1, x_:x_ + 1],
                    None, op0=ALU.is_ge), [(vc, vc.n)])
                kb.cop("vector", lambda e: e.tensor_add(
                    etab[0:1, 0:NT], etab[0:1, 0:NT], etab[0:1, 32:32 + NT]),
                    [(vc, vc.n)])
            e4refs["etab_done"] = vc.n
        e4_steps.append(_e4_bases)

        def _e4_scanB(ee):
            def f():
                kb.cop("vector", lambda e, x=ee: e.tensor_scalar(
                    mrow[:], grow[0:1, x, :], 0.0, None, op0=ALU.is_gt),
                    [(vc, vc.n)])
                kb.cop("vector", lambda e: e.tensor_tensor_scan(
                    prow[:], mrow[:], zrow[:], 0.0, op0=ALU.add, op1=ALU.add),
                    [(vc, vc.n)])
                kb.cop("vector", lambda e, x=ee: e.tensor_scalar(
                    brow[0:1, 9:10], brow[0:1, x:x + 1], -1.0, None, op0=ALU.add),
                    [(vc, vc.n)])
                capb = kb.cop("tensor", lambda e: e.matmul(
                    bkF.ap[:, 0:1], onesf_sb[:], brow[0:1, 9:10],
                    start=True, stop=True), [(vc, vc.n)] + list(bkF.free))
                kb.cop("vector", lambda e: e.tensor_copy(e4b[:], bkF.ap[:, 0:1]),
                       [(tc, capb)])
                bkF.free = ((vc, vc.n),)
            return f

        def _e4_tile(ee, t):
            def f():
                par = t % 2
                cap1 = kb.cop("tensor", lambda e, tt=t: e.transpose(
                    bkF.ap[:, 0:1], prow[0:1, 128 * tt:128 * (tt + 1)],
                    idf[0:1, 0:1]), [(vc, vc.n)] + list(bkF.free))
                kb.cop("vector", lambda e, p=par: e.tensor_copy(
                    scol[:, p, 0:1], bkF.ap[:, 0:1]),
                    [(tc, cap1)] + list(tscat_free[par]))
                bkF.free = ((vc, vc.n),)
                cap2 = kb.cop("tensor", lambda e, x=ee, tt=t: e.transpose(
                    bkT0.ap[:, 0:1], grow[0:1, x, 128 * tt:128 * (tt + 1)],
                    idb[0:1, 0:1]), list(bkT0.free))
                kb.cop("vector", lambda e, p=par: e.tensor_copy(
                    scol[:, p, 1:2], bkT0.ap[:, 0:1]), [(tc, cap2)])
                bkT0.free = ((vc, vc.n),)
                kb.cop("vector", lambda e, p=par, tt=t: e.tensor_scalar(
                    sdata[:, p, 0:1], iota_sb[:], float(128 * tt), None, op0=ALU.add),
                    [(vc, vc.n)])
                kb.cop("vector", lambda e, p=par: e.tensor_copy(
                    sdata[:, p, 1:2], scol[:, p, 1:2]), [(vc, vc.n)])
                kb.cop("vector", lambda e, p=par: e.tensor_add(
                    scol[:, p, 2:3], scol[:, p, 0:1], e4b[:]), [(vc, vc.n)])
                kb.cop("vector", lambda e, p=par: e.tensor_scalar(
                    scol[:, p, 3:4], scol[:, p, 1:2], 0.0, None, op0=ALU.is_gt),
                    [(vc, vc.n)])
                kb.cop("vector", lambda e, p=par: e.tensor_scalar(
                    scol[:, p, 2:3], scol[:, p, 2:3], -float(1 << 23), None,
                    op0=ALU.add), [(vc, vc.n)])
                kb.cop("vector", lambda e, p=par: e.scalar_tensor_tensor(
                    scol[:, p, 2:3], scol[:, p, 2:3], 1.0, scol[:, p, 3:4],
                    op0=ALU.mult, op1=ALU.mult), [(vc, vc.n)])
                kb.cop("vector", lambda e, p=par: e.tensor_scalar(
                    scol[:, p, 2:3], scol[:, p, 2:3], float(1 << 23), None,
                    op0=ALU.add), [(vc, vc.n)])
                kb.cop("vector", lambda e, p=par: e.tensor_copy(
                    sidx[:, p], scol[:, p, 2:3]), [(vc, vc.n)])
                dcap = vc.n
                stok = tscat.load("gpsimd", par, lambda e, p=par: e.indirect_dma_start(
                    out=rtg_s, out_offset=IndirectOffsetOnAxis(ap=sidx[:, p], axis=0),
                    in_=sdata[:, p], in_offset=None,
                    bounds_check=bregs["tab"], oob_is_err=False),
                    [(vc, dcap), e4refs["ftok"]])
                tscat_free[par] = [stok]
            return f
        for _ee in range(8):
            e4_steps.append(_e4_scanB(_ee))
            for _t in range(NK):
                e4_steps.append(_e4_tile(_ee, _t))
        e4_state = {"i": 0}

        def e4pump(n):
            for _ in range(n):
                i = e4_state["i"]
                if i >= len(e4_steps):
                    return
                e4_state["i"] = i + 1
                e4_steps[i]()

        for grp in range(4):
            tokx = x1L.load("sync", 0, lambda e, g=grp: e.dma_start(
                out=xcg1[:],
                in_=x1s[s, 512 * g:512 * (g + 1), :].rearrange("(t p) d -> p t d", p=128)),
                list(stC_tot) + list(x1_free))
            tile_caps = []
            for t in range(4):
                par = t % 2
                xnd = _ln(kb, xcg1[:, t, :], xn2_bf[:, par], stats, xn2_bf[:, par],
                          eps_sb[:, 0:1], tokx, [(tc, tc.n)] + list(xn_free[par]))
                xnS.load("gpsimd", par, lambda e, p=par, r=512 * grp + 128 * t:
                         e.dma_start(out=xn2s_rows[r:r + 128, :], in_=xn2_bf[:, p]),
                         [(vc, xnd)])
                xn_free[par] = [(xnS.sems[par], xnS.sems[par].n)]
                for k in range(NK):
                    bk = bkT0 if (t * NK + k) % 2 == 0 else bkT1
                    cap = kb.cop("tensor", lambda e, b=bk, kk=k, p=par: e.transpose(
                        b.ap[:, 0:128], xn2_bf[:, p, 128 * kk:128 * (kk + 1)], idb[:]),
                        [(vc, xnd)] + list(bk.free))
                    kb.cop("vector", lambda e, b=bk, kk=k, tt=t: e.tensor_copy(
                        xn2T[:, kk, 128 * tt:128 * (tt + 1)], b.ap[:, 0:128]), [(tc, cap)])
                    bk.free = ((vc, vc.n),)
                tile_caps.append(vc.n)
            for t in range(4):
                for k in range(NK):
                    waits = ([(vc, tile_caps[t])] + list(bkE.free)) if k == 0 else []
                    kb.cop("tensor", lambda e, kk=k, tt=t: e.matmul(
                        bkE.ap[:, 0:8], xn2T[:, kk, 128 * tt:128 * (tt + 1)],
                        gateT_sb[:, kk, :], start=(kk == 0), stop=(kk == NK - 1)),
                        waits, count=(k == NK - 1))
                cap = tc.n
                kb.cop("scalar", lambda e, tt=t: e.activation(
                    scores[:, tt, :], bkE.ap[:, 0:8], AF.Sigmoid), [(tc, cap), (vc, vc.n)])
                bkE.free = ((sc, sc.n),)
                sgd = sc.n
                kb.cop("vector", lambda e, tt=t: e.tensor_add(
                    gtmp[:, 0, :], scores[:, tt, :], eb_sb[:]), [(sc, sgd), (vc, vc.n)])
                kb.cop("vector", lambda e: e.reduce_max(gm[:, 0:1], gtmp[:, 0, :],
                                                        axis=AX.X), [(vc, vc.n)])
                kb.cop("vector", lambda e: e.tensor_scalar(
                    gtmp[:, 1, :], gtmp[:, 0, :], gm[:, 0:1], None, op0=ALU.is_equal),
                    [(vc, vc.n)])
                kb.cop("vector", lambda e: e.scalar_tensor_tensor(
                    gtmp[:, 2, :], gtmp[:, 1, :], NEGM, gtmp[:, 0, :],
                    op0=ALU.mult, op1=ALU.add), [(vc, vc.n)])
                kb.cop("vector", lambda e: e.reduce_max(gm[:, 1:2], gtmp[:, 2, :],
                                                        axis=AX.X), [(vc, vc.n)])
                kb.cop("vector", lambda e: e.tensor_scalar(
                    gtmp[:, 2, :], gtmp[:, 2, :], gm[:, 1:2], None, op0=ALU.is_equal),
                    [(vc, vc.n)])
                kb.cop("vector", lambda e: e.tensor_add(
                    gtmp[:, 1, :], gtmp[:, 1, :], gtmp[:, 2, :]), [(vc, vc.n)])
                kb.cop("vector", lambda e, tt=t: e.tensor_mul(
                    gtok[:], scores[:, tt, :], gtmp[:, 1, :]), [(vc, vc.n), (tc, tc.n)])
                gdone = vc.n
                for ee in range(8):
                    capg = kb.cop("tensor", lambda e, x=ee: e.transpose(
                        bkF.ap[0:1, 0:128], gtok[:, x:x + 1], idf[:]),
                        [(vc, gdone)] + list(bkF.free))
                    kb.cop("vector", lambda e, x=ee, c0=512 * grp + 128 * t: e.tensor_copy(
                        grow[0:1, x, c0:c0 + 128], bkF.ap[0:1, 0:128]), [(tc, capg)])
                    bkF.free = ((vc, vc.n),)
            for hc in range(8):
                ih = rot["ih"]
                rot["ih"] += 1
                wp = ih % 2
                tokw = w13L.load("sync", wp, lambda e, p=wp, hh=hc: e.dma_start(
                    out=w13buf[:, p], in_=w13s[8, hh]),
                    list(prep_tot) + list(w13_free[wp]))
                b1 = u1_banks[rot["u"] % 2]
                b3 = u3_banks[rot["u"] % 2]
                rot["u"] += 1
                for k in range(NK):
                    waits = ([tokw, (vc, tile_caps[3])] + list(b1.free)) if k == 0 else []
                    kb.cop("tensor", lambda e, b=b1, p=wp, kk=k: e.matmul(
                        b.ap[:], w13buf[:, p, 128 * kk:128 * (kk + 1)], xn2T[:, kk, :],
                        start=(kk == 0), stop=(kk == NK - 1)),
                        waits, count=(k == NK - 1))
                cap1 = tc.n
                for k in range(NK):
                    waits = list(b3.free) if k == 0 else []
                    kb.cop("tensor", lambda e, b=b3, p=wp, kk=k: e.matmul(
                        b.ap[:], w13buf[:, p, 2048 + 128 * kk:2048 + 128 * (kk + 1)], xn2T[:, kk, :],
                        start=(kk == 0), stop=(kk == NK - 1)),
                        waits, count=(k == NK - 1))
                cap3 = tc.n
                w13_free[wp] = [(tc, cap3)]
                hp = hc % 2
                kb.cop("scalar", lambda e, b=b1, p=hp: e.activation(
                    tmp1[:, p], b.ap[:], AF.Silu), [(tc, cap1), (vc, vc.n), (sc, sc.n)])
                b1.free = ((sc, sc.n),)
                sil = sc.n
                kb.cop("vector", lambda e, b=b3, hh=hc, p=hp: e.tensor_mul(
                    h_sb[:, hh, :], b.ap[:], tmp1[:, p]),
                    [(tc, cap3), (sc, sil), (vc, vc.n)])
                b3.free = ((vc, vc.n),)
                if grp == 3:
                    e4pump(8)
            h_done = vc.n
            pendw = None

            def emit_epi(dc, wsb_done):
                sp = dc % 2
                for t in range(4):
                    tb = tr_banks[rot["t"] % 2]
                    rot["t"] += 1
                    capt = kb.cop("tensor", lambda e, b=tb, p=sp, tt=t: e.transpose(
                        b.ap[:, 0:128], wsb[:, p, 128 * tt:128 * (tt + 1)], idf[:]),
                        [(vc, wsb_done)] + list(tb.free))
                    pp = rot["p"] % 2
                    rot["p"] += 1
                    kb.cop("vector", lambda e, b=tb, tt=t, d_=dc, q=pp: e.tensor_add(
                        piece[:, q], b.ap[:, 0:128], xcg1[:, tt, 128 * d_:128 * (d_ + 1)]),
                        [(tc, capt), tokx] + list(piece_free[pp]))
                    tb.free = ((vc, vc.n),)
                    addcap = vc.n
                    stE.load("gpsimd", pp, lambda e, q=pp, r=512 * grp + 128 * t, d_=dc:
                             e.dma_start(
                                 out=y_out[s, r:r + 128, 128 * d_:128 * (d_ + 1)],
                                 in_=piece[:, q]), [(vc, addcap)])
                    piece_free[pp] = [(stE.sems[pp], stE.sems[pp].n)]
                wsb_free[sp] = [(tc, tc.n)]

            for dc in range(NK):
                wp = dc % 2
                tokw2 = w2L.load("sync", wp, lambda e, p=wp, d_=dc: e.dma_start(
                    out=w2buf[:, p], in_=w2s[d_, :, 8]), list(prep_tot) + list(w2_free[wp]))
                bk = acc_banks[rot["a"] % 2]
                rot["a"] += 1
                for hc in range(8):
                    waits = ([(vc, h_done), tokw2] + list(bk.free)) if hc == 0 else []
                    kb.cop("tensor", lambda e, b=bk, p=wp, hh=hc: e.matmul(
                        b.ap[:], w2buf[:, p, 128 * hh:128 * (hh + 1)], h_sb[:, hh, :],
                        start=(hh == 0), stop=(hh == 7)),
                        waits, count=(hc == 7))
                cap = tc.n
                w2_free[wp] = [(tc, cap)]
                sp = dc % 2
                kb.cop("vector", lambda e, b=bk, p=sp: e.tensor_copy(
                    wsb[:, p], b.ap[:]), [(tc, cap)] + list(wsb_free[sp]))
                bk.free = ((vc, vc.n),)
                if pendw is not None:
                    emit_epi(*pendw)
                pendw = (dc, vc.n)
                if grp == 3:
                    e4pump(12)
            emit_epi(*pendw)
            x1_free = [(vc, vc.n)]
        scores_done = vc.n
        e4pump(len(e4_steps))
        xn_store_tot = xnS.totals()
        shared_store_tot = stE.totals()
        tscat_tot = tscat.totals()
        esT.close()
        xg = sbE("xg", [128, 2, 2, D], BF16)
        xgT = sbE("xgT", [128, NK, 256], BF16)
        yr = sbE("yr", [128, 2, 2, D], F32)
        rtgt = sbE("rtgt", [128, 2, 2, 2], F32)
        tidx = sbE("tidx", [128, 2, 2, 1], I32)
        gvc = sbE("gvc", [128, 2, 2, 1], F32)
        ebtab = sbE("ebtab", [128, 32], F32)
        etab_bf = sbE("etab_bf", [1, 32], BF16)
        w13tf = sbE("w13tf", [128, 8, NT], F32)
        w2tf = sbE("w2tf", [128, NK, NT], F32)
        w13ti = sbE("w13ti", [128, 8, NT, 1], I32)
        w2ti = sbE("w2ti", [128, NK, NT, 1], I32)

        kb.cop("scalar", lambda e: e.copy(etab_bf[:], etab[0:1, 0:32]),
               [(vc, e4refs["etab_done"])])
        capb = kb.cop("tensor", lambda e: e.matmul(
            bkF.ap[:, 0:32], ones_sb[:], etab_bf[:], start=True, stop=True),
            [(sc, sc.n)] + list(bkF.free))
        kb.cop("vector", lambda e: e.tensor_copy(ebtab[:], bkF.ap[:, 0:32]),
               [(tc, capb), (vc, vc.n)])
        bkF.free = ((vc, vc.n),)
        kb.cop("vector", lambda e: e.tensor_scalar(
            w13tf[:, 0, :], ebtab[:, 0:NT], 1024.0, iota_sb[:, 0:1],
            op0=ALU.mult, op1=ALU.add), [(vc, vc.n)])
        for hh in range(1, 8):
            kb.cop("vector", lambda e, h_=hh: e.tensor_scalar(
                w13tf[:, h_, :], w13tf[:, 0, :], float(128 * h_), None,
                op0=ALU.add), [(vc, vc.n)])
        kb.cop("vector", lambda e: e.tensor_scalar(
            w2tf[:, 0, :], ebtab[:, 0:NT], 1.0, iota9_sb[:, 0:1],
            op0=ALU.mult, op1=ALU.add), [(vc, vc.n)])
        for d_ in range(1, NK):
            kb.cop("vector", lambda e, dd=d_: e.tensor_scalar(
                w2tf[:, dd, :], w2tf[:, 0, :], float(1152 * dd), None,
                op0=ALU.add), [(vc, vc.n)])
        kb.cop("vector", lambda e: e.tensor_copy(
            w13ti.rearrange("p h t o -> p (h t o)"),
            w13tf.rearrange("p h t -> p (h t)")), [(vc, vc.n)])
        kb.cop("vector", lambda e: e.tensor_copy(
            w2ti.rearrange("p d t o -> p (d t o)"),
            w2tf.rearrange("p d t -> p (d t)")), [(vc, vc.n)])
        tabs_done = vc.n

        kb.cop("vector", lambda e: e.memset(xg[:], 0.0), [(vc, vc.n)])
        xg_clear = vc.n
        prev_scat = []
        xg_free = {}
        rl_free = [(), ()]
        tok_free = [(), ()]
        yr_free = [(), ()]
        for gi in range(NT):
            gp = gi % 2
            rtoks = []
            for t in range(2):
                rtoks.append(rload.load("sync", 2 * gp + t,
                    lambda e, i=gi, tt=t, g=gp: e.dma_start(
                        out=rtgt[:, g, tt],
                        in_=rtg_s[GRP * i + 128 * tt:GRP * i + 128 * (tt + 1), :]),
                    (list(tscat_tot) + list(rl_free[gp])) if t == 0 else []))
            kb.cop("vector", lambda e, g=gp: e.tensor_copy(
                tidx[:, g, 0, :], rtgt[:, g, 0, 0:1]),
                [rtoks[0], rtoks[1], (vc, vc.n)] + list(tok_free[gp]))
            kb.cop("vector", lambda e, g=gp: e.tensor_copy(
                tidx[:, g, 1, :], rtgt[:, g, 1, 0:1]), [(vc, vc.n)])
            kb.cop("vector", lambda e, g=gp: e.tensor_copy(
                gvc[:, g, 0, :], rtgt[:, g, 0, 1:2]), [(vc, vc.n)])
            kb.cop("vector", lambda e, g=gp: e.tensor_copy(
                gvc[:, g, 1, :], rtgt[:, g, 1, 1:2]), [(vc, vc.n)])
            rl_free[gp] = [(vc, vc.n)]
            tcap = vc.n
            for t in range(2):
                gtokx = xgL.load("gpsimd", 2 * gp + t, lambda e, g=gp, tt=t: e.indirect_dma_start(
                    out=xg[:, g, tt], out_offset=None,
                    in_=xn2s_rows,
                    in_offset=IndirectOffsetOnAxis(ap=tidx[:, g, tt], axis=0),
                    bounds_check=bregs["tok"], oob_is_err=False),
                    ([(vc, tcap)] + list(xn_store_tot) + list(xg_free.get((gp, t), [(vc, xg_clear)]))))
                kb.cop("vector", lambda e, g=gp, tt=t: e.tensor_scalar(
                    xg[:, g, tt], xg[:, g, tt], gvc[:, g, tt, 0:1], None, op0=ALU.mult),
                    [gtokx, (vc, vc.n)])
                xsc = vc.n
                xg_free[(gp, t)] = [(vc, xsc)]
                for k in range(NK):
                    bk = bkT0 if (t * NK + k) % 2 == 0 else bkT1
                    capk = kb.cop("tensor", lambda e, b=bk, kk=k, g=gp, tt=t: e.transpose(
                        b.ap[:, 0:128], xg[:, g, tt, 128 * kk:128 * (kk + 1)], idb[:]),
                        [(vc, xsc)] + list(bk.free))
                    kb.cop("vector", lambda e, b=bk, kk=k, tt=t: e.tensor_copy(
                        xgT[:, kk, 128 * tt:128 * (tt + 1)], b.ap[:, 0:128]),
                        [(tc, capk)])
                    bk.free = ((vc, vc.n),)
            xgT_done = vc.n
            for hc in range(8):
                ih = rot["ih"]
                rot["ih"] += 1
                wp = ih % 3
                tokw = wg13.load("gpsimd", wp, lambda e, p=wp, i=gi, hh=hc:
                                 e.indirect_dma_start(
                    out=w13buf[:, p], out_offset=None,
                    in_=w13s_rows,
                    in_offset=IndirectOffsetOnAxis(ap=w13ti[:, hh, i], axis=0),
                    bounds_check=bregs["w13"], oob_is_err=False),
                    [(vc, tabs_done)] + list(prep_tot) + list(w13_free[wp]))
                b1 = u1_banks[rot["u"] % 2]
                b3 = u3_banks[rot["u"] % 2]
                rot["u"] += 1
                for k in range(NK):
                    waits = ([tokw, (vc, xgT_done)] + list(b1.free)) if k == 0 else []
                    kb.cop("tensor", lambda e, b=b1, p=wp, kk=k: e.matmul(
                        b.ap[:, 0:GRP], w13buf[:, p, 128 * kk:128 * (kk + 1)],
                        xgT[:, kk, :],
                        start=(kk == 0), stop=(kk == NK - 1)),
                        waits, count=(k == NK - 1))
                cap1 = tc.n
                for k in range(NK):
                    waits = list(b3.free) if k == 0 else []
                    kb.cop("tensor", lambda e, b=b3, p=wp, kk=k: e.matmul(
                        b.ap[:, 0:GRP], w13buf[:, p, 2048 + 128 * kk:2048 + 128 * (kk + 1)],
                        xgT[:, kk, :],
                        start=(kk == 0), stop=(kk == NK - 1)),
                        waits, count=(k == NK - 1))
                cap3 = tc.n
                w13_free[wp] = [(tc, cap3)]
                hp = hc % 2
                kb.cop("scalar", lambda e, b=b1, p=hp: e.activation(
                    tmp1[:, p, 0:GRP], b.ap[:, 0:GRP], AF.Silu),
                    [(tc, cap1), (vc, vc.n), (sc, sc.n)])
                b1.free = ((sc, sc.n),)
                sil = sc.n
                kb.cop("vector", lambda e, b=b3, hh=hc, p=hp: e.tensor_mul(
                    h_sb[:, hh, 0:GRP], b.ap[:, 0:GRP], tmp1[:, p, 0:GRP]),
                    [(tc, cap3), (sc, sil), (vc, vc.n)])
                b3.free = ((vc, vc.n),)
            h_done = vc.n
            for dc in range(NK):
                wp = dc % 3
                tokw2 = wg2.load("gpsimd", wp, lambda e, p=wp, i=gi, d_=dc:
                                 e.indirect_dma_start(
                    out=w2buf[:, p], out_offset=None,
                    in_=w2s_rows,
                    in_offset=IndirectOffsetOnAxis(ap=w2ti[:, d_, i], axis=0),
                    bounds_check=bregs["w2"], oob_is_err=False),
                    [(vc, tabs_done)] + list(prep_tot) + list(w2_free[wp]))
                bk = acc_banks[rot["a"] % 2]
                rot["a"] += 1
                for hc in range(8):
                    waits = ([(vc, h_done), tokw2] + list(bk.free)) if hc == 0 else []
                    kb.cop("tensor", lambda e, b=bk, p=wp, hh=hc: e.matmul(
                        b.ap[:, 0:GRP], w2buf[:, p, 128 * hh:128 * (hh + 1)],
                        h_sb[:, hh, 0:GRP],
                        start=(hh == 0), stop=(hh == 7)),
                        waits, count=(hc == 7))
                cap = tc.n
                w2_free[wp] = [(tc, cap)]
                sp = dc % 2
                kb.cop("vector", lambda e, b=bk, p=sp: e.tensor_copy(
                    wsb[:, p, 0:GRP], b.ap[:, 0:GRP]), [(tc, cap)] + list(wsb_free[sp]))
                bk.free = ((vc, vc.n),)
                wsb_done = vc.n
                for t in range(2):
                    tb = tr_banks[rot["t"] % 2]
                    rot["t"] += 1
                    capt = kb.cop("tensor", lambda e, b=tb, p=sp, tt=t: e.transpose(
                        b.ap[:, 0:128], wsb[:, p, 128 * tt:128 * (tt + 1)], idf[:]),
                        [(vc, wsb_done)] + list(tb.free))
                    kb.cop("vector", lambda e, b=tb, tt=t, d_=dc, g=gp: e.tensor_copy(
                        yr[:, g, tt, 128 * d_:128 * (d_ + 1)], b.ap[:, 0:128]),
                        [(tc, capt)] + (list(yr_free[gp]) if dc == 0 else []))
                    tb.free = ((vc, vc.n),)
                wsb_free[sp] = [(tc, tc.n)]
            yr_done = vc.n
            for t in range(2):
                stok = stR.load("gpsimd", 0, lambda e, tt=t, g=gp: e.indirect_dma_start(
                    out=y_flat, out_offset=IndirectOffsetOnAxis(ap=tidx[:, g, tt], axis=0),
                    in_=yr[:, g, tt], in_offset=None, element_offset=s * T * D,
                    bounds_check=bregs["tok"], oob_is_err=False, compute_op=ALU.add),
                    [(vc, yr_done)] + list(shared_store_tot) + list(prev_scat))
                prev_scat = [stok]
            yr_free[gp] = [stok]
            tok_free[gp] = [stok]
        esE.close()
        return list(stE.totals()) + list(stR.totals())

    out_tots = []
    prep_closed = False
    for s in range(SEQ):
        esR = ExitStack()
        QlatT = esR.enter_context(nc.sbuf_tensor(f"QlatT{s}", [128, 8, T], BF16, side="right"))
        CkvT = esR.enter_context(nc.sbuf_tensor(f"CkvT{s}", [128, 4, T], BF16, side="right"))
        phase_A(s, QlatT, CkvT)
        kb.barrier()
        esCtx = ExitStack()
        ctxT = esCtx.enter_context(nc.sbuf_tensor(f"ctxT{s}", [128, NK, T], BF16))
        phase_B(s, QlatT, CkvT, ctxT)
        kb.barrier()
        esR.close()
        stC_tot = phase_C(s, ctxT)
        esCtx.close()
        if stage >= 2:
            if not prep_closed:
                pump(len(prep_jobs))
                es_prep.close()
                prep_closed = True
            prep_tot = prep_st.totals()
            kb.barrier()
            out_tots.append(phase_E(s, stC_tot, prep_tot))
            kb.barrier()
        else:
            out_tots.append(stC_tot)
            kb.barrier()
    if not prep_closed:
        pump(len(prep_jobs))
        es_prep.close()

    final_waits = [tok for tots in out_tots for tok in tots]
    kb.op("gpsimd", None, waits=final_waits)

    es_final = ExitStack()
    es_final.enter_context(es_outer)
    block = es_final.enter_context(nc.Block())
    kb.emit(nc, block, es_final)
    es_final.close()
    return nc


def _ln(kb, src_ap, xn_out, stat, sq, eps_ap, ld_tok, extra_waits=()):
    vc, scc = kb.ecnt["vector"], kb.ecnt["scalar"]
    w0 = [ld_tok, (vc, vc.n)] + list(extra_waits)
    kb.cop("vector", lambda e, a=src_ap, o=stat: e.reduce_sum(
        o[:, 0:1], a, axis=AX.X), w0)
    kb.cop("vector", lambda e, a=src_ap, o=stat, q=sq: e.scalar_tensor_tensor(
        q[:], a, 1.0, a, op0=ALU.mult, op1=ALU.mult, accum_out=o[:, 1:2]),
        [ld_tok])
    kb.cop("vector", lambda e, o=stat: e.tensor_scalar(
        o[:, 2:3], o[:, 0:1], -1.0 / D, None, op0=ALU.mult), [(vc, vc.n)])
    kb.cop("vector", lambda e, o=stat: e.tensor_mul(
        o[:, 3:4], o[:, 2:3], o[:, 2:3]), [(vc, vc.n)])
    kb.cop("vector", lambda e, o=stat: e.scalar_tensor_tensor(
        o[:, 4:5], o[:, 1:2], 1.0 / D, o[:, 3:4], op0=ALU.mult, op1=ALU.subtract),
        [(vc, vc.n)])
    cv = vc.n
    kb.cop("scalar", lambda e, o=stat, b_=eps_ap: e.activation(
        o[:, 5:6], o[:, 4:5], AF.Sqrt, bias=b_), [(vc, cv)])
    kb.cop("vector", lambda e, o=stat: e.reciprocal(o[:, 6:7], o[:, 5:6]),
           [(scc, scc.n)])
    kb.cop("vector", lambda e, a=src_ap, o=stat, x=xn_out: e.tensor_scalar(
        x[:], a, o[:, 2:3], o[:, 6:7], op0=ALU.add, op1=ALU.mult), [(vc, vc.n)])
    return vc.n


def _pack(inputs):
    f32 = np.float32
    asc = np.ascontiguousarray
    wqd = np.asarray(inputs["L0_attn_Wq_down"], f32)
    wkvd = np.asarray(inputs["L0_attn_Wkv_down"], f32)
    wo = np.asarray(inputs["L0_attn_Wo"], f32)
    gate = np.asarray(inputs["L0_ffn_gate"], f32)
    bo = np.asarray(inputs["L0_attn_bo"], f32)
    eb = np.asarray(inputs["L0_ffn_expert_bias"], f32)
    w1 = np.asarray(inputs["L0_ffn_w1"], f32)
    w2 = np.asarray(inputs["L0_ffn_w2"], f32)
    w3 = np.asarray(inputs["L0_ffn_w3"], f32)
    sw1 = np.asarray(inputs["L0_ffn_shared_w1"], f32)
    sw2 = np.asarray(inputs["L0_ffn_shared_w2"], f32)
    sw3 = np.asarray(inputs["L0_ffn_shared_w3"], f32)

    d = {}
    d["wqdT"] = asc(wqd.reshape(RQ, NK, 128).transpose(2, 1, 0))
    d["wkvdT"] = asc(wkvd.reshape(RK, NK, 128).transpose(2, 1, 0))
    d["woT"] = asc(wo.reshape(D, NK, 128).transpose(2, 1, 0))
    d["wqu"] = asc(np.asarray(inputs["L0_attn_Wq_up"], f32))
    d["wku"] = asc(np.asarray(inputs["L0_attn_Wk_up"], f32))
    d["wvu"] = asc(np.asarray(inputs["L0_attn_Wv_up"], f32))
    d["bo_p"] = asc(bo.reshape(NK, 128).T)
    d["gateT_p"] = asc(gate.T.reshape(NK, 128, 8).transpose(1, 0, 2))
    d["eb_p"] = asc(np.broadcast_to(eb, (128, 8)))
    w1all = np.concatenate([w1, sw1[None]], axis=0)
    w3all = np.concatenate([w3, sw3[None]], axis=0)
    w2all = np.concatenate([w2, sw2[None]], axis=0)
    w1p = w1all.reshape(NE, 8, 128, NK, 128).transpose(0, 1, 4, 3, 2)
    w3p = w3all.reshape(NE, 8, 128, NK, 128).transpose(0, 1, 4, 3, 2)
    d["w13p"] = asc(np.stack([w1p, w3p], axis=3))
    d["w2p"] = asc(w2all.reshape(NE, NK, 128, 8, 128).transpose(1, 4, 0, 3, 2))
    d["idf"] = np.eye(128, dtype=f32)
    d["idb"] = np.eye(128, dtype=f32).astype(BF)
    jj, ii = np.meshgrid(np.arange(128), np.arange(128))
    d["maskb"] = np.where(jj > ii, np.float32(NEGM), np.float32(0.0)).astype(BF)
    d["iota_in"] = np.arange(128, dtype=f32)[:, None]
    d["grprow"] = (np.arange(32, dtype=f32) * GRP)[None, :]
    return d


_NC_CACHE = {}


def _get_nc(stage=2):
    if stage not in _NC_CACHE:
        _NC_CACHE[stage] = build_program(stage)
    return _NC_CACHE[stage]


def _run(inputs, stage=2):
    shared = _pack(inputs)
    x = np.asarray(inputs["L0_attn_x"], np.float32)
    in_maps = []
    for c in range(NCORE):
        m = dict(shared)
        m["x"] = np.ascontiguousarray(x[SEQ * c:SEQ * (c + 1)])
        in_maps.append(m)
    nc = _get_nc(stage)
    res = run_bass_kernel_spmd(nc, in_maps, core_ids=list(range(NCORE)))
    out = np.concatenate([np.asarray(r["y"]) for r in res.results], axis=0)
    return out


def kernel(L0_attn_x, L0_attn_Wq_down, L0_attn_Wkv_down, L0_attn_Wq_up,
           L0_attn_Wk_up, L0_attn_Wv_up, L0_attn_Wo, L0_attn_bo,
           L0_ffn_gate, L0_ffn_expert_bias, L0_ffn_w1, L0_ffn_w2, L0_ffn_w3,
           L0_ffn_shared_w1, L0_ffn_shared_w2, L0_ffn_shared_w3):
    inputs = dict(
        L0_attn_x=L0_attn_x, L0_attn_Wq_down=L0_attn_Wq_down,
        L0_attn_Wkv_down=L0_attn_Wkv_down, L0_attn_Wq_up=L0_attn_Wq_up,
        L0_attn_Wk_up=L0_attn_Wk_up, L0_attn_Wv_up=L0_attn_Wv_up,
        L0_attn_Wo=L0_attn_Wo, L0_attn_bo=L0_attn_bo, L0_ffn_gate=L0_ffn_gate,
        L0_ffn_expert_bias=L0_ffn_expert_bias, L0_ffn_w1=L0_ffn_w1,
        L0_ffn_w2=L0_ffn_w2, L0_ffn_w3=L0_ffn_w3,
        L0_ffn_shared_w1=L0_ffn_shared_w1, L0_ffn_shared_w2=L0_ffn_shared_w2,
        L0_ffn_shared_w3=L0_ffn_shared_w3)
    return _run(inputs, stage=2)


# version 12
# speedup vs baseline: 1.0100x; 1.0100x over previous
import math
import numpy as np
import ml_dtypes
import concourse.bass as bass
import concourse.mybir as mybir
from concourse.bass_utils import run_bass_kernel_spmd
from concourse.bass_core_rust import IndirectOffsetOnAxis
from contextlib import ExitStack

F32 = mybir.dt.float32
I32 = mybir.dt.int32
BF16 = mybir.dt.bfloat16
AF = mybir.ActivationFunctionType
ALU = mybir.AluOpType
AX = mybir.AxisListType

BF = ml_dtypes.bfloat16
NEGM = -1.0e30
D = 2048
T = 2048
RQ = 1024
RK = 512
NE = 9
NCORE = 8
SEQ = 2
NK = 16
EPS = 1e-5
GRP = 256
NT = 23
NTG = NT * GRP

ENGS = ("sync", "gpsimd", "tensor", "vector", "scalar")


class Ctr:
    __slots__ = ("name", "n", "h")

    def __init__(self, name):
        self.name = name
        self.n = 0
        self.h = None


class Kb:
    def __init__(self):
        self.ctrs = []
        self.items = {e: [] for e in ENGS}
        self.ecnt = {e: self.ctr("c_" + e) for e in ("tensor", "vector", "scalar")}
        self._bar = {e: () for e in ENGS}

    def ctr(self, name):
        c = Ctr(name)
        self.ctrs.append(c)
        return c

    def op(self, eng, fn, waits=(), inc=None, incn=1):
        ws = tuple((c, v) for (c, v) in waits if v > 0)
        if self._bar[eng]:
            ws = self._bar[eng] + ws
            self._bar[eng] = ()
        self.items[eng].append((fn, ws, inc, incn))
        if inc is not None:
            inc.n += incn

    def cop(self, eng, fn, waits=(), count=True):
        c = self.ecnt[eng]
        self.op(eng, fn, waits, inc=c if count else None, incn=1)
        return c.n

    def barrier(self):
        snap = tuple((c, c.n) for c in self.ctrs if c.n > 0)
        for e in ENGS:
            self._bar[e] = snap

    def emit(self, nc, block, es):
        for c in self.ctrs:
            c.h = es.enter_context(nc.semaphore(c.name))
        for e in ENGS:
            items = self.items[e]

            def make(lst):
                def run(eng):
                    for fn, ws, inc, incn in lst:
                        for (c, v) in ws:
                            eng.wait_ge(c.h, v)
                        inst = fn(eng) if fn is not None else None
                        if inc is not None:
                            tgt = inst if inst is not None else eng.nop()
                            tgt.then_inc(inc.h, incn)
                return run

            getattr(block, e)(make(items))


class DStream:

    def __init__(self, kb, name, nbuf):
        self.kb = kb
        self.sems = [kb.ctr(f"{name}{i}") for i in range(nbuf)]

    def load(self, eng, par, fn, waits=()):
        s = self.sems[par]
        self.kb.op(eng, fn, waits, inc=s, incn=16)
        return (s, s.n)

    def totals(self):
        return tuple((s, s.n) for s in self.sems if s.n > 0)


class Bank:
    __slots__ = ("ap", "free")

    def __init__(self, ap):
        self.ap = ap
        self.free = ()


def build_program(stage=2):
    nc = bass.Bass("TRN2", target_bir_lowering=False)

    x_in = nc.dram_tensor("x", [SEQ, T, D], F32, kind="ExternalInput").ap()
    wqdT = nc.dram_tensor("wqdT", [128, NK, RQ], F32, kind="ExternalInput").ap()
    wkvdT = nc.dram_tensor("wkvdT", [128, NK, RK], F32, kind="ExternalInput").ap()
    woT = nc.dram_tensor("woT", [128, NK, D], F32, kind="ExternalInput").ap()
    wqu = nc.dram_tensor("wqu", [D, RQ], F32, kind="ExternalInput").ap()
    wku = nc.dram_tensor("wku", [D, RK], F32, kind="ExternalInput").ap()
    wvu = nc.dram_tensor("wvu", [D, RK], F32, kind="ExternalInput").ap()
    bo_p = nc.dram_tensor("bo_p", [128, NK], F32, kind="ExternalInput").ap()
    gateT_p = nc.dram_tensor("gateT_p", [128, NK, 8], F32, kind="ExternalInput").ap()
    eb_p = nc.dram_tensor("eb_p", [128, 8], F32, kind="ExternalInput").ap()
    w13p = nc.dram_tensor("w13p", [NE, 8, 128, 2, NK, 128], F32, kind="ExternalInput").ap()
    w2p = nc.dram_tensor("w2p", [4, 128, NE, 4, 8, 128], F32, kind="ExternalInput").ap()
    idf_p = nc.dram_tensor("idf", [128, 128], F32, kind="ExternalInput").ap()
    idb_p = nc.dram_tensor("idb", [128, 128], BF16, kind="ExternalInput").ap()
    maskb_p = nc.dram_tensor("maskb", [128, 128], BF16, kind="ExternalInput").ap()
    iota_p = nc.dram_tensor("iota_in", [128, 1], F32, kind="ExternalInput").ap()
    grprow_p = nc.dram_tensor("grprow", [1, 32], F32, kind="ExternalInput").ap()
    y_out = nc.dram_tensor("y", [SEQ, T, D], F32, kind="ExternalOutput").ap()

    w13s = nc.dram_tensor("w13s", [NE, 8, 128, 2, NK, 128], BF16, kind="Internal").ap()
    w2s = nc.dram_tensor("w2s", [4, 128, NE, 4, 8, 128], BF16, kind="Internal").ap()
    x1s = nc.dram_tensor("x1s", [SEQ, T, D], F32, kind="Internal").ap()
    xn2s_t = [nc.dram_tensor(f"xn2s{i}", [T, D], BF16, kind="Internal").ap()
              for i in range(SEQ)]
    rtg_t = [nc.dram_tensor(f"rtg{i}", [NTG, 2], F32, kind="Internal").ap()
             for i in range(SEQ)]

    kb = Kb()
    tc, vc, sc = kb.ecnt["tensor"], kb.ecnt["vector"], kb.ecnt["scalar"]

    es_outer = ExitStack()
    sb = lambda name, shape, dt: es_outer.enter_context(nc.sbuf_tensor(name, shape, dt))
    ps = lambda name, shape, dt: es_outer.enter_context(nc.psum_tensor(name, shape, dt))

    idf = sb("idf_sb", [128, 128], F32)
    idb = sb("idb_sb", [128, 128], BF16)
    maskb = sb("maskb_sb", [128, 128], BF16)
    bo_sb = sb("bo_sb", [128, NK], F32)
    eb_sb = sb("eb_sb", [128, 8], F32)
    gateT_sb = sb("gateT_sb", [128, NK, 8], BF16)
    ones_sb = sb("ones_sb", [1, 128], BF16)
    onesf_sb = sb("onesf_sb", [1, 128], F32)
    eps_sb = sb("eps_sb", [128, 1], F32)
    iota_sb = sb("iota_sb", [128, 1], F32)
    iota9_sb = sb("iota9_sb", [128, 1], F32)
    grprow_sb = sb("grprow_sb", [1, 32], F32)

    bkA = Bank(ps("psA", [128, 512], F32).ap())
    bkB = Bank(ps("psB", [128, 512], F32).ap())
    bkC = Bank(ps("psC", [128, 512], F32).ap())
    bkD = Bank(ps("psD", [128, 512], F32).ap())
    bkE = Bank(ps("psE", [128, 512], F32).ap())
    bkF = Bank(ps("psF", [128, 512], F32).ap())
    bkT0 = Bank(ps("psT0", [128, 1024], BF16).ap())
    bkT1 = Bank(ps("psT1", [128, 1024], BF16).ap())

    ld1 = DStream(kb, "ld1", 1)
    for dst, src in ((idf, idf_p), (idb, idb_p), (maskb, maskb_p),
                     (bo_sb, bo_p), (eb_sb, eb_p), (iota_sb, iota_p),
                     (grprow_sb, grprow_p)):
        ld1.load("sync", 0, lambda e, d=dst, s=src: e.dma_start(out=d[:], in_=s))
    ld1.load("gpsimd", 0, lambda e: e.dma_start(out=gateT_sb[:], in_=gateT_p))
    bregs = {}

    def _setup_regs(e):
        bregs["tok"] = e.alloc_register("bnd_tok")
        bregs["tab"] = e.alloc_register("bnd_tab")
        bregs["w13"] = e.alloc_register("bnd_w13")
        bregs["w2"] = e.alloc_register("bnd_w2")
        e.reg_alu(bregs["tok"], T - 1, 0, ALU.add)
        e.reg_alu(bregs["w13"], NE * 8 * 128 - 1, 0, ALU.add)
        e.reg_alu(bregs["w2"], 4 * 128 * NE - 1, 0, ALU.add)
        return e.reg_alu(bregs["tab"], NTG - 1, 0, ALU.add)

    kb.op("gpsimd", _setup_regs)
    kb.cop("vector", lambda e: e.memset(ones_sb[:], 1.0))
    kb.cop("vector", lambda e: e.memset(onesf_sb[:], 1.0))
    kb.cop("vector", lambda e: e.memset(eps_sb[:], EPS))
    kb.cop("vector", lambda e: e.tensor_scalar(
        iota9_sb[:], iota_sb[:], 9.0, None, op0=ALU.mult), list(ld1.totals()))
    kb.barrier()

    es_prep = ExitStack()
    prep_buf = es_prep.enter_context(nc.sbuf_tensor("prep_buf", [128, 2, 4608], BF16))
    prep_ld = DStream(kb, "pld", 2)
    prep_st = DStream(kb, "pst", 2)
    w13p_f = w13p.rearrange("e h p u k f -> (e h) p (u k f)")
    w13s_f = w13s.rearrange("e h p u k f -> (e h) p (u k f)")
    w2p_f = w2p.rearrange("g p e d h f -> (g p) (e d h f)")
    w2s_f = w2s.rearrange("g p e d h f -> (g p) (e d h f)")
    prep_jobs = [(w13p_f[i], w13s_f[i], 4096) for i in range(72)]
    for i in range(4):
        for hf in range(8):
            prep_jobs.append((w2p_f[128 * i:128 * (i + 1), 4608 * hf:4608 * (hf + 1)],
                              w2s_f[128 * i:128 * (i + 1), 4608 * hf:4608 * (hf + 1)],
                              4608))
    prep_state = {"i": 0}

    def pump(n=1):
        for _ in range(n):
            i = prep_state["i"]
            if i >= len(prep_jobs):
                return
            prep_state["i"] = i + 1
            par = i % 2
            src, dst, width = prep_jobs[i]
            waits = []
            if i >= 2:
                waits.append((prep_st.sems[par], prep_st.sems[par].n))
            ltok = prep_ld.load("gpsimd", par,
                                lambda e, s=src, p=par, w=width: e.dma_start(
                                    out=prep_buf[:, p, 0:w], in_=s), waits)
            prep_st.load("sync", par,
                         lambda e, d=dst, p=par, w=width: e.dma_start(
                             out=d, in_=prep_buf[:, p, 0:w]), [ltok])

    def phase_A(s, QlatT, CkvT):
        esA = ExitStack()
        sbA = lambda nm, shape, dt: esA.enter_context(nc.sbuf_tensor(f"{nm}_a{s}", shape, dt))
        xtile = sbA("xtile", [128, 2, D], F32)
        xn_bf = sbA("xn_bf", [128, 2, D], BF16)
        xnT = sbA("xnT", [128, NK, 512], BF16)
        stats = sbA("astat", [128, 2, 8], F32)
        wqd_sb = sbA("wqd_sb", [128, NK, RQ], BF16)
        wkvd_sb = sbA("wkvd_sb", [128, NK, RK], BF16)

        xA = DStream(kb, f"xA{s}", 2)
        wA = DStream(kb, f"wA{s}", 1)
        wA.load("gpsimd", 0, lambda e: e.dma_start(out=wqd_sb[:], in_=wqdT))
        tok_w = wA.load("gpsimd", 0, lambda e: e.dma_start(out=wkvd_sb[:], in_=wkvdT))
        xt_free = [(), ()]
        qkv_banks = [bkA, bkB]
        tp_banks = [bkT0, bkT1]
        rot = [0, 0]

        for cg in range(4):
            for t in range(4):
                i = 4 * cg + t
                par = i % 2
                tok = xA.load("sync", par, lambda e, p=par, r=512 * cg + 128 * t:
                              e.dma_start(out=xtile[:, p], in_=x_in[s, r:r + 128, :]),
                              xt_free[par])
                _ln(kb, xtile[:, par], xn_bf[:, par], stats[:, par], xn_bf[:, par], eps_sb[:, 0:1], tok,
                    [(tc, tc.n)])
                xt_free[par] = [(vc, vc.n)]
                xn_done = vc.n
                for k in range(NK):
                    bk = tp_banks[rot[0] % 2]
                    rot[0] += 1
                    cap = kb.cop("tensor", lambda e, b=bk, p=par, kk=k: e.transpose(
                        b.ap[:, 0:128], xn_bf[:, p, 128 * kk:128 * (kk + 1)], idb[:]),
                        [(vc, xn_done)] + list(bk.free))
                    kb.cop("vector", lambda e, b=bk, kk=k, tt=t: e.tensor_copy(
                        xnT[:, kk, 128 * tt:128 * (tt + 1)], b.ap[:, 0:128]), [(tc, cap)])
                    bk.free = ((vc, vc.n),)
            xnT_ready = vc.n
            pump(4)
            for rt in range(12):
                q_side = rt < 8
                wsel = wqd_sb if q_side else wkvd_sb
                col = rt if q_side else rt - 8
                bk = qkv_banks[rot[1] % 2]
                rot[1] += 1
                for k in range(NK):
                    waits = ([(vc, xnT_ready), tok_w] + list(bk.free)) if k == 0 else []
                    kb.cop("tensor", lambda e, b=bk, w=wsel, c=col, kk=k:
                           e.matmul(b.ap[:], w[:, kk, 128 * c:128 * (c + 1)],
                                    xnT[:, kk, :], start=(kk == 0), stop=(kk == NK - 1)),
                           waits, count=(k == NK - 1))
                cap = tc.n
                dst = QlatT if q_side else CkvT
                kb.cop("vector", lambda e, b=bk, d_=dst, c=col, g=cg: e.tensor_copy(
                    d_[:, c, 512 * g:512 * (g + 1)], b.ap[:]), [(tc, cap)])
                bk.free = ((vc, vc.n),)
        esA.close()

    def phase_B(s, QlatT, CkvT, ctxT):
        esB = ExitStack()
        sbB = lambda nm, shape, dt: esB.enter_context(nc.sbuf_tensor(f"{nm}_b{s}", shape, dt))
        wqh = sbB("wqh", [128, 2, RQ], BF16)
        wkh = sbB("wkh", [128, 2, RK], BF16)
        wvh = sbB("wvh", [128, 2, RK], BF16)
        uvT = sbB("uvT", [128, 4, 128], BF16)
        mm_sb = sbB("mm_sb", [128, 8, 512], BF16)
        qhT = sbB("qhT", [128, 4, T], BF16)
        vup = sbB("vup", [128, NK, 128], BF16)
        p_sb = sbB("p_sb", [128, 2, T], BF16)
        pt4 = sbB("pt4", [128, NK, 512], BF16)
        mst = sbB("mst", [128, 8], F32)
        sstat = sbB("sstat", [128, 4], F32)

        whl = DStream(kb, f"wh{s}", 2)
        wh_free = [(), ()]
        lg_banks = [bkA, bkB, bkC, bkD, bkE, bkF]
        tp_banks = [bkT0, bkT1]
        rot = {"lg": 0, "tp": 0}
        prev_pv = [0]

        for h in range(16):
            par = h % 2
            whl.load("gpsimd", par, lambda e, p=par, hh=h: e.dma_start(
                out=wqh[:, p], in_=wqu[128 * hh:128 * (hh + 1), :]), wh_free[par])
            whl.load("gpsimd", par, lambda e, p=par, hh=h: e.dma_start(
                out=wkh[:, p], in_=wku[128 * hh:128 * (hh + 1), :]))
            tok_h = whl.load("gpsimd", par, lambda e, p=par, hh=h: e.dma_start(
                out=wvh[:, p], in_=wvu[128 * hh:128 * (hh + 1), :]))
            pump(6)
            for c in range(4):
                bk = tp_banks[rot["tp"] % 2]
                rot["tp"] += 1
                cap = kb.cop("tensor", lambda e, b=bk, p=par, cc=c: e.transpose(
                    b.ap[:, 0:128], wvh[:, p, 128 * cc:128 * (cc + 1)], idb[:]),
                    [tok_h] + list(bk.free))
                kb.cop("vector", lambda e, b=bk, cc=c: e.tensor_copy(
                    uvT[:, cc, :], b.ap[:, 0:128]), [(tc, cap)])
                bk.free = ((vc, vc.n),)
            for j in range(8):
                bk = lg_banks[rot["lg"] % 6]
                rot["lg"] += 1
                cap = kb.cop("tensor", lambda e, b=bk, p=par, jj=j: e.matmul(
                    b.ap[:], wqh[:, p, 128 * jj:128 * (jj + 1)], wkh[:, p],
                    start=True, stop=True), [tok_h] + list(bk.free))
                kb.cop("vector", lambda e, b=bk, jj=j: e.tensor_scalar(
                    mm_sb[:, jj, :], b.ap[:], 1.0 / math.sqrt(128.0), None,
                    op0=ALU.mult), [(tc, cap)])
                bk.free = ((vc, vc.n),)
            mm_done = vc.n
            wh_free[par] = [(tc, tc.n)]
            for rt in range(4):
                for g4 in range(4):
                    bk = lg_banks[rot["lg"] % 6]
                    rot["lg"] += 1
                    for j in range(8):
                        waits = ([(vc, mm_done)] + list(bk.free)) if j == 0 else []
                        kb.cop("tensor", lambda e, b=bk, r=rt, g=g4, jj=j, Q=QlatT:
                               e.matmul(b.ap[:], mm_sb[:, jj, 128 * r:128 * (r + 1)],
                                        Q[:, jj, 512 * g:512 * (g + 1)],
                                        start=(jj == 0), stop=(jj == 7)),
                               waits, count=(j == 7))
                    cap = tc.n
                    kb.cop("vector", lambda e, b=bk, r=rt, g=g4: e.tensor_copy(
                        qhT[:, r, 512 * g:512 * (g + 1)], b.ap[:]), [(tc, cap)])
                    bk.free = ((vc, vc.n),)
            uv_done = vc.n
            for st in range(NK):
                bk = lg_banks[rot["lg"] % 6]
                rot["lg"] += 1
                for c in range(4):
                    waits = ([(vc, uv_done)] + list(bk.free)) if c == 0 else []
                    kb.cop("tensor", lambda e, b=bk, s_=st, cc=c, C=CkvT: e.matmul(
                        b.ap[:, 0:128], C[:, cc, 128 * s_:128 * (s_ + 1)],
                        uvT[:, cc, :], start=(cc == 0), stop=(cc == 3)),
                        waits, count=(c == 3))
                cap = tc.n
                kb.cop("vector", lambda e, b=bk, s_=st: e.tensor_copy(
                    vup[:, s_, :], b.ap[:, 0:128]), [(tc, cap)])
                bk.free = ((vc, vc.n),)
            qh_done = vc.n

            def emit_tr(tq, p_done):
                par = tq % 2
                g = tq // 4
                m = tq - 4 * g
                for st in range(tq + 1):
                    bk = tp_banks[rot["tp"] % 2]
                    rot["tp"] += 1
                    cap = kb.cop("tensor", lambda e, b=bk, s_=st, p=par: e.transpose(
                        b.ap[:, 0:128], p_sb[:, p, 128 * s_:128 * (s_ + 1)], idb[:]),
                        [(vc, p_done)] + list(bk.free))
                    kb.cop("vector", lambda e, b=bk, s_=st, mm=m: e.tensor_copy(
                        pt4[:, s_, 128 * mm:128 * (mm + 1)], b.ap[:, 0:128]),
                        [(tc, cap)])
                    bk.free = ((vc, vc.n),)
                for st in range(tq + 1, 4 * g + 4):
                    kb.cop("vector", lambda e, s_=st, mm=m: e.memset(
                        pt4[:, s_, 128 * mm:128 * (mm + 1)], 0.0),
                        [(tc, prev_pv[0])])
                return vc.n

            def emit_pv(g):
                bk = lg_banks[rot["lg"] % 6]
                rot["lg"] += 1
                nst = 4 * g + 4
                for st in range(nst):
                    waits = ([(vc, pt_done[0])] + list(bk.free)) if st == 0 else []
                    kb.cop("tensor", lambda e, b=bk, s_=st, n=nst: e.matmul(
                        b.ap[:], vup[:, s_, :], pt4[:, s_, :],
                        start=(s_ == 0), stop=(s_ == n - 1)),
                        waits, count=(st == nst - 1))
                prev_pv[0] = tc.n
                kb.cop("vector", lambda e, b=bk, hh=h, gg=g, X=ctxT: e.tensor_copy(
                    X[:, hh, 512 * gg:512 * (gg + 1)], b.ap[:]), [(tc, prev_pv[0])])
                bk.free = ((vc, vc.n),)

            pend = None
            pt_done = [0]
            for tq in range(16):
                g = tq // 4
                m = tq - 4 * g
                par = tq % 2
                slen = 128 * (tq + 1)
                ng = (slen + 511) // 512
                exp_banks = []
                for scg in range(ng):
                    w = min(512, slen - 512 * scg)
                    bk = lg_banks[rot["lg"] % 6]
                    rot["lg"] += 1
                    diag = (scg == ng - 1)
                    for c in range(4):
                        waits = ([(vc, qh_done)] + list(bk.free)) if c == 0 else []
                        kb.cop("tensor", lambda e, b=bk, cc=c, q=tq, sg=scg, ww=w,
                               C=CkvT, dg=diag:
                               e.matmul(b.ap[:, 0:ww],
                                        qhT[:, cc, 128 * q:128 * (q + 1)],
                                        C[:, cc, 512 * sg:512 * sg + ww],
                                        start=(cc == 0), stop=(cc == 3 and not dg)),
                               waits, count=(c == 3 and not diag))
                    if diag:
                        kb.cop("tensor", lambda e, b=bk, ww=w: e.matmul(
                            b.ap[:, ww - 128:ww], idb[:], maskb[:],
                            start=False, stop=True), [], count=True)
                    cap = tc.n
                    kb.cop("vector", lambda e, b=bk, sg=scg, ww=w: e.reduce_max(
                        mst[:, sg:sg + 1], b.ap[:, 0:ww], axis=AX.X), [(tc, cap)])
                    exp_banks.append((bk, w))
                kb.cop("vector", lambda e, n=ng: e.reduce_max(
                    sstat[:, 0:1], mst[:, 0:n], axis=AX.X, negate=True),
                    [(vc, vc.n), (sc, sc.n)])
                negm_done = vc.n
                if pend is not None:
                    pt_done[0] = emit_tr(*pend)
                if m == 0 and g > 0:
                    emit_pv(g - 1)
                for scg, (bk, w) in enumerate(exp_banks):
                    kb.cop("scalar", lambda e, b=bk, sg=scg, ww=w, p=par:
                           e.activation(
                               p_sb[:, p, 512 * sg:512 * sg + ww], b.ap[:, 0:ww],
                               AF.Exp, bias=sstat[:, 0:1],
                               accum_out=mst[:, 4 + sg:5 + sg]),
                           [(vc, negm_done)])
                    bk.free = ((sc, sc.n),)
                exp_done = sc.n
                kb.cop("vector", lambda e, n=ng: e.reduce_sum(
                    sstat[:, 1:2], mst[:, 4:4 + n], axis=AX.X), [(sc, exp_done)])
                kb.cop("vector", lambda e: e.reciprocal(sstat[:, 2:3], sstat[:, 1:2]),
                       [(vc, vc.n)])
                kb.cop("vector", lambda e, L=slen, p=par: e.tensor_scalar(
                    p_sb[:, p, 0:L], p_sb[:, p, 0:L], sstat[:, 2:3], None,
                    op0=ALU.mult), [(vc, vc.n)])
                pend = (tq, vc.n)
            pt_done[0] = emit_tr(*pend)
            emit_pv(3)
        esB.close()

    def phase_C(s, ctxT):
        esC = ExitStack()
        sbC = lambda nm, shape, dt: esC.enter_context(nc.sbuf_tensor(f"{nm}_c{s}", shape, dt))
        wo_sb = sbC("wo_sb", [128, NK, D], BF16)
        xcg = sbC("xcg", [128, 4, D], F32)
        wsb = sbC("wsbC", [128, 2, 512], F32)
        piece = sbC("pieceC", [128, 2, 128], F32)

        wC = DStream(kb, f"wC{s}", 2)
        tok_wo1 = wC.load("gpsimd", 0, lambda e: e.dma_start(
            out=wo_sb[:, :, 0:1024], in_=woT[:, :, 0:1024]))
        tok_wo2 = wC.load("gpsimd", 1, lambda e: e.dma_start(
            out=wo_sb[:, :, 1024:2048], in_=woT[:, :, 1024:2048]))
        xC = DStream(kb, f"xC{s}", 1)
        stC = DStream(kb, f"stC{s}", 2)
        acc_banks = [bkA, bkB]
        tr_banks = [bkC, bkD]
        xc_free = []
        piece_free = [(), ()]
        wsb_free = [(), ()]
        rot = {"a": 0, "t": 0, "p": 0}
        dest = x1s if stage >= 2 else y_out

        for cg in range(4):
            tokx = xC.load("sync", 0, lambda e, c=cg: e.dma_start(
                out=xcg[:],
                in_=x_in[s, 512 * c:512 * (c + 1), :].rearrange("(t p) d -> p t d", p=128)),
                list(xc_free))
            pump(2)
            for dc in range(NK):
                bk = acc_banks[rot["a"] % 2]
                rot["a"] += 1
                for k in range(NK):
                    tw = tok_wo1 if dc < 8 else tok_wo2
                    waits = ([tw] + list(bk.free)) if k == 0 else []
                    kb.cop("tensor", lambda e, b=bk, d_=dc, kk=k, c=cg, X=ctxT: e.matmul(
                        b.ap[:], wo_sb[:, kk, 128 * d_:128 * (d_ + 1)],
                        X[:, kk, 512 * c:512 * (c + 1)],
                        start=(kk == 0), stop=(kk == NK - 1)),
                        waits, count=(k == NK - 1))
                cap = tc.n
                wp = dc % 2
                kb.cop("vector", lambda e, b=bk, d_=dc, p=wp: e.tensor_scalar(
                    wsb[:, p], b.ap[:], bo_sb[:, d_:d_ + 1], None, op0=ALU.add),
                    [(tc, cap)] + list(wsb_free[wp]))
                bk.free = ((vc, vc.n),)
                wsb_done = vc.n
                for t in range(4):
                    tb = tr_banks[rot["t"] % 2]
                    rot["t"] += 1
                    capt = kb.cop("tensor", lambda e, b=tb, p=wp, tt=t: e.transpose(
                        b.ap[:, 0:128], wsb[:, p, 128 * tt:128 * (tt + 1)], idf[:]),
                        [(vc, wsb_done)] + list(tb.free))
                    pp = rot["p"] % 2
                    rot["p"] += 1
                    kb.cop("vector", lambda e, b=tb, tt=t, d_=dc, q=pp: e.tensor_add(
                        piece[:, q], b.ap[:, 0:128], xcg[:, tt, 128 * d_:128 * (d_ + 1)]),
                        [(tc, capt), tokx] + list(piece_free[pp]))
                    tb.free = ((vc, vc.n),)
                    addcap = vc.n
                    stC.load("gpsimd", pp, lambda e, q=pp, r=512 * cg + 128 * t, d_=dc:
                             e.dma_start(
                                 out=dest[s, r:r + 128, 128 * d_:128 * (d_ + 1)],
                                 in_=piece[:, q]), [(vc, addcap)])
                    piece_free[pp] = [(stC.sems[pp], stC.sems[pp].n)]
                wsb_free[wp] = [(tc, tc.n)]
            xc_free = [(vc, vc.n)]
        esC.close()
        return stC.totals()

    def phase_E(s, stC_tot, prep_tot):
        esE = ExitStack()
        sbE = lambda nm, shape, dt: esE.enter_context(nc.sbuf_tensor(f"{nm}_e{s}", shape, dt))
        xn2T = sbE("xn2T", [128, NK, 512], BF16)
        h_sb = sbE("h_sb", [128, 8, 512], BF16)
        w13buf = sbE("w13buf", [128, 3, 2 * NK * 128], BF16)
        w2buf = sbE("w2buf", [128, 2, 4096], BF16)
        scores = sbE("scores", [128, 4, 8], F32)
        gtok = sbE("gtok", [128, 8], F32)
        gtmp = sbE("gtmp", [128, 3, 8], F32)
        gm = sbE("gm", [128, 2], F32)
        xcg1 = sbE("xcg1", [128, 4, D], F32)
        xn2_bf = sbE("xn2_bf", [128, 2, D], BF16)
        stats = sbE("estat", [128, 8], F32)
        tmp1 = sbE("tmp1", [128, 2, 512], BF16)
        wsb = sbE("wsbE", [128, 2, 512], F32)
        piece = sbE("pieceE", [128, 2, 128], F32)
        crow = sbE("crow", [1, 16], F32)
        brow = sbE("brow", [1, 16], F32)
        etab = sbE("etab", [1, 64], F32)
        esT = ExitStack()
        sbT = lambda nm, shape, dt: esT.enter_context(nc.sbuf_tensor(f"{nm}_e{s}", shape, dt))
        grow = sbT("grow", [1, 8, T], BF16)
        mrow = sbT("mrow", [1, T], BF16)
        zrow = sbT("zrow", [1, T], BF16)
        prow = sbT("prow", [1, T], F32)
        fill2 = sbT("fill2", [128, NTG // 128, 2], F32)
        hugec = sbT("hugec", [128, 1], F32)
        sdata = sbT("sdata", [128, 2, 2], F32)
        scol = sbT("scol", [128, 2, 4], F32)
        sidx = sbT("sidx", [128, 2, 1], I32)
        e4b = sbT("e4b", [128, 1], F32)

        x1L = DStream(kb, f"x1L{s}", 1)
        xnS = DStream(kb, f"xnS{s}", 2)
        w13L = DStream(kb, f"w13L{s}", 2)
        w2L = DStream(kb, f"w2L{s}", 2)
        stE = DStream(kb, f"stE{s}", 2)
        tfill = DStream(kb, f"tf{s}", 1)
        tscat = DStream(kb, f"ts{s}", 2)
        rload = DStream(kb, f"rl{s}", 4)
        wg13 = DStream(kb, f"wg13{s}", 3)
        wg2 = DStream(kb, f"wg2{s}", 3)
        xgL = DStream(kb, f"xgL{s}", 4)
        stR = DStream(kb, f"stR{s}", 1)
        u1_banks = [bkA, bkC]
        u3_banks = [bkB, bkD]
        acc_banks = [bkA, bkB]
        tr_banks = [bkC, bkD]
        x1_free = []
        w13_free = [(), (), ()]
        w2_free = [(), ()]
        piece_free = [(), ()]
        wsb_free = [(), ()]
        xn_free = [(), ()]
        tscat_free = [(), ()]
        rot = {"u": 0, "a": 0, "t": 0, "p": 0, "ih": 0}
        w13s_rows = w13s.rearrange("e h p u k f -> (e h p) (u k f)")
        w2s_rows = w2s.rearrange("g p e d h f -> (g p e) (d h f)")
        xn2s_rows = xn2s_t[s]
        rtg_s = rtg_t[s]
        y_flat = y_out.rearrange("q t d -> (q t) d")

        e4_steps = []
        e4refs = {}

        def _e4_fill():
            kb.cop("vector", lambda e: e.memset(zrow[:], 0.0), [(vc, vc.n)])
            kb.cop("vector", lambda e: e.memset(fill2[:, :, 0:1], float(1 << 23)), [])
            kb.cop("vector", lambda e: e.memset(fill2[:, :, 1:2], 0.0), [])
            fill_done = vc.n
            e4refs["ftok"] = tfill.load("gpsimd", 0, lambda e: e.dma_start(
                out=rtg_s.rearrange("(a p) c -> p a c", p=128), in_=fill2[:]),
                [(vc, fill_done)])
        e4_steps.append(_e4_fill)

        def _e4_scanA(ee):
            def f():
                kb.cop("vector", lambda e, x=ee: e.tensor_scalar(
                    mrow[:], grow[0:1, x, :], 0.0, None, op0=ALU.is_gt), [(vc, vc.n)])
                kb.cop("vector", lambda e: e.tensor_tensor_scan(
                    prow[:], mrow[:], zrow[:], 0.0, op0=ALU.add, op1=ALU.add),
                    [(vc, vc.n)])
                kb.cop("vector", lambda e, x=ee: e.tensor_copy(
                    crow[0:1, x:x + 1], prow[0:1, T - 1:T]), [(vc, vc.n)])
            return f
        for _ee in range(8):
            e4_steps.append(_e4_scanA(_ee))

        def _e4_bases():
            kb.cop("vector", lambda e: e.tensor_scalar(
                crow[0:1, 8:16], crow[0:1, 0:8], float(GRP - 1), 1.0 / GRP,
                op0=ALU.add, op1=ALU.mult), [(vc, vc.n)])
            kb.cop("vector", lambda e: e.tensor_scalar(
                crow[0:1, 8:16], crow[0:1, 8:16], -0.498046875, 12582912.0,
                op0=ALU.add, op1=ALU.add), [(vc, vc.n)])
            kb.cop("vector", lambda e: e.tensor_scalar(
                crow[0:1, 8:16], crow[0:1, 8:16], -12582912.0, float(GRP),
                op0=ALU.add, op1=ALU.mult), [(vc, vc.n)])
            kb.cop("vector", lambda e: e.tensor_tensor_scan(
                brow[0:1, 1:9], crow[0:1, 8:16], zrow[0:1, 0:8], 0.0,
                op0=ALU.add, op1=ALU.add), [(vc, vc.n)])
            kb.cop("vector", lambda e: e.memset(brow[0:1, 0:1], 0.0), [(vc, vc.n)])
            kb.cop("vector", lambda e: e.memset(etab[:], 0.0), [(vc, vc.n)])
            for x in range(1, 9):
                kb.cop("vector", lambda e, x_=x: e.tensor_scalar(
                    etab[0:1, 32:32 + NT], grprow_sb[0:1, 0:NT], brow[0:1, x_:x_ + 1],
                    None, op0=ALU.is_ge), [(vc, vc.n)])
                kb.cop("vector", lambda e: e.tensor_add(
                    etab[0:1, 0:NT], etab[0:1, 0:NT], etab[0:1, 32:32 + NT]),
                    [(vc, vc.n)])
            e4refs["etab_done"] = vc.n
        e4_steps.append(_e4_bases)

        def _e4_scanB(ee):
            def f():
                kb.cop("vector", lambda e, x=ee: e.tensor_scalar(
                    mrow[:], grow[0:1, x, :], 0.0, None, op0=ALU.is_gt),
                    [(vc, vc.n)])
                kb.cop("vector", lambda e: e.tensor_tensor_scan(
                    prow[:], mrow[:], zrow[:], 0.0, op0=ALU.add, op1=ALU.add),
                    [(vc, vc.n)])
                kb.cop("vector", lambda e, x=ee: e.tensor_scalar(
                    brow[0:1, 9:10], brow[0:1, x:x + 1], -1.0, None, op0=ALU.add),
                    [(vc, vc.n)])
                capb = kb.cop("tensor", lambda e: e.matmul(
                    bkF.ap[:, 0:1], onesf_sb[:], brow[0:1, 9:10],
                    start=True, stop=True), [(vc, vc.n)] + list(bkF.free))
                kb.cop("vector", lambda e: e.tensor_copy(e4b[:], bkF.ap[:, 0:1]),
                       [(tc, capb)])
                bkF.free = ((vc, vc.n),)
            return f

        def _e4_tile(ee, t):
            def f():
                par = t % 2
                cap1 = kb.cop("tensor", lambda e, tt=t: e.transpose(
                    bkF.ap[:, 0:1], prow[0:1, 128 * tt:128 * (tt + 1)],
                    idf[0:1, 0:1]), [(vc, vc.n)] + list(bkF.free))
                kb.cop("vector", lambda e, p=par: e.tensor_copy(
                    scol[:, p, 0:1], bkF.ap[:, 0:1]),
                    [(tc, cap1)] + list(tscat_free[par]))
                bkF.free = ((vc, vc.n),)
                cap2 = kb.cop("tensor", lambda e, x=ee, tt=t: e.transpose(
                    bkT0.ap[:, 0:1], grow[0:1, x, 128 * tt:128 * (tt + 1)],
                    idb[0:1, 0:1]), list(bkT0.free))
                kb.cop("vector", lambda e, p=par: e.tensor_copy(
                    scol[:, p, 1:2], bkT0.ap[:, 0:1]), [(tc, cap2)])
                bkT0.free = ((vc, vc.n),)
                kb.cop("vector", lambda e, p=par, tt=t: e.tensor_scalar(
                    sdata[:, p, 0:1], iota_sb[:], float(128 * tt), None, op0=ALU.add),
                    [(vc, vc.n)])
                kb.cop("vector", lambda e, p=par: e.tensor_copy(
                    sdata[:, p, 1:2], scol[:, p, 1:2]), [(vc, vc.n)])
                kb.cop("vector", lambda e, p=par: e.tensor_add(
                    scol[:, p, 2:3], scol[:, p, 0:1], e4b[:]), [(vc, vc.n)])
                kb.cop("vector", lambda e, p=par: e.tensor_scalar(
                    scol[:, p, 3:4], scol[:, p, 1:2], 0.0, None, op0=ALU.is_gt),
                    [(vc, vc.n)])
                kb.cop("vector", lambda e, p=par: e.tensor_scalar(
                    scol[:, p, 2:3], scol[:, p, 2:3], -float(1 << 23), None,
                    op0=ALU.add), [(vc, vc.n)])
                kb.cop("vector", lambda e, p=par: e.scalar_tensor_tensor(
                    scol[:, p, 2:3], scol[:, p, 2:3], 1.0, scol[:, p, 3:4],
                    op0=ALU.mult, op1=ALU.mult), [(vc, vc.n)])
                kb.cop("vector", lambda e, p=par: e.tensor_scalar(
                    scol[:, p, 2:3], scol[:, p, 2:3], float(1 << 23), None,
                    op0=ALU.add), [(vc, vc.n)])
                kb.cop("vector", lambda e, p=par: e.tensor_copy(
                    sidx[:, p], scol[:, p, 2:3]), [(vc, vc.n)])
                dcap = vc.n
                stok = tscat.load("gpsimd", par, lambda e, p=par: e.indirect_dma_start(
                    out=rtg_s, out_offset=IndirectOffsetOnAxis(ap=sidx[:, p], axis=0),
                    in_=sdata[:, p], in_offset=None,
                    bounds_check=bregs["tab"], oob_is_err=False),
                    [(vc, dcap), e4refs["ftok"]])
                tscat_free[par] = [stok]
            return f
        for _ee in range(8):
            e4_steps.append(_e4_scanB(_ee))
            for _t in range(NK):
                e4_steps.append(_e4_tile(_ee, _t))
        e4_state = {"i": 0}

        def e4pump(n):
            for _ in range(n):
                i = e4_state["i"]
                if i >= len(e4_steps):
                    return
                e4_state["i"] = i + 1
                e4_steps[i]()

        for grp in range(4):
            tokx = x1L.load("sync", 0, lambda e, g=grp: e.dma_start(
                out=xcg1[:],
                in_=x1s[s, 512 * g:512 * (g + 1), :].rearrange("(t p) d -> p t d", p=128)),
                list(stC_tot) + list(x1_free))
            tile_caps = []
            for t in range(4):
                par = t % 2
                xnd = _ln(kb, xcg1[:, t, :], xn2_bf[:, par], stats, xn2_bf[:, par],
                          eps_sb[:, 0:1], tokx, [(tc, tc.n)] + list(xn_free[par]))
                xnS.load("gpsimd", par, lambda e, p=par, r=512 * grp + 128 * t:
                         e.dma_start(out=xn2s_rows[r:r + 128, :], in_=xn2_bf[:, p]),
                         [(vc, xnd)])
                xn_free[par] = [(xnS.sems[par], xnS.sems[par].n)]
                for k in range(NK):
                    bk = bkT0 if (t * NK + k) % 2 == 0 else bkT1
                    cap = kb.cop("tensor", lambda e, b=bk, kk=k, p=par: e.transpose(
                        b.ap[:, 0:128], xn2_bf[:, p, 128 * kk:128 * (kk + 1)], idb[:]),
                        [(vc, xnd)] + list(bk.free))
                    kb.cop("vector", lambda e, b=bk, kk=k, tt=t: e.tensor_copy(
                        xn2T[:, kk, 128 * tt:128 * (tt + 1)], b.ap[:, 0:128]), [(tc, cap)])
                    bk.free = ((vc, vc.n),)
                tile_caps.append(vc.n)
            for t in range(4):
                for k in range(NK):
                    waits = ([(vc, tile_caps[t])] + list(bkE.free)) if k == 0 else []
                    kb.cop("tensor", lambda e, kk=k, tt=t: e.matmul(
                        bkE.ap[:, 0:8], xn2T[:, kk, 128 * tt:128 * (tt + 1)],
                        gateT_sb[:, kk, :], start=(kk == 0), stop=(kk == NK - 1)),
                        waits, count=(k == NK - 1))
                cap = tc.n
                kb.cop("scalar", lambda e, tt=t: e.activation(
                    scores[:, tt, :], bkE.ap[:, 0:8], AF.Sigmoid), [(tc, cap), (vc, vc.n)])
                bkE.free = ((sc, sc.n),)
                sgd = sc.n
                kb.cop("vector", lambda e, tt=t: e.tensor_add(
                    gtmp[:, 0, :], scores[:, tt, :], eb_sb[:]), [(sc, sgd), (vc, vc.n)])
                kb.cop("vector", lambda e: e.reduce_max(gm[:, 0:1], gtmp[:, 0, :],
                                                        axis=AX.X), [(vc, vc.n)])
                kb.cop("vector", lambda e: e.tensor_scalar(
                    gtmp[:, 1, :], gtmp[:, 0, :], gm[:, 0:1], None, op0=ALU.is_equal),
                    [(vc, vc.n)])
                kb.cop("vector", lambda e: e.scalar_tensor_tensor(
                    gtmp[:, 2, :], gtmp[:, 1, :], NEGM, gtmp[:, 0, :],
                    op0=ALU.mult, op1=ALU.add), [(vc, vc.n)])
                kb.cop("vector", lambda e: e.reduce_max(gm[:, 1:2], gtmp[:, 2, :],
                                                        axis=AX.X), [(vc, vc.n)])
                kb.cop("vector", lambda e: e.tensor_scalar(
                    gtmp[:, 2, :], gtmp[:, 2, :], gm[:, 1:2], None, op0=ALU.is_equal),
                    [(vc, vc.n)])
                kb.cop("vector", lambda e: e.tensor_add(
                    gtmp[:, 1, :], gtmp[:, 1, :], gtmp[:, 2, :]), [(vc, vc.n)])
                kb.cop("vector", lambda e, tt=t: e.tensor_mul(
                    gtok[:], scores[:, tt, :], gtmp[:, 1, :]), [(vc, vc.n), (tc, tc.n)])
                gdone = vc.n
                for ee in range(8):
                    capg = kb.cop("tensor", lambda e, x=ee: e.transpose(
                        bkF.ap[0:1, 0:128], gtok[:, x:x + 1], idf[:]),
                        [(vc, gdone)] + list(bkF.free))
                    kb.cop("vector", lambda e, x=ee, c0=512 * grp + 128 * t: e.tensor_copy(
                        grow[0:1, x, c0:c0 + 128], bkF.ap[0:1, 0:128]), [(tc, capg)])
                    bkF.free = ((vc, vc.n),)
            for hc in range(8):
                ih = rot["ih"]
                rot["ih"] += 1
                wp = ih % 2
                tokw = w13L.load("sync", wp, lambda e, p=wp, hh=hc: e.dma_start(
                    out=w13buf[:, p], in_=w13s[8, hh]),
                    list(prep_tot) + list(w13_free[wp]))
                b1 = u1_banks[rot["u"] % 2]
                b3 = u3_banks[rot["u"] % 2]
                rot["u"] += 1
                for k in range(NK):
                    waits = ([tokw, (vc, tile_caps[3])] + list(b1.free)) if k == 0 else []
                    kb.cop("tensor", lambda e, b=b1, p=wp, kk=k: e.matmul(
                        b.ap[:], w13buf[:, p, 128 * kk:128 * (kk + 1)], xn2T[:, kk, :],
                        start=(kk == 0), stop=(kk == NK - 1)),
                        waits, count=(k == NK - 1))
                cap1 = tc.n
                for k in range(NK):
                    waits = list(b3.free) if k == 0 else []
                    kb.cop("tensor", lambda e, b=b3, p=wp, kk=k: e.matmul(
                        b.ap[:], w13buf[:, p, 2048 + 128 * kk:2048 + 128 * (kk + 1)], xn2T[:, kk, :],
                        start=(kk == 0), stop=(kk == NK - 1)),
                        waits, count=(k == NK - 1))
                cap3 = tc.n
                w13_free[wp] = [(tc, cap3)]
                hp = hc % 2
                kb.cop("scalar", lambda e, b=b1, p=hp: e.activation(
                    tmp1[:, p], b.ap[:], AF.Silu), [(tc, cap1), (vc, vc.n), (sc, sc.n)])
                b1.free = ((sc, sc.n),)
                sil = sc.n
                kb.cop("vector", lambda e, b=b3, hh=hc, p=hp: e.tensor_mul(
                    h_sb[:, hh, :], b.ap[:], tmp1[:, p]),
                    [(tc, cap3), (sc, sil), (vc, vc.n)])
                b3.free = ((vc, vc.n),)
                if grp == 3:
                    e4pump(8)
            h_done = vc.n
            pendw = None

            def emit_epi(dc, wsb_done):
                sp = dc % 2
                for t in range(4):
                    tb = tr_banks[rot["t"] % 2]
                    rot["t"] += 1
                    capt = kb.cop("tensor", lambda e, b=tb, p=sp, tt=t: e.transpose(
                        b.ap[:, 0:128], wsb[:, p, 128 * tt:128 * (tt + 1)], idf[:]),
                        [(vc, wsb_done)] + list(tb.free))
                    pp = rot["p"] % 2
                    rot["p"] += 1
                    kb.cop("vector", lambda e, b=tb, tt=t, d_=dc, q=pp: e.tensor_add(
                        piece[:, q], b.ap[:, 0:128], xcg1[:, tt, 128 * d_:128 * (d_ + 1)]),
                        [(tc, capt), tokx] + list(piece_free[pp]))
                    tb.free = ((vc, vc.n),)
                    addcap = vc.n
                    stE.load("gpsimd", pp, lambda e, q=pp, r=512 * grp + 128 * t, d_=dc:
                             e.dma_start(
                                 out=y_out[s, r:r + 128, 128 * d_:128 * (d_ + 1)],
                                 in_=piece[:, q]), [(vc, addcap)])
                    piece_free[pp] = [(stE.sems[pp], stE.sems[pp].n)]
                wsb_free[sp] = [(tc, tc.n)]

            for dc in range(NK):
                wp = dc % 2
                tokw2 = w2L.load("sync", wp, lambda e, p=wp, d_=dc: e.dma_start(
                    out=w2buf[:, p, 0:1024], in_=w2s[d_ // 4, :, 8, d_ % 4]),
                    list(prep_tot) + list(w2_free[wp]))
                bk = acc_banks[rot["a"] % 2]
                rot["a"] += 1
                for hc in range(8):
                    waits = ([(vc, h_done), tokw2] + list(bk.free)) if hc == 0 else []
                    kb.cop("tensor", lambda e, b=bk, p=wp, hh=hc: e.matmul(
                        b.ap[:], w2buf[:, p, 128 * hh:128 * (hh + 1)], h_sb[:, hh, :],
                        start=(hh == 0), stop=(hh == 7)),
                        waits, count=(hc == 7))
                cap = tc.n
                w2_free[wp] = [(tc, cap)]
                sp = dc % 2
                kb.cop("vector", lambda e, b=bk, p=sp: e.tensor_copy(
                    wsb[:, p], b.ap[:]), [(tc, cap)] + list(wsb_free[sp]))
                bk.free = ((vc, vc.n),)
                if pendw is not None:
                    emit_epi(*pendw)
                pendw = (dc, vc.n)
                if grp == 3:
                    e4pump(12)
            emit_epi(*pendw)
            x1_free = [(vc, vc.n)]
        scores_done = vc.n
        e4pump(len(e4_steps))
        xn_store_tot = xnS.totals()
        shared_store_tot = stE.totals()
        tscat_tot = tscat.totals()
        esT.close()
        xg = sbE("xg", [128, 2, 2, D], BF16)
        xgT = sbE("xgT", [128, NK, 256], BF16)
        yr = sbE("yr", [128, 2, 2, D], F32)
        rtgt = sbE("rtgt", [128, 2, 2, 2], F32)
        tidx = sbE("tidx", [128, 2, 2, 1], I32)
        gvc = sbE("gvc", [128, 2, 2, 1], F32)
        ebtab = sbE("ebtab", [128, 32], F32)
        etab_bf = sbE("etab_bf", [1, 32], BF16)
        w13tf = sbE("w13tf", [128, 8, NT], F32)
        w2tf = sbE("w2tf", [128, 4, NT], F32)
        w13ti = sbE("w13ti", [128, 8, NT, 1], I32)
        w2ti = sbE("w2ti", [128, 4, NT, 1], I32)

        kb.cop("scalar", lambda e: e.copy(etab_bf[:], etab[0:1, 0:32]),
               [(vc, e4refs["etab_done"])])
        capb = kb.cop("tensor", lambda e: e.matmul(
            bkF.ap[:, 0:32], ones_sb[:], etab_bf[:], start=True, stop=True),
            [(sc, sc.n)] + list(bkF.free))
        kb.cop("vector", lambda e: e.tensor_copy(ebtab[:], bkF.ap[:, 0:32]),
               [(tc, capb), (vc, vc.n)])
        bkF.free = ((vc, vc.n),)
        kb.cop("vector", lambda e: e.tensor_scalar(
            w13tf[:, 0, :], ebtab[:, 0:NT], 1024.0, iota_sb[:, 0:1],
            op0=ALU.mult, op1=ALU.add), [(vc, vc.n)])
        for hh in range(1, 8):
            kb.cop("vector", lambda e, h_=hh: e.tensor_scalar(
                w13tf[:, h_, :], w13tf[:, 0, :], float(128 * h_), None,
                op0=ALU.add), [(vc, vc.n)])
        kb.cop("vector", lambda e: e.tensor_scalar(
            w2tf[:, 0, :], ebtab[:, 0:NT], 1.0, iota9_sb[:, 0:1],
            op0=ALU.mult, op1=ALU.add), [(vc, vc.n)])
        for d_ in range(1, 4):
            kb.cop("vector", lambda e, dd=d_: e.tensor_scalar(
                w2tf[:, dd, :], w2tf[:, 0, :], float(1152 * dd), None,
                op0=ALU.add), [(vc, vc.n)])
        kb.cop("vector", lambda e: e.tensor_copy(
            w13ti.rearrange("p h t o -> p (h t o)"),
            w13tf.rearrange("p h t -> p (h t)")), [(vc, vc.n)])
        kb.cop("vector", lambda e: e.tensor_copy(
            w2ti.rearrange("p d t o -> p (d t o)"),
            w2tf.rearrange("p d t -> p (d t)")), [(vc, vc.n)])
        tabs_done = vc.n

        kb.cop("vector", lambda e: e.memset(xg[:], 0.0), [(vc, vc.n)])
        xg_clear = vc.n
        prev_scat = []
        xg_free = {}
        rl_free = [(), ()]
        tok_free = [(), ()]
        yr_free = [(), ()]
        for gi in range(NT):
            gp = gi % 2
            rtoks = []
            for t in range(2):
                rtoks.append(rload.load("sync", 2 * gp + t,
                    lambda e, i=gi, tt=t, g=gp: e.dma_start(
                        out=rtgt[:, g, tt],
                        in_=rtg_s[GRP * i + 128 * tt:GRP * i + 128 * (tt + 1), :]),
                    (list(tscat_tot) + list(rl_free[gp])) if t == 0 else []))
            kb.cop("vector", lambda e, g=gp: e.tensor_copy(
                tidx[:, g, 0, :], rtgt[:, g, 0, 0:1]),
                [rtoks[0], rtoks[1], (vc, vc.n)] + list(tok_free[gp]))
            kb.cop("vector", lambda e, g=gp: e.tensor_copy(
                tidx[:, g, 1, :], rtgt[:, g, 1, 0:1]), [(vc, vc.n)])
            kb.cop("vector", lambda e, g=gp: e.tensor_copy(
                gvc[:, g, 0, :], rtgt[:, g, 0, 1:2]), [(vc, vc.n)])
            kb.cop("vector", lambda e, g=gp: e.tensor_copy(
                gvc[:, g, 1, :], rtgt[:, g, 1, 1:2]), [(vc, vc.n)])
            rl_free[gp] = [(vc, vc.n)]
            tcap = vc.n
            for t in range(2):
                gtokx = xgL.load("gpsimd", 2 * gp + t, lambda e, g=gp, tt=t: e.indirect_dma_start(
                    out=xg[:, g, tt], out_offset=None,
                    in_=xn2s_rows,
                    in_offset=IndirectOffsetOnAxis(ap=tidx[:, g, tt], axis=0),
                    bounds_check=bregs["tok"], oob_is_err=False),
                    ([(vc, tcap)] + list(xn_store_tot) + list(xg_free.get((gp, t), [(vc, xg_clear)]))))
                kb.cop("vector", lambda e, g=gp, tt=t: e.tensor_scalar(
                    xg[:, g, tt], xg[:, g, tt], gvc[:, g, tt, 0:1], None, op0=ALU.mult),
                    [gtokx, (vc, vc.n)])
                xsc = vc.n
                xg_free[(gp, t)] = [(vc, xsc)]
                for k in range(NK):
                    bk = bkT0 if (t * NK + k) % 2 == 0 else bkT1
                    capk = kb.cop("tensor", lambda e, b=bk, kk=k, g=gp, tt=t: e.transpose(
                        b.ap[:, 0:128], xg[:, g, tt, 128 * kk:128 * (kk + 1)], idb[:]),
                        [(vc, xsc)] + list(bk.free))
                    kb.cop("vector", lambda e, b=bk, kk=k, tt=t: e.tensor_copy(
                        xgT[:, kk, 128 * tt:128 * (tt + 1)], b.ap[:, 0:128]),
                        [(tc, capk)])
                    bk.free = ((vc, vc.n),)
            xgT_done = vc.n
            for hc in range(8):
                ih = rot["ih"]
                rot["ih"] += 1
                wp = ih % 3
                tokw = wg13.load("gpsimd", wp, lambda e, p=wp, i=gi, hh=hc:
                                 e.indirect_dma_start(
                    out=w13buf[:, p], out_offset=None,
                    in_=w13s_rows,
                    in_offset=IndirectOffsetOnAxis(ap=w13ti[:, hh, i], axis=0),
                    bounds_check=bregs["w13"], oob_is_err=False),
                    [(vc, tabs_done)] + list(prep_tot) + list(w13_free[wp]))
                b1 = u1_banks[rot["u"] % 2]
                b3 = u3_banks[rot["u"] % 2]
                rot["u"] += 1
                for k in range(NK):
                    waits = ([tokw, (vc, xgT_done)] + list(b1.free)) if k == 0 else []
                    kb.cop("tensor", lambda e, b=b1, p=wp, kk=k: e.matmul(
                        b.ap[:, 0:GRP], w13buf[:, p, 128 * kk:128 * (kk + 1)],
                        xgT[:, kk, :],
                        start=(kk == 0), stop=(kk == NK - 1)),
                        waits, count=(k == NK - 1))
                cap1 = tc.n
                for k in range(NK):
                    waits = list(b3.free) if k == 0 else []
                    kb.cop("tensor", lambda e, b=b3, p=wp, kk=k: e.matmul(
                        b.ap[:, 0:GRP], w13buf[:, p, 2048 + 128 * kk:2048 + 128 * (kk + 1)],
                        xgT[:, kk, :],
                        start=(kk == 0), stop=(kk == NK - 1)),
                        waits, count=(k == NK - 1))
                cap3 = tc.n
                w13_free[wp] = [(tc, cap3)]
                hp = hc % 2
                kb.cop("scalar", lambda e, b=b1, p=hp: e.activation(
                    tmp1[:, p, 0:GRP], b.ap[:, 0:GRP], AF.Silu),
                    [(tc, cap1), (vc, vc.n), (sc, sc.n)])
                b1.free = ((sc, sc.n),)
                sil = sc.n
                kb.cop("vector", lambda e, b=b3, hh=hc, p=hp: e.tensor_mul(
                    h_sb[:, hh, 0:GRP], b.ap[:, 0:GRP], tmp1[:, p, 0:GRP]),
                    [(tc, cap3), (sc, sil), (vc, vc.n)])
                b3.free = ((vc, vc.n),)
            h_done = vc.n
            for dc in range(NK):
                wp = (dc // 4) % 2
                if dc % 4 == 0:
                    tokw2 = wg2.load("gpsimd", wp, lambda e, p=wp, i=gi, g_=dc // 4:
                                     e.indirect_dma_start(
                        out=w2buf[:, p], out_offset=None,
                        in_=w2s_rows,
                        in_offset=IndirectOffsetOnAxis(ap=w2ti[:, g_, i], axis=0),
                        bounds_check=bregs["w2"], oob_is_err=False),
                        [(vc, tabs_done)] + list(prep_tot) + list(w2_free[wp]))
                bk = acc_banks[rot["a"] % 2]
                rot["a"] += 1
                for hc in range(8):
                    waits = ([(vc, h_done), tokw2] + list(bk.free)) if hc == 0 else []
                    kb.cop("tensor", lambda e, b=bk, p=wp, hh=hc, d4=dc % 4: e.matmul(
                        b.ap[:, 0:GRP],
                        w2buf[:, p, 1024 * d4 + 128 * hh:1024 * d4 + 128 * (hh + 1)],
                        h_sb[:, hh, 0:GRP],
                        start=(hh == 0), stop=(hh == 7)),
                        waits, count=(hc == 7))
                cap = tc.n
                if dc % 4 == 3:
                    w2_free[wp] = [(tc, cap)]
                sp = dc % 2
                kb.cop("vector", lambda e, b=bk, p=sp: e.tensor_copy(
                    wsb[:, p, 0:GRP], b.ap[:, 0:GRP]), [(tc, cap)] + list(wsb_free[sp]))
                bk.free = ((vc, vc.n),)
                wsb_done = vc.n
                for t in range(2):
                    tb = tr_banks[rot["t"] % 2]
                    rot["t"] += 1
                    capt = kb.cop("tensor", lambda e, b=tb, p=sp, tt=t: e.transpose(
                        b.ap[:, 0:128], wsb[:, p, 128 * tt:128 * (tt + 1)], idf[:]),
                        [(vc, wsb_done)] + list(tb.free))
                    kb.cop("vector", lambda e, b=tb, tt=t, d_=dc, g=gp: e.tensor_copy(
                        yr[:, g, tt, 128 * d_:128 * (d_ + 1)], b.ap[:, 0:128]),
                        [(tc, capt)] + (list(yr_free[gp]) if dc == 0 else []))
                    tb.free = ((vc, vc.n),)
                wsb_free[sp] = [(tc, tc.n)]
            yr_done = vc.n
            for t in range(2):
                stok = stR.load("gpsimd", 0, lambda e, tt=t, g=gp: e.indirect_dma_start(
                    out=y_flat, out_offset=IndirectOffsetOnAxis(ap=tidx[:, g, tt], axis=0),
                    in_=yr[:, g, tt], in_offset=None, element_offset=s * T * D,
                    bounds_check=bregs["tok"], oob_is_err=False, compute_op=ALU.add),
                    [(vc, yr_done)] + list(shared_store_tot) + list(prev_scat))
                prev_scat = [stok]
            yr_free[gp] = [stok]
            tok_free[gp] = [stok]
        esE.close()
        return list(stE.totals()) + list(stR.totals())

    out_tots = []
    prep_closed = False
    for s in range(SEQ):
        esR = ExitStack()
        QlatT = esR.enter_context(nc.sbuf_tensor(f"QlatT{s}", [128, 8, T], BF16, side="right"))
        CkvT = esR.enter_context(nc.sbuf_tensor(f"CkvT{s}", [128, 4, T], BF16, side="right"))
        phase_A(s, QlatT, CkvT)
        kb.barrier()
        esCtx = ExitStack()
        ctxT = esCtx.enter_context(nc.sbuf_tensor(f"ctxT{s}", [128, NK, T], BF16))
        phase_B(s, QlatT, CkvT, ctxT)
        kb.barrier()
        esR.close()
        stC_tot = phase_C(s, ctxT)
        esCtx.close()
        if stage >= 2:
            if not prep_closed:
                pump(len(prep_jobs))
                es_prep.close()
                prep_closed = True
            prep_tot = prep_st.totals()
            kb.barrier()
            out_tots.append(phase_E(s, stC_tot, prep_tot))
            kb.barrier()
        else:
            out_tots.append(stC_tot)
            kb.barrier()
    if not prep_closed:
        pump(len(prep_jobs))
        es_prep.close()

    final_waits = [tok for tots in out_tots for tok in tots]
    kb.op("gpsimd", None, waits=final_waits)

    es_final = ExitStack()
    es_final.enter_context(es_outer)
    block = es_final.enter_context(nc.Block())
    kb.emit(nc, block, es_final)
    es_final.close()
    return nc


def _ln(kb, src_ap, xn_out, stat, sq, eps_ap, ld_tok, extra_waits=()):
    vc, scc = kb.ecnt["vector"], kb.ecnt["scalar"]
    w0 = [ld_tok, (vc, vc.n)] + list(extra_waits)
    kb.cop("vector", lambda e, a=src_ap, o=stat: e.reduce_sum(
        o[:, 0:1], a, axis=AX.X), w0)
    kb.cop("vector", lambda e, a=src_ap, o=stat, q=sq: e.scalar_tensor_tensor(
        q[:], a, 1.0, a, op0=ALU.mult, op1=ALU.mult, accum_out=o[:, 1:2]),
        [ld_tok])
    kb.cop("vector", lambda e, o=stat: e.tensor_scalar(
        o[:, 2:3], o[:, 0:1], -1.0 / D, None, op0=ALU.mult), [(vc, vc.n)])
    kb.cop("vector", lambda e, o=stat: e.tensor_mul(
        o[:, 3:4], o[:, 2:3], o[:, 2:3]), [(vc, vc.n)])
    kb.cop("vector", lambda e, o=stat: e.scalar_tensor_tensor(
        o[:, 4:5], o[:, 1:2], 1.0 / D, o[:, 3:4], op0=ALU.mult, op1=ALU.subtract),
        [(vc, vc.n)])
    cv = vc.n
    kb.cop("scalar", lambda e, o=stat, b_=eps_ap: e.activation(
        o[:, 5:6], o[:, 4:5], AF.Sqrt, bias=b_), [(vc, cv)])
    kb.cop("vector", lambda e, o=stat: e.reciprocal(o[:, 6:7], o[:, 5:6]),
           [(scc, scc.n)])
    kb.cop("vector", lambda e, a=src_ap, o=stat, x=xn_out: e.tensor_scalar(
        x[:], a, o[:, 2:3], o[:, 6:7], op0=ALU.add, op1=ALU.mult), [(vc, vc.n)])
    return vc.n


def _pack(inputs):
    f32 = np.float32
    asc = np.ascontiguousarray
    wqd = np.asarray(inputs["L0_attn_Wq_down"], f32)
    wkvd = np.asarray(inputs["L0_attn_Wkv_down"], f32)
    wo = np.asarray(inputs["L0_attn_Wo"], f32)
    gate = np.asarray(inputs["L0_ffn_gate"], f32)
    bo = np.asarray(inputs["L0_attn_bo"], f32)
    eb = np.asarray(inputs["L0_ffn_expert_bias"], f32)
    w1 = np.asarray(inputs["L0_ffn_w1"], f32)
    w2 = np.asarray(inputs["L0_ffn_w2"], f32)
    w3 = np.asarray(inputs["L0_ffn_w3"], f32)
    sw1 = np.asarray(inputs["L0_ffn_shared_w1"], f32)
    sw2 = np.asarray(inputs["L0_ffn_shared_w2"], f32)
    sw3 = np.asarray(inputs["L0_ffn_shared_w3"], f32)

    d = {}
    d["wqdT"] = asc(wqd.reshape(RQ, NK, 128).transpose(2, 1, 0))
    d["wkvdT"] = asc(wkvd.reshape(RK, NK, 128).transpose(2, 1, 0))
    d["woT"] = asc(wo.reshape(D, NK, 128).transpose(2, 1, 0))
    d["wqu"] = asc(np.asarray(inputs["L0_attn_Wq_up"], f32))
    d["wku"] = asc(np.asarray(inputs["L0_attn_Wk_up"], f32))
    d["wvu"] = asc(np.asarray(inputs["L0_attn_Wv_up"], f32))
    d["bo_p"] = asc(bo.reshape(NK, 128).T)
    d["gateT_p"] = asc(gate.T.reshape(NK, 128, 8).transpose(1, 0, 2))
    d["eb_p"] = asc(np.broadcast_to(eb, (128, 8)))
    w1all = np.concatenate([w1, sw1[None]], axis=0)
    w3all = np.concatenate([w3, sw3[None]], axis=0)
    w2all = np.concatenate([w2, sw2[None]], axis=0)
    w1p = w1all.reshape(NE, 8, 128, NK, 128).transpose(0, 1, 4, 3, 2)
    w3p = w3all.reshape(NE, 8, 128, NK, 128).transpose(0, 1, 4, 3, 2)
    d["w13p"] = asc(np.stack([w1p, w3p], axis=3))
    d["w2p"] = asc(w2all.reshape(NE, 4, 4, 128, 8, 128).transpose(1, 5, 0, 2, 4, 3))
    d["idf"] = np.eye(128, dtype=f32)
    d["idb"] = np.eye(128, dtype=f32).astype(BF)
    jj, ii = np.meshgrid(np.arange(128), np.arange(128))
    d["maskb"] = np.where(jj > ii, np.float32(NEGM), np.float32(0.0)).astype(BF)
    d["iota_in"] = np.arange(128, dtype=f32)[:, None]
    d["grprow"] = (np.arange(32, dtype=f32) * GRP)[None, :]
    return d


_NC_CACHE = {}


def _get_nc(stage=2):
    if stage not in _NC_CACHE:
        _NC_CACHE[stage] = build_program(stage)
    return _NC_CACHE[stage]


def _run(inputs, stage=2):
    shared = _pack(inputs)
    x = np.asarray(inputs["L0_attn_x"], np.float32)
    in_maps = []
    for c in range(NCORE):
        m = dict(shared)
        m["x"] = np.ascontiguousarray(x[SEQ * c:SEQ * (c + 1)])
        in_maps.append(m)
    nc = _get_nc(stage)
    res = run_bass_kernel_spmd(nc, in_maps, core_ids=list(range(NCORE)))
    out = np.concatenate([np.asarray(r["y"]) for r in res.results], axis=0)
    return out


def kernel(L0_attn_x, L0_attn_Wq_down, L0_attn_Wkv_down, L0_attn_Wq_up,
           L0_attn_Wk_up, L0_attn_Wv_up, L0_attn_Wo, L0_attn_bo,
           L0_ffn_gate, L0_ffn_expert_bias, L0_ffn_w1, L0_ffn_w2, L0_ffn_w3,
           L0_ffn_shared_w1, L0_ffn_shared_w2, L0_ffn_shared_w3):
    inputs = dict(
        L0_attn_x=L0_attn_x, L0_attn_Wq_down=L0_attn_Wq_down,
        L0_attn_Wkv_down=L0_attn_Wkv_down, L0_attn_Wq_up=L0_attn_Wq_up,
        L0_attn_Wk_up=L0_attn_Wk_up, L0_attn_Wv_up=L0_attn_Wv_up,
        L0_attn_Wo=L0_attn_Wo, L0_attn_bo=L0_attn_bo, L0_ffn_gate=L0_ffn_gate,
        L0_ffn_expert_bias=L0_ffn_expert_bias, L0_ffn_w1=L0_ffn_w1,
        L0_ffn_w2=L0_ffn_w2, L0_ffn_w3=L0_ffn_w3,
        L0_ffn_shared_w1=L0_ffn_shared_w1, L0_ffn_shared_w2=L0_ffn_shared_w2,
        L0_ffn_shared_w3=L0_ffn_shared_w3)
    return _run(inputs, stage=2)


# version 13
# speedup vs baseline: 1.0117x; 1.0017x over previous
import math
import numpy as np
import ml_dtypes
import concourse.bass as bass
import concourse.mybir as mybir
from concourse.bass_utils import run_bass_kernel_spmd
from concourse.bass_core_rust import IndirectOffsetOnAxis
from contextlib import ExitStack

F32 = mybir.dt.float32
I32 = mybir.dt.int32
BF16 = mybir.dt.bfloat16
AF = mybir.ActivationFunctionType
ALU = mybir.AluOpType
AX = mybir.AxisListType

BF = ml_dtypes.bfloat16
NEGM = -1.0e30
D = 2048
T = 2048
RQ = 1024
RK = 512
NE = 9
NCORE = 8
SEQ = 2
NK = 16
EPS = 1e-5
GRP = 256
NT = 23
NTG = NT * GRP

ENGS = ("sync", "gpsimd", "tensor", "vector", "scalar")


class Ctr:
    __slots__ = ("name", "n", "h")

    def __init__(self, name):
        self.name = name
        self.n = 0
        self.h = None


class Kb:
    def __init__(self):
        self.ctrs = []
        self.items = {e: [] for e in ENGS}
        self.ecnt = {e: self.ctr("c_" + e) for e in ("tensor", "vector", "scalar")}
        self._bar = {e: () for e in ENGS}

    def ctr(self, name):
        c = Ctr(name)
        self.ctrs.append(c)
        return c

    def op(self, eng, fn, waits=(), inc=None, incn=1):
        ws = tuple((c, v) for (c, v) in waits if v > 0)
        if self._bar[eng]:
            ws = self._bar[eng] + ws
            self._bar[eng] = ()
        self.items[eng].append((fn, ws, inc, incn))
        if inc is not None:
            inc.n += incn

    def cop(self, eng, fn, waits=(), count=True):
        c = self.ecnt[eng]
        self.op(eng, fn, waits, inc=c if count else None, incn=1)
        return c.n

    def barrier(self):
        snap = tuple((c, c.n) for c in self.ctrs if c.n > 0)
        for e in ENGS:
            self._bar[e] = snap

    def emit(self, nc, block, es):
        for c in self.ctrs:
            c.h = es.enter_context(nc.semaphore(c.name))
        for e in ENGS:
            items = self.items[e]

            def make(lst):
                def run(eng):
                    for fn, ws, inc, incn in lst:
                        for (c, v) in ws:
                            eng.wait_ge(c.h, v)
                        inst = fn(eng) if fn is not None else None
                        if inc is not None:
                            tgt = inst if inst is not None else eng.nop()
                            tgt.then_inc(inc.h, incn)
                return run

            getattr(block, e)(make(items))


class DStream:

    def __init__(self, kb, name, nbuf):
        self.kb = kb
        self.sems = [kb.ctr(f"{name}{i}") for i in range(nbuf)]

    def load(self, eng, par, fn, waits=()):
        s = self.sems[par]
        self.kb.op(eng, fn, waits, inc=s, incn=16)
        return (s, s.n)

    def totals(self):
        return tuple((s, s.n) for s in self.sems if s.n > 0)


class Bank:
    __slots__ = ("ap", "free")

    def __init__(self, ap):
        self.ap = ap
        self.free = ()


def build_program(stage=2):
    nc = bass.Bass("TRN2", target_bir_lowering=False)

    x_in = nc.dram_tensor("x", [SEQ, T, D], F32, kind="ExternalInput").ap()
    wqdT = nc.dram_tensor("wqdT", [128, NK, RQ], F32, kind="ExternalInput").ap()
    wkvdT = nc.dram_tensor("wkvdT", [128, NK, RK], F32, kind="ExternalInput").ap()
    woT = nc.dram_tensor("woT", [128, NK, D], F32, kind="ExternalInput").ap()
    wqu = nc.dram_tensor("wqu", [D, RQ], F32, kind="ExternalInput").ap()
    wku = nc.dram_tensor("wku", [D, RK], F32, kind="ExternalInput").ap()
    wvu = nc.dram_tensor("wvu", [D, RK], F32, kind="ExternalInput").ap()
    bo_p = nc.dram_tensor("bo_p", [128, NK], F32, kind="ExternalInput").ap()
    gateT_p = nc.dram_tensor("gateT_p", [128, NK, 8], F32, kind="ExternalInput").ap()
    eb_p = nc.dram_tensor("eb_p", [128, 8], F32, kind="ExternalInput").ap()
    w13p = nc.dram_tensor("w13p", [NE, 8, 128, 2, NK, 128], F32, kind="ExternalInput").ap()
    w2p = nc.dram_tensor("w2p", [4, 128, NE, 4, 8, 128], F32, kind="ExternalInput").ap()
    idf_p = nc.dram_tensor("idf", [128, 128], F32, kind="ExternalInput").ap()
    idb_p = nc.dram_tensor("idb", [128, 128], BF16, kind="ExternalInput").ap()
    maskb_p = nc.dram_tensor("maskb", [128, 128], BF16, kind="ExternalInput").ap()
    iota_p = nc.dram_tensor("iota_in", [128, 1], F32, kind="ExternalInput").ap()
    grprow_p = nc.dram_tensor("grprow", [1, 32], F32, kind="ExternalInput").ap()
    y_out = nc.dram_tensor("y", [SEQ, T, D], F32, kind="ExternalOutput").ap()

    w13s = nc.dram_tensor("w13s", [NE, 8, 128, 2, NK, 128], BF16, kind="Internal").ap()
    w2s = nc.dram_tensor("w2s", [4, 128, NE, 4, 8, 128], BF16, kind="Internal").ap()
    x1s = nc.dram_tensor("x1s", [SEQ, T, D], F32, kind="Internal").ap()
    xn2s_t = [nc.dram_tensor(f"xn2s{i}", [T, D], BF16, kind="Internal").ap()
              for i in range(SEQ)]
    rtg_t = [nc.dram_tensor(f"rtg{i}", [NTG, 2], F32, kind="Internal").ap()
             for i in range(SEQ)]

    kb = Kb()
    tc, vc, sc = kb.ecnt["tensor"], kb.ecnt["vector"], kb.ecnt["scalar"]

    es_outer = ExitStack()
    sb = lambda name, shape, dt: es_outer.enter_context(nc.sbuf_tensor(name, shape, dt))
    ps = lambda name, shape, dt: es_outer.enter_context(nc.psum_tensor(name, shape, dt))

    idf = sb("idf_sb", [128, 128], F32)
    idb = sb("idb_sb", [128, 128], BF16)
    maskb = sb("maskb_sb", [128, 128], BF16)
    bo_sb = sb("bo_sb", [128, NK], F32)
    eb_sb = sb("eb_sb", [128, 8], F32)
    gateT_sb = sb("gateT_sb", [128, NK, 8], BF16)
    ones_sb = sb("ones_sb", [1, 128], BF16)
    onesf_sb = sb("onesf_sb", [1, 128], F32)
    eps_sb = sb("eps_sb", [128, 1], F32)
    iota_sb = sb("iota_sb", [128, 1], F32)
    iota9_sb = sb("iota9_sb", [128, 1], F32)
    grprow_sb = sb("grprow_sb", [1, 32], F32)

    bkA = Bank(ps("psA", [128, 512], F32).ap())
    bkB = Bank(ps("psB", [128, 512], F32).ap())
    bkC = Bank(ps("psC", [128, 512], F32).ap())
    bkD = Bank(ps("psD", [128, 512], F32).ap())
    bkE = Bank(ps("psE", [128, 512], F32).ap())
    bkF = Bank(ps("psF", [128, 512], F32).ap())
    bkT0 = Bank(ps("psT0", [128, 1024], BF16).ap())
    bkT1 = Bank(ps("psT1", [128, 1024], BF16).ap())

    ld1 = DStream(kb, "ld1", 1)
    for dst, src in ((idf, idf_p), (idb, idb_p), (maskb, maskb_p),
                     (bo_sb, bo_p), (eb_sb, eb_p), (iota_sb, iota_p),
                     (grprow_sb, grprow_p)):
        ld1.load("sync", 0, lambda e, d=dst, s=src: e.dma_start(out=d[:], in_=s))
    ld1.load("gpsimd", 0, lambda e: e.dma_start(out=gateT_sb[:], in_=gateT_p))
    bregs = {}

    def _setup_regs(e):
        bregs["tok"] = e.alloc_register("bnd_tok")
        bregs["tab"] = e.alloc_register("bnd_tab")
        bregs["w13"] = e.alloc_register("bnd_w13")
        bregs["w2"] = e.alloc_register("bnd_w2")
        e.reg_alu(bregs["tok"], T - 1, 0, ALU.add)
        e.reg_alu(bregs["w13"], NE * 8 * 128 - 1, 0, ALU.add)
        e.reg_alu(bregs["w2"], 4 * 128 * NE - 1, 0, ALU.add)
        return e.reg_alu(bregs["tab"], NTG - 1, 0, ALU.add)

    kb.op("gpsimd", _setup_regs)
    kb.cop("vector", lambda e: e.memset(ones_sb[:], 1.0))
    kb.cop("vector", lambda e: e.memset(onesf_sb[:], 1.0))
    kb.cop("vector", lambda e: e.memset(eps_sb[:], EPS))
    kb.cop("vector", lambda e: e.tensor_scalar(
        iota9_sb[:], iota_sb[:], 9.0, None, op0=ALU.mult), list(ld1.totals()))
    kb.barrier()

    es_prep = ExitStack()
    prep_buf = es_prep.enter_context(nc.sbuf_tensor("prep_buf", [128, 2, 4608], BF16))
    prep_ld = DStream(kb, "pld", 2)
    prep_st = DStream(kb, "pst", 2)
    w13p_f = w13p.rearrange("e h p u k f -> (e h) p (u k f)")
    w13s_f = w13s.rearrange("e h p u k f -> (e h) p (u k f)")
    w2p_f = w2p.rearrange("g p e d h f -> (g p) (e d h f)")
    w2s_f = w2s.rearrange("g p e d h f -> (g p) (e d h f)")
    prep_jobs = [(w13p_f[i], w13s_f[i], 4096) for i in range(72)]
    for i in range(4):
        for hf in range(8):
            prep_jobs.append((w2p_f[128 * i:128 * (i + 1), 4608 * hf:4608 * (hf + 1)],
                              w2s_f[128 * i:128 * (i + 1), 4608 * hf:4608 * (hf + 1)],
                              4608))
    prep_state = {"i": 0}

    def pump(n=1):
        for _ in range(n):
            i = prep_state["i"]
            if i >= len(prep_jobs):
                return
            prep_state["i"] = i + 1
            par = i % 2
            src, dst, width = prep_jobs[i]
            waits = []
            if i >= 2:
                waits.append((prep_st.sems[par], prep_st.sems[par].n))
            ltok = prep_ld.load("gpsimd", par,
                                lambda e, s=src, p=par, w=width: e.dma_start(
                                    out=prep_buf[:, p, 0:w], in_=s), waits)
            prep_st.load("sync", par,
                         lambda e, d=dst, p=par, w=width: e.dma_start(
                             out=d, in_=prep_buf[:, p, 0:w]), [ltok])

    def phase_A(s, QlatT, CkvT):
        esA = ExitStack()
        sbA = lambda nm, shape, dt: esA.enter_context(nc.sbuf_tensor(f"{nm}_a{s}", shape, dt))
        xtile = sbA("xtile", [128, 2, D], F32)
        xn_bf = sbA("xn_bf", [128, 2, D], BF16)
        xnT = sbA("xnT", [128, NK, 512], BF16)
        stats = sbA("astat", [128, 2, 8], F32)
        wqd_sb = sbA("wqd_sb", [128, NK, RQ], BF16)
        wkvd_sb = sbA("wkvd_sb", [128, NK, RK], BF16)

        xA = DStream(kb, f"xA{s}", 2)
        wA = DStream(kb, f"wA{s}", 1)
        wA.load("gpsimd", 0, lambda e: e.dma_start(out=wqd_sb[:], in_=wqdT))
        tok_w = wA.load("gpsimd", 0, lambda e: e.dma_start(out=wkvd_sb[:], in_=wkvdT))
        xt_free = [(), ()]
        qkv_banks = [bkA, bkB]
        tp_banks = [bkT0, bkT1]
        rot = [0, 0]

        for cg in range(4):
            for t in range(4):
                i = 4 * cg + t
                par = i % 2
                tok = xA.load("sync", par, lambda e, p=par, r=512 * cg + 128 * t:
                              e.dma_start(out=xtile[:, p], in_=x_in[s, r:r + 128, :]),
                              xt_free[par])
                _ln(kb, xtile[:, par], xn_bf[:, par], stats[:, par], xn_bf[:, par], eps_sb[:, 0:1], tok,
                    [(tc, tc.n)])
                xt_free[par] = [(vc, vc.n)]
                xn_done = vc.n
                for k in range(NK):
                    bk = tp_banks[rot[0] % 2]
                    rot[0] += 1
                    cap = kb.cop("tensor", lambda e, b=bk, p=par, kk=k: e.transpose(
                        b.ap[:, 0:128], xn_bf[:, p, 128 * kk:128 * (kk + 1)], idb[:]),
                        [(vc, xn_done)] + list(bk.free))
                    kb.cop("vector", lambda e, b=bk, kk=k, tt=t: e.tensor_copy(
                        xnT[:, kk, 128 * tt:128 * (tt + 1)], b.ap[:, 0:128]), [(tc, cap)])
                    bk.free = ((vc, vc.n),)
            xnT_ready = vc.n
            pump(4)
            for rt in range(12):
                q_side = rt < 8
                wsel = wqd_sb if q_side else wkvd_sb
                col = rt if q_side else rt - 8
                bk = qkv_banks[rot[1] % 2]
                rot[1] += 1
                for k in range(NK):
                    waits = ([(vc, xnT_ready), tok_w] + list(bk.free)) if k == 0 else []
                    kb.cop("tensor", lambda e, b=bk, w=wsel, c=col, kk=k:
                           e.matmul(b.ap[:], w[:, kk, 128 * c:128 * (c + 1)],
                                    xnT[:, kk, :], start=(kk == 0), stop=(kk == NK - 1)),
                           waits, count=(k == NK - 1))
                cap = tc.n
                dst = QlatT if q_side else CkvT
                kb.cop("vector", lambda e, b=bk, d_=dst, c=col, g=cg: e.tensor_copy(
                    d_[:, c, 512 * g:512 * (g + 1)], b.ap[:]), [(tc, cap)])
                bk.free = ((vc, vc.n),)
        esA.close()

    def phase_B(s, QlatT, CkvT, ctxT):
        esB = ExitStack()
        sbB = lambda nm, shape, dt: esB.enter_context(nc.sbuf_tensor(f"{nm}_b{s}", shape, dt))
        wqh = sbB("wqh", [128, 2, RQ], BF16)
        wkh = sbB("wkh", [128, 2, RK], BF16)
        wvh = sbB("wvh", [128, 2, RK], BF16)
        uvT = sbB("uvT", [128, 4, 128], BF16)
        mm_sb = sbB("mm_sb", [128, 8, 512], BF16)
        qhT = sbB("qhT", [128, 4, T], BF16)
        vup = sbB("vup", [128, NK, 128], BF16)
        p_sb = sbB("p_sb", [128, 2, T], BF16)
        pt4 = sbB("pt4", [128, NK, 512], BF16)
        mst = sbB("mst", [128, 8], F32)
        sstat = sbB("sstat", [128, 4], F32)

        whl = DStream(kb, f"wh{s}", 2)
        wh_free = [(), ()]
        lg_banks = [bkA, bkB, bkC, bkD, bkE, bkF]
        tp_banks = [bkT0, bkT1]
        rot = {"lg": 0, "tp": 0}
        prev_pv = [0]

        for h in range(16):
            par = h % 2
            whl.load("gpsimd", par, lambda e, p=par, hh=h: e.dma_start(
                out=wqh[:, p], in_=wqu[128 * hh:128 * (hh + 1), :]), wh_free[par])
            whl.load("gpsimd", par, lambda e, p=par, hh=h: e.dma_start(
                out=wkh[:, p], in_=wku[128 * hh:128 * (hh + 1), :]))
            tok_h = whl.load("gpsimd", par, lambda e, p=par, hh=h: e.dma_start(
                out=wvh[:, p], in_=wvu[128 * hh:128 * (hh + 1), :]))
            pump(6)
            for c in range(4):
                bk = tp_banks[rot["tp"] % 2]
                rot["tp"] += 1
                cap = kb.cop("tensor", lambda e, b=bk, p=par, cc=c: e.transpose(
                    b.ap[:, 0:128], wvh[:, p, 128 * cc:128 * (cc + 1)], idb[:]),
                    [tok_h] + list(bk.free))
                kb.cop("vector", lambda e, b=bk, cc=c: e.tensor_copy(
                    uvT[:, cc, :], b.ap[:, 0:128]), [(tc, cap)])
                bk.free = ((vc, vc.n),)
            for j in range(8):
                bk = lg_banks[rot["lg"] % 6]
                rot["lg"] += 1
                cap = kb.cop("tensor", lambda e, b=bk, p=par, jj=j: e.matmul(
                    b.ap[:], wqh[:, p, 128 * jj:128 * (jj + 1)], wkh[:, p],
                    start=True, stop=True), [tok_h] + list(bk.free))
                kb.cop("vector", lambda e, b=bk, jj=j: e.tensor_scalar(
                    mm_sb[:, jj, :], b.ap[:], 1.0 / math.sqrt(128.0), None,
                    op0=ALU.mult), [(tc, cap)])
                bk.free = ((vc, vc.n),)
            mm_done = vc.n
            wh_free[par] = [(tc, tc.n)]
            for rt in range(4):
                for g4 in range(4):
                    bk = lg_banks[rot["lg"] % 6]
                    rot["lg"] += 1
                    for j in range(8):
                        waits = ([(vc, mm_done)] + list(bk.free)) if j == 0 else []
                        kb.cop("tensor", lambda e, b=bk, r=rt, g=g4, jj=j, Q=QlatT:
                               e.matmul(b.ap[:], mm_sb[:, jj, 128 * r:128 * (r + 1)],
                                        Q[:, jj, 512 * g:512 * (g + 1)],
                                        start=(jj == 0), stop=(jj == 7)),
                               waits, count=(j == 7))
                    cap = tc.n
                    kb.cop("vector", lambda e, b=bk, r=rt, g=g4: e.tensor_copy(
                        qhT[:, r, 512 * g:512 * (g + 1)], b.ap[:]), [(tc, cap)])
                    bk.free = ((vc, vc.n),)
            uv_done = vc.n
            for st in range(NK):
                bk = lg_banks[rot["lg"] % 6]
                rot["lg"] += 1
                for c in range(4):
                    waits = ([(vc, uv_done)] + list(bk.free)) if c == 0 else []
                    kb.cop("tensor", lambda e, b=bk, s_=st, cc=c, C=CkvT: e.matmul(
                        b.ap[:, 0:128], C[:, cc, 128 * s_:128 * (s_ + 1)],
                        uvT[:, cc, :], start=(cc == 0), stop=(cc == 3)),
                        waits, count=(c == 3))
                cap = tc.n
                kb.cop("vector", lambda e, b=bk, s_=st: e.tensor_copy(
                    vup[:, s_, :], b.ap[:, 0:128]), [(tc, cap)])
                bk.free = ((vc, vc.n),)
            qh_done = vc.n

            def emit_tr(tq, p_done):
                par = tq % 2
                g = tq // 4
                m = tq - 4 * g
                for st in range(tq + 1):
                    bk = tp_banks[rot["tp"] % 2]
                    rot["tp"] += 1
                    cap = kb.cop("tensor", lambda e, b=bk, s_=st, p=par: e.transpose(
                        b.ap[:, 0:128], p_sb[:, p, 128 * s_:128 * (s_ + 1)], idb[:]),
                        [(vc, p_done)] + list(bk.free))
                    kb.cop("vector", lambda e, b=bk, s_=st, mm=m: e.tensor_copy(
                        pt4[:, s_, 128 * mm:128 * (mm + 1)], b.ap[:, 0:128]),
                        [(tc, cap)])
                    bk.free = ((vc, vc.n),)
                for st in range(tq + 1, 4 * g + 4):
                    kb.cop("vector", lambda e, s_=st, mm=m: e.memset(
                        pt4[:, s_, 128 * mm:128 * (mm + 1)], 0.0),
                        [(tc, prev_pv[0])])
                return vc.n

            def emit_pv(g):
                bk = lg_banks[rot["lg"] % 6]
                rot["lg"] += 1
                nst = 4 * g + 4
                for st in range(nst):
                    waits = ([(vc, pt_done[0])] + list(bk.free)) if st == 0 else []
                    kb.cop("tensor", lambda e, b=bk, s_=st, n=nst: e.matmul(
                        b.ap[:], vup[:, s_, :], pt4[:, s_, :],
                        start=(s_ == 0), stop=(s_ == n - 1)),
                        waits, count=(st == nst - 1))
                prev_pv[0] = tc.n
                kb.cop("vector", lambda e, b=bk, hh=h, gg=g, X=ctxT: e.tensor_copy(
                    X[:, hh, 512 * gg:512 * (gg + 1)], b.ap[:]), [(tc, prev_pv[0])])
                bk.free = ((vc, vc.n),)

            pend = None
            pt_done = [0]
            for tq in range(16):
                g = tq // 4
                m = tq - 4 * g
                par = tq % 2
                slen = 128 * (tq + 1)
                ng = (slen + 511) // 512
                exp_banks = []
                for scg in range(ng):
                    w = min(512, slen - 512 * scg)
                    bk = lg_banks[rot["lg"] % 6]
                    rot["lg"] += 1
                    diag = (scg == ng - 1)
                    for c in range(4):
                        waits = ([(vc, qh_done)] + list(bk.free)) if c == 0 else []
                        kb.cop("tensor", lambda e, b=bk, cc=c, q=tq, sg=scg, ww=w,
                               C=CkvT, dg=diag:
                               e.matmul(b.ap[:, 0:ww],
                                        qhT[:, cc, 128 * q:128 * (q + 1)],
                                        C[:, cc, 512 * sg:512 * sg + ww],
                                        start=(cc == 0), stop=(cc == 3 and not dg)),
                               waits, count=(c == 3 and not diag))
                    if diag:
                        kb.cop("tensor", lambda e, b=bk, ww=w: e.matmul(
                            b.ap[:, ww - 128:ww], idb[:], maskb[:],
                            start=False, stop=True), [], count=True)
                    cap = tc.n
                    kb.cop("vector", lambda e, b=bk, sg=scg, ww=w: e.reduce_max(
                        mst[:, sg:sg + 1], b.ap[:, 0:ww], axis=AX.X), [(tc, cap)])
                    exp_banks.append((bk, w))
                kb.cop("vector", lambda e, n=ng: e.reduce_max(
                    sstat[:, 0:1], mst[:, 0:n], axis=AX.X, negate=True),
                    [(vc, vc.n), (sc, sc.n)])
                negm_done = vc.n
                if pend is not None:
                    pt_done[0] = emit_tr(*pend)
                if m == 0 and g > 0:
                    emit_pv(g - 1)
                for scg, (bk, w) in enumerate(exp_banks):
                    kb.cop("scalar", lambda e, b=bk, sg=scg, ww=w, p=par:
                           e.activation(
                               p_sb[:, p, 512 * sg:512 * sg + ww], b.ap[:, 0:ww],
                               AF.Exp, bias=sstat[:, 0:1],
                               accum_out=mst[:, 4 + sg:5 + sg]),
                           [(vc, negm_done)])
                    bk.free = ((sc, sc.n),)
                exp_done = sc.n
                kb.cop("vector", lambda e, n=ng: e.reduce_sum(
                    sstat[:, 1:2], mst[:, 4:4 + n], axis=AX.X), [(sc, exp_done)])
                kb.cop("vector", lambda e: e.reciprocal(sstat[:, 2:3], sstat[:, 1:2]),
                       [(vc, vc.n)])
                kb.cop("vector", lambda e, L=slen, p=par: e.tensor_scalar(
                    p_sb[:, p, 0:L], p_sb[:, p, 0:L], sstat[:, 2:3], None,
                    op0=ALU.mult), [(vc, vc.n)])
                pend = (tq, vc.n)
            pt_done[0] = emit_tr(*pend)
            emit_pv(3)
        esB.close()

    def phase_C(s, ctxT):
        esC = ExitStack()
        sbC = lambda nm, shape, dt: esC.enter_context(nc.sbuf_tensor(f"{nm}_c{s}", shape, dt))
        wo_sb = sbC("wo_sb", [128, NK, D], BF16)
        xcg = sbC("xcg", [128, 4, D], F32)
        wsb = sbC("wsbC", [128, 2, 512], F32)
        piece = sbC("pieceC", [128, 2, 128], F32)

        wC = DStream(kb, f"wC{s}", 2)
        tok_wo1 = wC.load("gpsimd", 0, lambda e: e.dma_start(
            out=wo_sb[:, :, 0:1024], in_=woT[:, :, 0:1024]))
        tok_wo2 = wC.load("gpsimd", 1, lambda e: e.dma_start(
            out=wo_sb[:, :, 1024:2048], in_=woT[:, :, 1024:2048]))
        xC = DStream(kb, f"xC{s}", 1)
        stC = DStream(kb, f"stC{s}", 2)
        acc_banks = [bkA, bkB]
        tr_banks = [bkC, bkD]
        xc_free = []
        piece_free = [(), ()]
        wsb_free = [(), ()]
        rot = {"a": 0, "t": 0, "p": 0}
        dest = x1s if stage >= 2 else y_out

        for cg in range(4):
            tokx = xC.load("sync", 0, lambda e, c=cg: e.dma_start(
                out=xcg[:],
                in_=x_in[s, 512 * c:512 * (c + 1), :].rearrange("(t p) d -> p t d", p=128)),
                list(xc_free))
            pump(2)
            for dc in range(NK):
                bk = acc_banks[rot["a"] % 2]
                rot["a"] += 1
                for k in range(NK):
                    tw = tok_wo1 if dc < 8 else tok_wo2
                    waits = ([tw] + list(bk.free)) if k == 0 else []
                    kb.cop("tensor", lambda e, b=bk, d_=dc, kk=k, c=cg, X=ctxT: e.matmul(
                        b.ap[:], wo_sb[:, kk, 128 * d_:128 * (d_ + 1)],
                        X[:, kk, 512 * c:512 * (c + 1)],
                        start=(kk == 0), stop=(kk == NK - 1)),
                        waits, count=(k == NK - 1))
                cap = tc.n
                wp = dc % 2
                kb.cop("vector", lambda e, b=bk, d_=dc, p=wp: e.tensor_scalar(
                    wsb[:, p], b.ap[:], bo_sb[:, d_:d_ + 1], None, op0=ALU.add),
                    [(tc, cap)] + list(wsb_free[wp]))
                bk.free = ((vc, vc.n),)
                wsb_done = vc.n
                for t in range(4):
                    tb = tr_banks[rot["t"] % 2]
                    rot["t"] += 1
                    capt = kb.cop("tensor", lambda e, b=tb, p=wp, tt=t: e.transpose(
                        b.ap[:, 0:128], wsb[:, p, 128 * tt:128 * (tt + 1)], idf[:]),
                        [(vc, wsb_done)] + list(tb.free))
                    pp = rot["p"] % 2
                    rot["p"] += 1
                    kb.cop("vector", lambda e, b=tb, tt=t, d_=dc, q=pp: e.tensor_add(
                        piece[:, q], b.ap[:, 0:128], xcg[:, tt, 128 * d_:128 * (d_ + 1)]),
                        [(tc, capt), tokx] + list(piece_free[pp]))
                    tb.free = ((vc, vc.n),)
                    addcap = vc.n
                    stC.load("gpsimd", pp, lambda e, q=pp, r=512 * cg + 128 * t, d_=dc:
                             e.dma_start(
                                 out=dest[s, r:r + 128, 128 * d_:128 * (d_ + 1)],
                                 in_=piece[:, q]), [(vc, addcap)])
                    piece_free[pp] = [(stC.sems[pp], stC.sems[pp].n)]
                wsb_free[wp] = [(tc, tc.n)]
            xc_free = [(vc, vc.n)]
        esC.close()
        return stC.totals()

    def phase_E(s, stC_tot, prep_tot):
        esE = ExitStack()
        sbE = lambda nm, shape, dt: esE.enter_context(nc.sbuf_tensor(f"{nm}_e{s}", shape, dt))
        xn2T = sbE("xn2T", [128, NK, 512], BF16)
        h_sb = sbE("h_sb", [128, 8, 512], BF16)
        w13buf = sbE("w13buf", [128, 3, 2 * NK * 128], BF16)
        w2buf = sbE("w2buf", [128, 2, 4096], BF16)
        scores = sbE("scores", [128, 4, 8], F32)
        gtok = sbE("gtok", [128, 8], F32)
        gtmp = sbE("gtmp", [128, 3, 8], F32)
        gm = sbE("gm", [128, 2], F32)
        xcg1 = sbE("xcg1", [128, 4, D], F32)
        xn2_bf = sbE("xn2_bf", [128, 2, D], BF16)
        stats = sbE("estat", [128, 8], F32)
        tmp1 = sbE("tmp1", [128, 2, 512], BF16)
        wsb = sbE("wsbE", [128, 2, 512], F32)
        piece = sbE("pieceE", [128, 2, 128], F32)
        crow = sbE("crow", [1, 16], F32)
        brow = sbE("brow", [1, 16], F32)
        etab = sbE("etab", [1, 64], F32)
        esT = ExitStack()
        sbT = lambda nm, shape, dt: esT.enter_context(nc.sbuf_tensor(f"{nm}_e{s}", shape, dt))
        grow = sbT("grow", [1, 8, T], BF16)
        mrow = sbT("mrow", [1, T], BF16)
        zrow = sbT("zrow", [1, T], BF16)
        prow = sbT("prow", [1, T], F32)
        fill2 = sbT("fill2", [128, NTG // 128, 2], F32)
        hugec = sbT("hugec", [128, 1], F32)
        sdata = sbT("sdata", [128, 2, 2], F32)
        scol = sbT("scol", [128, 2, 4], F32)
        sidx = sbT("sidx", [128, 2, 1], I32)
        e4b = sbT("e4b", [128, 1], F32)

        x1L = DStream(kb, f"x1L{s}", 1)
        xnS = DStream(kb, f"xnS{s}", 2)
        w13L = DStream(kb, f"w13L{s}", 2)
        w2L = DStream(kb, f"w2L{s}", 2)
        stE = DStream(kb, f"stE{s}", 2)
        tfill = DStream(kb, f"tf{s}", 1)
        tscat = DStream(kb, f"ts{s}", 2)
        rload = DStream(kb, f"rl{s}", 4)
        wg13 = DStream(kb, f"wg13{s}", 3)
        wg2 = DStream(kb, f"wg2{s}", 3)
        xgL = DStream(kb, f"xgL{s}", 4)
        stR = DStream(kb, f"stR{s}", 1)
        u1_banks = [bkA, bkC]
        u3_banks = [bkB, bkD]
        acc_banks = [bkA, bkB]
        tr_banks = [bkC, bkD]
        x1_free = []
        w13_free = [(), (), ()]
        w2_free = [(), ()]
        piece_free = [(), ()]
        wsb_free = [(), ()]
        xn_free = [(), ()]
        tscat_free = [(), ()]
        rot = {"u": 0, "a": 0, "t": 0, "p": 0, "ih": 0}
        w13s_rows = w13s.rearrange("e h p u k f -> (e h p) (u k f)")
        w2s_rows = w2s.rearrange("g p e d h f -> (g p e) (d h f)")
        xn2s_rows = xn2s_t[s]
        rtg_s = rtg_t[s]
        y_flat = y_out.rearrange("q t d -> (q t) d")

        e4_steps = []
        e4refs = {}

        def _e4_fill():
            kb.cop("vector", lambda e: e.memset(zrow[:], 0.0), [(vc, vc.n)])
            kb.cop("vector", lambda e: e.memset(fill2[:, :, 0:1], float(1 << 23)), [])
            kb.cop("vector", lambda e: e.memset(fill2[:, :, 1:2], 0.0), [])
            fill_done = vc.n
            e4refs["ftok"] = tfill.load("gpsimd", 0, lambda e: e.dma_start(
                out=rtg_s.rearrange("(a p) c -> p a c", p=128), in_=fill2[:]),
                [(vc, fill_done)])
        e4_steps.append(_e4_fill)

        def _e4_scanA(ee):
            def f():
                kb.cop("vector", lambda e, x=ee: e.tensor_scalar(
                    mrow[:], grow[0:1, x, :], 0.0, None, op0=ALU.is_gt), [(vc, vc.n)])
                kb.cop("vector", lambda e: e.tensor_tensor_scan(
                    prow[:], mrow[:], zrow[:], 0.0, op0=ALU.add, op1=ALU.add),
                    [(vc, vc.n)])
                kb.cop("vector", lambda e, x=ee: e.tensor_copy(
                    crow[0:1, x:x + 1], prow[0:1, T - 1:T]), [(vc, vc.n)])
            return f
        for _ee in range(8):
            e4_steps.append(_e4_scanA(_ee))

        def _e4_bases():
            kb.cop("vector", lambda e: e.tensor_scalar(
                crow[0:1, 8:16], crow[0:1, 0:8], float(GRP - 1), 1.0 / GRP,
                op0=ALU.add, op1=ALU.mult), [(vc, vc.n)])
            kb.cop("vector", lambda e: e.tensor_scalar(
                crow[0:1, 8:16], crow[0:1, 8:16], -0.498046875, 12582912.0,
                op0=ALU.add, op1=ALU.add), [(vc, vc.n)])
            kb.cop("vector", lambda e: e.tensor_scalar(
                crow[0:1, 8:16], crow[0:1, 8:16], -12582912.0, float(GRP),
                op0=ALU.add, op1=ALU.mult), [(vc, vc.n)])
            kb.cop("vector", lambda e: e.tensor_tensor_scan(
                brow[0:1, 1:9], crow[0:1, 8:16], zrow[0:1, 0:8], 0.0,
                op0=ALU.add, op1=ALU.add), [(vc, vc.n)])
            kb.cop("vector", lambda e: e.memset(brow[0:1, 0:1], 0.0), [(vc, vc.n)])
            kb.cop("vector", lambda e: e.memset(etab[:], 0.0), [(vc, vc.n)])
            for x in range(1, 9):
                kb.cop("vector", lambda e, x_=x: e.tensor_scalar(
                    etab[0:1, 32:32 + NT], grprow_sb[0:1, 0:NT], brow[0:1, x_:x_ + 1],
                    None, op0=ALU.is_ge), [(vc, vc.n)])
                kb.cop("vector", lambda e: e.tensor_add(
                    etab[0:1, 0:NT], etab[0:1, 0:NT], etab[0:1, 32:32 + NT]),
                    [(vc, vc.n)])
            e4refs["etab_done"] = vc.n
        e4_steps.append(_e4_bases)

        def _e4_scanB(ee):
            def f():
                kb.cop("vector", lambda e, x=ee: e.tensor_scalar(
                    mrow[:], grow[0:1, x, :], 0.0, None, op0=ALU.is_gt),
                    [(vc, vc.n)])
                kb.cop("vector", lambda e: e.tensor_tensor_scan(
                    prow[:], mrow[:], zrow[:], 0.0, op0=ALU.add, op1=ALU.add),
                    [(vc, vc.n)])
                kb.cop("vector", lambda e, x=ee: e.tensor_scalar(
                    brow[0:1, 9:10], brow[0:1, x:x + 1], -1.0, None, op0=ALU.add),
                    [(vc, vc.n)])
                capb = kb.cop("tensor", lambda e: e.matmul(
                    bkF.ap[:, 0:1], onesf_sb[:], brow[0:1, 9:10],
                    start=True, stop=True), [(vc, vc.n)] + list(bkF.free))
                kb.cop("vector", lambda e: e.tensor_copy(e4b[:], bkF.ap[:, 0:1]),
                       [(tc, capb)])
                bkF.free = ((vc, vc.n),)
            return f

        def _e4_tile(ee, t):
            def f():
                par = t % 2
                cap1 = kb.cop("tensor", lambda e, tt=t: e.transpose(
                    bkF.ap[:, 0:1], prow[0:1, 128 * tt:128 * (tt + 1)],
                    idf[0:1, 0:1]), [(vc, vc.n)] + list(bkF.free))
                kb.cop("vector", lambda e, p=par: e.tensor_copy(
                    scol[:, p, 0:1], bkF.ap[:, 0:1]),
                    [(tc, cap1)] + list(tscat_free[par]))
                bkF.free = ((vc, vc.n),)
                cap2 = kb.cop("tensor", lambda e, x=ee, tt=t: e.transpose(
                    bkT0.ap[:, 0:1], grow[0:1, x, 128 * tt:128 * (tt + 1)],
                    idb[0:1, 0:1]), list(bkT0.free))
                kb.cop("vector", lambda e, p=par: e.tensor_copy(
                    scol[:, p, 1:2], bkT0.ap[:, 0:1]), [(tc, cap2)])
                bkT0.free = ((vc, vc.n),)
                kb.cop("vector", lambda e, p=par, tt=t: e.tensor_scalar(
                    sdata[:, p, 0:1], iota_sb[:], float(128 * tt), None, op0=ALU.add),
                    [(vc, vc.n)])
                kb.cop("vector", lambda e, p=par: e.tensor_copy(
                    sdata[:, p, 1:2], scol[:, p, 1:2]), [(vc, vc.n)])
                kb.cop("vector", lambda e, p=par: e.tensor_add(
                    scol[:, p, 2:3], scol[:, p, 0:1], e4b[:]), [(vc, vc.n)])
                kb.cop("vector", lambda e, p=par: e.tensor_scalar(
                    scol[:, p, 3:4], scol[:, p, 1:2], 0.0, None, op0=ALU.is_gt),
                    [(vc, vc.n)])
                kb.cop("vector", lambda e, p=par: e.tensor_scalar(
                    scol[:, p, 2:3], scol[:, p, 2:3], -float(1 << 23), None,
                    op0=ALU.add), [(vc, vc.n)])
                kb.cop("vector", lambda e, p=par: e.scalar_tensor_tensor(
                    scol[:, p, 2:3], scol[:, p, 2:3], 1.0, scol[:, p, 3:4],
                    op0=ALU.mult, op1=ALU.mult), [(vc, vc.n)])
                kb.cop("vector", lambda e, p=par: e.tensor_scalar(
                    scol[:, p, 2:3], scol[:, p, 2:3], float(1 << 23), None,
                    op0=ALU.add), [(vc, vc.n)])
                kb.cop("vector", lambda e, p=par: e.tensor_copy(
                    sidx[:, p], scol[:, p, 2:3]), [(vc, vc.n)])
                dcap = vc.n
                stok = tscat.load("gpsimd", par, lambda e, p=par: e.indirect_dma_start(
                    out=rtg_s, out_offset=IndirectOffsetOnAxis(ap=sidx[:, p], axis=0),
                    in_=sdata[:, p], in_offset=None,
                    bounds_check=bregs["tab"], oob_is_err=False),
                    [(vc, dcap), e4refs["ftok"]])
                tscat_free[par] = [stok]
            return f
        for _ee in range(8):
            e4_steps.append(_e4_scanB(_ee))
            for _t in range(NK):
                e4_steps.append(_e4_tile(_ee, _t))
        e4_state = {"i": 0}

        def e4pump(n):
            for _ in range(n):
                i = e4_state["i"]
                if i >= len(e4_steps):
                    return
                e4_state["i"] = i + 1
                e4_steps[i]()

        for grp in range(4):
            tokx = x1L.load("sync", 0, lambda e, g=grp: e.dma_start(
                out=xcg1[:],
                in_=x1s[s, 512 * g:512 * (g + 1), :].rearrange("(t p) d -> p t d", p=128)),
                list(stC_tot) + list(x1_free))
            tile_caps = []
            for t in range(4):
                par = t % 2
                xnd = _ln(kb, xcg1[:, t, :], xn2_bf[:, par], stats, xn2_bf[:, par],
                          eps_sb[:, 0:1], tokx, [(tc, tc.n)] + list(xn_free[par]))
                xnS.load("gpsimd", par, lambda e, p=par, r=512 * grp + 128 * t:
                         e.dma_start(out=xn2s_rows[r:r + 128, :], in_=xn2_bf[:, p]),
                         [(vc, xnd)])
                xn_free[par] = [(xnS.sems[par], xnS.sems[par].n)]
                for k in range(NK):
                    bk = bkT0 if (t * NK + k) % 2 == 0 else bkT1
                    cap = kb.cop("tensor", lambda e, b=bk, kk=k, p=par: e.transpose(
                        b.ap[:, 0:128], xn2_bf[:, p, 128 * kk:128 * (kk + 1)], idb[:]),
                        [(vc, xnd)] + list(bk.free))
                    kb.cop("vector", lambda e, b=bk, kk=k, tt=t: e.tensor_copy(
                        xn2T[:, kk, 128 * tt:128 * (tt + 1)], b.ap[:, 0:128]), [(tc, cap)])
                    bk.free = ((vc, vc.n),)
                tile_caps.append(vc.n)
            for t in range(4):
                for k in range(NK):
                    waits = ([(vc, tile_caps[t])] + list(bkE.free)) if k == 0 else []
                    kb.cop("tensor", lambda e, kk=k, tt=t: e.matmul(
                        bkE.ap[:, 0:8], xn2T[:, kk, 128 * tt:128 * (tt + 1)],
                        gateT_sb[:, kk, :], start=(kk == 0), stop=(kk == NK - 1)),
                        waits, count=(k == NK - 1))
                cap = tc.n
                kb.cop("scalar", lambda e, tt=t: e.activation(
                    scores[:, tt, :], bkE.ap[:, 0:8], AF.Sigmoid), [(tc, cap), (vc, vc.n)])
                bkE.free = ((sc, sc.n),)
                sgd = sc.n
                kb.cop("vector", lambda e, tt=t: e.tensor_add(
                    gtmp[:, 0, :], scores[:, tt, :], eb_sb[:]), [(sc, sgd), (vc, vc.n)])
                kb.cop("vector", lambda e: e.reduce_max(gm[:, 0:1], gtmp[:, 0, :],
                                                        axis=AX.X), [(vc, vc.n)])
                kb.cop("vector", lambda e: e.tensor_scalar(
                    gtmp[:, 1, :], gtmp[:, 0, :], gm[:, 0:1], None, op0=ALU.is_equal),
                    [(vc, vc.n)])
                kb.cop("vector", lambda e: e.scalar_tensor_tensor(
                    gtmp[:, 2, :], gtmp[:, 1, :], NEGM, gtmp[:, 0, :],
                    op0=ALU.mult, op1=ALU.add), [(vc, vc.n)])
                kb.cop("vector", lambda e: e.reduce_max(gm[:, 1:2], gtmp[:, 2, :],
                                                        axis=AX.X), [(vc, vc.n)])
                kb.cop("vector", lambda e: e.tensor_scalar(
                    gtmp[:, 2, :], gtmp[:, 2, :], gm[:, 1:2], None, op0=ALU.is_equal),
                    [(vc, vc.n)])
                kb.cop("vector", lambda e: e.tensor_add(
                    gtmp[:, 1, :], gtmp[:, 1, :], gtmp[:, 2, :]), [(vc, vc.n)])
                kb.cop("vector", lambda e, tt=t: e.tensor_mul(
                    gtok[:], scores[:, tt, :], gtmp[:, 1, :]), [(vc, vc.n), (tc, tc.n)])
                gdone = vc.n
                for ee in range(8):
                    capg = kb.cop("tensor", lambda e, x=ee: e.transpose(
                        bkF.ap[0:1, 0:128], gtok[:, x:x + 1], idf[:]),
                        [(vc, gdone)] + list(bkF.free))
                    kb.cop("vector", lambda e, x=ee, c0=512 * grp + 128 * t: e.tensor_copy(
                        grow[0:1, x, c0:c0 + 128], bkF.ap[0:1, 0:128]), [(tc, capg)])
                    bkF.free = ((vc, vc.n),)
            for hc in range(8):
                ih = rot["ih"]
                rot["ih"] += 1
                wp = ih % 2
                tokw = w13L.load("sync", wp, lambda e, p=wp, hh=hc: e.dma_start(
                    out=w13buf[:, p], in_=w13s[8, hh]),
                    list(prep_tot) + list(w13_free[wp]))
                b1 = u1_banks[rot["u"] % 2]
                b3 = u3_banks[rot["u"] % 2]
                rot["u"] += 1
                for k in range(NK):
                    waits = ([tokw, (vc, tile_caps[3])] + list(b1.free)) if k == 0 else []
                    kb.cop("tensor", lambda e, b=b1, p=wp, kk=k: e.matmul(
                        b.ap[:], w13buf[:, p, 128 * kk:128 * (kk + 1)], xn2T[:, kk, :],
                        start=(kk == 0), stop=(kk == NK - 1)),
                        waits, count=(k == NK - 1))
                cap1 = tc.n
                for k in range(NK):
                    waits = list(b3.free) if k == 0 else []
                    kb.cop("tensor", lambda e, b=b3, p=wp, kk=k: e.matmul(
                        b.ap[:], w13buf[:, p, 2048 + 128 * kk:2048 + 128 * (kk + 1)], xn2T[:, kk, :],
                        start=(kk == 0), stop=(kk == NK - 1)),
                        waits, count=(k == NK - 1))
                cap3 = tc.n
                w13_free[wp] = [(tc, cap3)]
                hp = hc % 2
                kb.cop("scalar", lambda e, b=b1, p=hp: e.activation(
                    tmp1[:, p], b.ap[:], AF.Silu), [(tc, cap1), (vc, vc.n), (sc, sc.n)])
                b1.free = ((sc, sc.n),)
                sil = sc.n
                kb.cop("vector", lambda e, b=b3, hh=hc, p=hp: e.tensor_mul(
                    h_sb[:, hh, :], b.ap[:], tmp1[:, p]),
                    [(tc, cap3), (sc, sil), (vc, vc.n)])
                b3.free = ((vc, vc.n),)
                if grp == 3:
                    e4pump(8)
            h_done = vc.n
            pendw = None

            def emit_epi(dc, wsb_done):
                sp = dc % 2
                for t in range(4):
                    tb = tr_banks[rot["t"] % 2]
                    rot["t"] += 1
                    capt = kb.cop("tensor", lambda e, b=tb, p=sp, tt=t: e.transpose(
                        b.ap[:, 0:128], wsb[:, p, 128 * tt:128 * (tt + 1)], idf[:]),
                        [(vc, wsb_done)] + list(tb.free))
                    pp = rot["p"] % 2
                    rot["p"] += 1
                    kb.cop("vector", lambda e, b=tb, tt=t, d_=dc, q=pp: e.tensor_add(
                        piece[:, q], b.ap[:, 0:128], xcg1[:, tt, 128 * d_:128 * (d_ + 1)]),
                        [(tc, capt), tokx] + list(piece_free[pp]))
                    tb.free = ((vc, vc.n),)
                    addcap = vc.n
                    stE.load("gpsimd", pp, lambda e, q=pp, r=512 * grp + 128 * t, d_=dc:
                             e.dma_start(
                                 out=y_out[s, r:r + 128, 128 * d_:128 * (d_ + 1)],
                                 in_=piece[:, q]), [(vc, addcap)])
                    piece_free[pp] = [(stE.sems[pp], stE.sems[pp].n)]
                wsb_free[sp] = [(tc, tc.n)]

            for dc in range(NK):
                wp = dc % 2
                tokw2 = w2L.load("sync", wp, lambda e, p=wp, d_=dc: e.dma_start(
                    out=w2buf[:, p, 0:1024], in_=w2s[d_ // 4, :, 8, d_ % 4]),
                    list(prep_tot) + list(w2_free[wp]))
                bk = acc_banks[rot["a"] % 2]
                rot["a"] += 1
                for hc in range(8):
                    waits = ([(vc, h_done), tokw2] + list(bk.free)) if hc == 0 else []
                    kb.cop("tensor", lambda e, b=bk, p=wp, hh=hc: e.matmul(
                        b.ap[:], w2buf[:, p, 128 * hh:128 * (hh + 1)], h_sb[:, hh, :],
                        start=(hh == 0), stop=(hh == 7)),
                        waits, count=(hc == 7))
                cap = tc.n
                w2_free[wp] = [(tc, cap)]
                sp = dc % 2
                kb.cop("vector", lambda e, b=bk, p=sp: e.tensor_copy(
                    wsb[:, p], b.ap[:]), [(tc, cap)] + list(wsb_free[sp]))
                bk.free = ((vc, vc.n),)
                if pendw is not None:
                    emit_epi(*pendw)
                pendw = (dc, vc.n)
                if grp == 3:
                    e4pump(12)
            emit_epi(*pendw)
            x1_free = [(vc, vc.n)]
        scores_done = vc.n
        e4pump(len(e4_steps))
        xn_store_tot = xnS.totals()
        shared_store_tot = stE.totals()
        tscat_tot = tscat.totals()
        esT.close()
        xg = sbE("xg", [128, 2, 2, D], BF16)
        xgT = sbE("xgT", [128, NK, 256], BF16)
        yr = sbE("yr", [128, 2, 2, D], F32)
        rtgt = sbE("rtgt", [128, 2, 2, 2], F32)
        tidx = sbE("tidx", [128, 2, 2, 1], I32)
        gvc = sbE("gvc", [128, 2, 2, 1], F32)
        ebtab = sbE("ebtab", [128, 32], F32)
        etab_bf = sbE("etab_bf", [1, 32], BF16)
        w13tf = sbE("w13tf", [128, 8, NT], F32)
        w2tf = sbE("w2tf", [128, 4, NT], F32)
        w13ti = sbE("w13ti", [128, 8, NT, 1], I32)
        w2ti = sbE("w2ti", [128, 4, NT, 1], I32)

        kb.cop("scalar", lambda e: e.copy(etab_bf[:], etab[0:1, 0:32]),
               [(vc, e4refs["etab_done"])])
        capb = kb.cop("tensor", lambda e: e.matmul(
            bkF.ap[:, 0:32], ones_sb[:], etab_bf[:], start=True, stop=True),
            [(sc, sc.n)] + list(bkF.free))
        kb.cop("vector", lambda e: e.tensor_copy(ebtab[:], bkF.ap[:, 0:32]),
               [(tc, capb), (vc, vc.n)])
        bkF.free = ((vc, vc.n),)
        kb.cop("vector", lambda e: e.tensor_scalar(
            w13tf[:, 0, :], ebtab[:, 0:NT], 1024.0, iota_sb[:, 0:1],
            op0=ALU.mult, op1=ALU.add), [(vc, vc.n)])
        for hh in range(1, 8):
            kb.cop("vector", lambda e, h_=hh: e.tensor_scalar(
                w13tf[:, h_, :], w13tf[:, 0, :], float(128 * h_), None,
                op0=ALU.add), [(vc, vc.n)])
        kb.cop("vector", lambda e: e.tensor_scalar(
            w2tf[:, 0, :], ebtab[:, 0:NT], 1.0, iota9_sb[:, 0:1],
            op0=ALU.mult, op1=ALU.add), [(vc, vc.n)])
        for d_ in range(1, 4):
            kb.cop("vector", lambda e, dd=d_: e.tensor_scalar(
                w2tf[:, dd, :], w2tf[:, 0, :], float(1152 * dd), None,
                op0=ALU.add), [(vc, vc.n)])
        kb.cop("vector", lambda e: e.tensor_copy(
            w13ti.rearrange("p h t o -> p (h t o)"),
            w13tf.rearrange("p h t -> p (h t)")), [(vc, vc.n)])
        kb.cop("vector", lambda e: e.tensor_copy(
            w2ti.rearrange("p d t o -> p (d t o)"),
            w2tf.rearrange("p d t -> p (d t)")), [(vc, vc.n)])
        tabs_done = vc.n

        bkEb = Bank(bkE.ap.bitcast(BF16))
        bkEb.free = bkE.free
        tpool = [bkT0, bkT1, bkEb]
        trot = [0]
        trpool = [bkC, bkD, bkF]
        kb.cop("vector", lambda e: e.memset(xg[:], 0.0), [(vc, vc.n)])
        xg_clear = vc.n
        prev_scat = []
        xg_free = {}
        rl_free = [(), ()]
        tok_free = [(), ()]
        yr_free = [(), ()]
        for gi in range(NT):
            gp = gi % 2
            rtoks = []
            for t in range(2):
                rtoks.append(rload.load("sync", 2 * gp + t,
                    lambda e, i=gi, tt=t, g=gp: e.dma_start(
                        out=rtgt[:, g, tt],
                        in_=rtg_s[GRP * i + 128 * tt:GRP * i + 128 * (tt + 1), :]),
                    (list(tscat_tot) + list(rl_free[gp])) if t == 0 else []))
            kb.cop("vector", lambda e, g=gp: e.tensor_copy(
                tidx[:, g, 0, :], rtgt[:, g, 0, 0:1]),
                [rtoks[0], rtoks[1], (vc, vc.n)] + list(tok_free[gp]))
            kb.cop("vector", lambda e, g=gp: e.tensor_copy(
                tidx[:, g, 1, :], rtgt[:, g, 1, 0:1]), [(vc, vc.n)])
            kb.cop("vector", lambda e, g=gp: e.tensor_copy(
                gvc[:, g, 0, :], rtgt[:, g, 0, 1:2]), [(vc, vc.n)])
            kb.cop("vector", lambda e, g=gp: e.tensor_copy(
                gvc[:, g, 1, :], rtgt[:, g, 1, 1:2]), [(vc, vc.n)])
            rl_free[gp] = [(vc, vc.n)]
            tcap = vc.n
            for t in range(2):
                gtokx = xgL.load("gpsimd", 2 * gp + t, lambda e, g=gp, tt=t: e.indirect_dma_start(
                    out=xg[:, g, tt], out_offset=None,
                    in_=xn2s_rows,
                    in_offset=IndirectOffsetOnAxis(ap=tidx[:, g, tt], axis=0),
                    bounds_check=bregs["tok"], oob_is_err=False),
                    ([(vc, tcap)] + list(xn_store_tot) + list(xg_free.get((gp, t), [(vc, xg_clear)]))))
                kb.cop("vector", lambda e, g=gp, tt=t: e.tensor_scalar(
                    xg[:, g, tt], xg[:, g, tt], gvc[:, g, tt, 0:1], None, op0=ALU.mult),
                    [gtokx, (vc, vc.n)])
                xsc = vc.n
                xg_free[(gp, t)] = [(vc, xsc)]
                for k in range(NK):
                    bk = tpool[trot[0] % 3]
                    trot[0] += 1
                    capk = kb.cop("tensor", lambda e, b=bk, kk=k, g=gp, tt=t: e.transpose(
                        b.ap[:, 0:128], xg[:, g, tt, 128 * kk:128 * (kk + 1)], idb[:]),
                        [(vc, xsc)] + list(bk.free))
                    kb.cop("vector", lambda e, b=bk, kk=k, tt=t: e.tensor_copy(
                        xgT[:, kk, 128 * tt:128 * (tt + 1)], b.ap[:, 0:128]),
                        [(tc, capk)])
                    bk.free = ((vc, vc.n),)
            xgT_done = vc.n
            for hc in range(8):
                ih = rot["ih"]
                rot["ih"] += 1
                wp = ih % 3
                tokw = wg13.load("gpsimd", wp, lambda e, p=wp, i=gi, hh=hc:
                                 e.indirect_dma_start(
                    out=w13buf[:, p], out_offset=None,
                    in_=w13s_rows,
                    in_offset=IndirectOffsetOnAxis(ap=w13ti[:, hh, i], axis=0),
                    bounds_check=bregs["w13"], oob_is_err=False),
                    [(vc, tabs_done)] + list(prep_tot) + list(w13_free[wp]))
                b1 = u1_banks[rot["u"] % 2]
                b3 = u3_banks[rot["u"] % 2]
                rot["u"] += 1
                for k in range(NK):
                    waits = ([tokw, (vc, xgT_done)] + list(b1.free)) if k == 0 else []
                    kb.cop("tensor", lambda e, b=b1, p=wp, kk=k: e.matmul(
                        b.ap[:, 0:GRP], w13buf[:, p, 128 * kk:128 * (kk + 1)],
                        xgT[:, kk, :],
                        start=(kk == 0), stop=(kk == NK - 1)),
                        waits, count=(k == NK - 1))
                cap1 = tc.n
                for k in range(NK):
                    waits = list(b3.free) if k == 0 else []
                    kb.cop("tensor", lambda e, b=b3, p=wp, kk=k: e.matmul(
                        b.ap[:, 0:GRP], w13buf[:, p, 2048 + 128 * kk:2048 + 128 * (kk + 1)],
                        xgT[:, kk, :],
                        start=(kk == 0), stop=(kk == NK - 1)),
                        waits, count=(k == NK - 1))
                cap3 = tc.n
                w13_free[wp] = [(tc, cap3)]
                hp = hc % 2
                kb.cop("scalar", lambda e, b=b1, p=hp: e.activation(
                    tmp1[:, p, 0:GRP], b.ap[:, 0:GRP], AF.Silu),
                    [(tc, cap1), (vc, vc.n), (sc, sc.n)])
                b1.free = ((sc, sc.n),)
                sil = sc.n
                kb.cop("vector", lambda e, b=b3, hh=hc, p=hp: e.tensor_mul(
                    h_sb[:, hh, 0:GRP], b.ap[:, 0:GRP], tmp1[:, p, 0:GRP]),
                    [(tc, cap3), (sc, sil), (vc, vc.n)])
                b3.free = ((vc, vc.n),)
            h_done = vc.n
            for dc in range(NK):
                wp = (dc // 4) % 2
                if dc % 4 == 0:
                    tokw2 = wg2.load("gpsimd", wp, lambda e, p=wp, i=gi, g_=dc // 4:
                                     e.indirect_dma_start(
                        out=w2buf[:, p], out_offset=None,
                        in_=w2s_rows,
                        in_offset=IndirectOffsetOnAxis(ap=w2ti[:, g_, i], axis=0),
                        bounds_check=bregs["w2"], oob_is_err=False),
                        [(vc, tabs_done)] + list(prep_tot) + list(w2_free[wp]))
                bk = acc_banks[rot["a"] % 2]
                rot["a"] += 1
                for hc in range(8):
                    waits = ([(vc, h_done), tokw2] + list(bk.free)) if hc == 0 else []
                    kb.cop("tensor", lambda e, b=bk, p=wp, hh=hc, d4=dc % 4: e.matmul(
                        b.ap[:, 0:GRP],
                        w2buf[:, p, 1024 * d4 + 128 * hh:1024 * d4 + 128 * (hh + 1)],
                        h_sb[:, hh, 0:GRP],
                        start=(hh == 0), stop=(hh == 7)),
                        waits, count=(hc == 7))
                cap = tc.n
                if dc % 4 == 3:
                    w2_free[wp] = [(tc, cap)]
                sp = dc % 2
                kb.cop("vector", lambda e, b=bk, p=sp: e.tensor_copy(
                    wsb[:, p, 0:GRP], b.ap[:, 0:GRP]), [(tc, cap)] + list(wsb_free[sp]))
                bk.free = ((vc, vc.n),)
                wsb_done = vc.n
                for t in range(2):
                    tb = trpool[rot["t"] % 3]
                    rot["t"] += 1
                    capt = kb.cop("tensor", lambda e, b=tb, p=sp, tt=t: e.transpose(
                        b.ap[:, 0:128], wsb[:, p, 128 * tt:128 * (tt + 1)], idf[:]),
                        [(vc, wsb_done)] + list(tb.free))
                    kb.cop("vector", lambda e, b=tb, tt=t, d_=dc, g=gp: e.tensor_copy(
                        yr[:, g, tt, 128 * d_:128 * (d_ + 1)], b.ap[:, 0:128]),
                        [(tc, capt)] + (list(yr_free[gp]) if dc == 0 else []))
                    tb.free = ((vc, vc.n),)
                wsb_free[sp] = [(tc, tc.n)]
            yr_done = vc.n
            for t in range(2):
                stok = stR.load("gpsimd", 0, lambda e, tt=t, g=gp: e.indirect_dma_start(
                    out=y_flat, out_offset=IndirectOffsetOnAxis(ap=tidx[:, g, tt], axis=0),
                    in_=yr[:, g, tt], in_offset=None, element_offset=s * T * D,
                    bounds_check=bregs["tok"], oob_is_err=False, compute_op=ALU.add),
                    [(vc, yr_done)] + list(shared_store_tot) + list(prev_scat))
                prev_scat = [stok]
            yr_free[gp] = [stok]
            tok_free[gp] = [stok]
        bkE.free = bkEb.free
        esE.close()
        return list(stE.totals()) + list(stR.totals())

    out_tots = []
    prep_closed = False
    for s in range(SEQ):
        esR = ExitStack()
        QlatT = esR.enter_context(nc.sbuf_tensor(f"QlatT{s}", [128, 8, T], BF16, side="right"))
        CkvT = esR.enter_context(nc.sbuf_tensor(f"CkvT{s}", [128, 4, T], BF16, side="right"))
        phase_A(s, QlatT, CkvT)
        kb.barrier()
        esCtx = ExitStack()
        ctxT = esCtx.enter_context(nc.sbuf_tensor(f"ctxT{s}", [128, NK, T], BF16))
        phase_B(s, QlatT, CkvT, ctxT)
        kb.barrier()
        esR.close()
        stC_tot = phase_C(s, ctxT)
        esCtx.close()
        if stage >= 2:
            if not prep_closed:
                pump(len(prep_jobs))
                es_prep.close()
                prep_closed = True
            prep_tot = prep_st.totals()
            kb.barrier()
            out_tots.append(phase_E(s, stC_tot, prep_tot))
            kb.barrier()
        else:
            out_tots.append(stC_tot)
            kb.barrier()
    if not prep_closed:
        pump(len(prep_jobs))
        es_prep.close()

    final_waits = [tok for tots in out_tots for tok in tots]
    kb.op("gpsimd", None, waits=final_waits)

    es_final = ExitStack()
    es_final.enter_context(es_outer)
    block = es_final.enter_context(nc.Block())
    kb.emit(nc, block, es_final)
    es_final.close()
    return nc


def _ln(kb, src_ap, xn_out, stat, sq, eps_ap, ld_tok, extra_waits=()):
    vc, scc = kb.ecnt["vector"], kb.ecnt["scalar"]
    w0 = [ld_tok, (vc, vc.n)] + list(extra_waits)
    kb.cop("vector", lambda e, a=src_ap, o=stat: e.reduce_sum(
        o[:, 0:1], a, axis=AX.X), w0)
    kb.cop("vector", lambda e, a=src_ap, o=stat, q=sq: e.scalar_tensor_tensor(
        q[:], a, 1.0, a, op0=ALU.mult, op1=ALU.mult, accum_out=o[:, 1:2]),
        [ld_tok])
    kb.cop("vector", lambda e, o=stat: e.tensor_scalar(
        o[:, 2:3], o[:, 0:1], -1.0 / D, None, op0=ALU.mult), [(vc, vc.n)])
    kb.cop("vector", lambda e, o=stat: e.tensor_mul(
        o[:, 3:4], o[:, 2:3], o[:, 2:3]), [(vc, vc.n)])
    kb.cop("vector", lambda e, o=stat: e.scalar_tensor_tensor(
        o[:, 4:5], o[:, 1:2], 1.0 / D, o[:, 3:4], op0=ALU.mult, op1=ALU.subtract),
        [(vc, vc.n)])
    cv = vc.n
    kb.cop("scalar", lambda e, o=stat, b_=eps_ap: e.activation(
        o[:, 5:6], o[:, 4:5], AF.Sqrt, bias=b_), [(vc, cv)])
    kb.cop("vector", lambda e, o=stat: e.reciprocal(o[:, 6:7], o[:, 5:6]),
           [(scc, scc.n)])
    kb.cop("vector", lambda e, a=src_ap, o=stat, x=xn_out: e.tensor_scalar(
        x[:], a, o[:, 2:3], o[:, 6:7], op0=ALU.add, op1=ALU.mult), [(vc, vc.n)])
    return vc.n


def _pack(inputs):
    f32 = np.float32
    asc = np.ascontiguousarray
    wqd = np.asarray(inputs["L0_attn_Wq_down"], f32)
    wkvd = np.asarray(inputs["L0_attn_Wkv_down"], f32)
    wo = np.asarray(inputs["L0_attn_Wo"], f32)
    gate = np.asarray(inputs["L0_ffn_gate"], f32)
    bo = np.asarray(inputs["L0_attn_bo"], f32)
    eb = np.asarray(inputs["L0_ffn_expert_bias"], f32)
    w1 = np.asarray(inputs["L0_ffn_w1"], f32)
    w2 = np.asarray(inputs["L0_ffn_w2"], f32)
    w3 = np.asarray(inputs["L0_ffn_w3"], f32)
    sw1 = np.asarray(inputs["L0_ffn_shared_w1"], f32)
    sw2 = np.asarray(inputs["L0_ffn_shared_w2"], f32)
    sw3 = np.asarray(inputs["L0_ffn_shared_w3"], f32)

    d = {}
    d["wqdT"] = asc(wqd.reshape(RQ, NK, 128).transpose(2, 1, 0))
    d["wkvdT"] = asc(wkvd.reshape(RK, NK, 128).transpose(2, 1, 0))
    d["woT"] = asc(wo.reshape(D, NK, 128).transpose(2, 1, 0))
    d["wqu"] = asc(np.asarray(inputs["L0_attn_Wq_up"], f32))
    d["wku"] = asc(np.asarray(inputs["L0_attn_Wk_up"], f32))
    d["wvu"] = asc(np.asarray(inputs["L0_attn_Wv_up"], f32))
    d["bo_p"] = asc(bo.reshape(NK, 128).T)
    d["gateT_p"] = asc(gate.T.reshape(NK, 128, 8).transpose(1, 0, 2))
    d["eb_p"] = asc(np.broadcast_to(eb, (128, 8)))
    w1all = np.concatenate([w1, sw1[None]], axis=0)
    w3all = np.concatenate([w3, sw3[None]], axis=0)
    w2all = np.concatenate([w2, sw2[None]], axis=0)
    w1p = w1all.reshape(NE, 8, 128, NK, 128).transpose(0, 1, 4, 3, 2)
    w3p = w3all.reshape(NE, 8, 128, NK, 128).transpose(0, 1, 4, 3, 2)
    d["w13p"] = asc(np.stack([w1p, w3p], axis=3))
    d["w2p"] = asc(w2all.reshape(NE, 4, 4, 128, 8, 128).transpose(1, 5, 0, 2, 4, 3))
    d["idf"] = np.eye(128, dtype=f32)
    d["idb"] = np.eye(128, dtype=f32).astype(BF)
    jj, ii = np.meshgrid(np.arange(128), np.arange(128))
    d["maskb"] = np.where(jj > ii, np.float32(NEGM), np.float32(0.0)).astype(BF)
    d["iota_in"] = np.arange(128, dtype=f32)[:, None]
    d["grprow"] = (np.arange(32, dtype=f32) * GRP)[None, :]
    return d


_NC_CACHE = {}


def _get_nc(stage=2):
    if stage not in _NC_CACHE:
        _NC_CACHE[stage] = build_program(stage)
    return _NC_CACHE[stage]


def _run(inputs, stage=2):
    shared = _pack(inputs)
    x = np.asarray(inputs["L0_attn_x"], np.float32)
    in_maps = []
    for c in range(NCORE):
        m = dict(shared)
        m["x"] = np.ascontiguousarray(x[SEQ * c:SEQ * (c + 1)])
        in_maps.append(m)
    nc = _get_nc(stage)
    res = run_bass_kernel_spmd(nc, in_maps, core_ids=list(range(NCORE)))
    out = np.concatenate([np.asarray(r["y"]) for r in res.results], axis=0)
    return out


def kernel(L0_attn_x, L0_attn_Wq_down, L0_attn_Wkv_down, L0_attn_Wq_up,
           L0_attn_Wk_up, L0_attn_Wv_up, L0_attn_Wo, L0_attn_bo,
           L0_ffn_gate, L0_ffn_expert_bias, L0_ffn_w1, L0_ffn_w2, L0_ffn_w3,
           L0_ffn_shared_w1, L0_ffn_shared_w2, L0_ffn_shared_w3):
    inputs = dict(
        L0_attn_x=L0_attn_x, L0_attn_Wq_down=L0_attn_Wq_down,
        L0_attn_Wkv_down=L0_attn_Wkv_down, L0_attn_Wq_up=L0_attn_Wq_up,
        L0_attn_Wk_up=L0_attn_Wk_up, L0_attn_Wv_up=L0_attn_Wv_up,
        L0_attn_Wo=L0_attn_Wo, L0_attn_bo=L0_attn_bo, L0_ffn_gate=L0_ffn_gate,
        L0_ffn_expert_bias=L0_ffn_expert_bias, L0_ffn_w1=L0_ffn_w1,
        L0_ffn_w2=L0_ffn_w2, L0_ffn_w3=L0_ffn_w3,
        L0_ffn_shared_w1=L0_ffn_shared_w1, L0_ffn_shared_w2=L0_ffn_shared_w2,
        L0_ffn_shared_w3=L0_ffn_shared_w3)
    return _run(inputs, stage=2)


# version 16
# speedup vs baseline: 1.0684x; 1.0561x over previous
import math
import numpy as np
import ml_dtypes
import concourse.bass as bass
import concourse.mybir as mybir
from concourse.bass_utils import run_bass_kernel_spmd
from concourse.bass_core_rust import IndirectOffsetOnAxis
from contextlib import ExitStack

F32 = mybir.dt.float32
I32 = mybir.dt.int32
BF16 = mybir.dt.bfloat16
AF = mybir.ActivationFunctionType
ALU = mybir.AluOpType
AX = mybir.AxisListType

BF = ml_dtypes.bfloat16
NEGM = -1.0e30
D = 2048
T = 2048
RQ = 1024
RK = 512
NE = 9
NCORE = 8
SEQ = 2
NK = 16
EPS = 1e-5
GRP = 256
NT = 23
NTG = NT * GRP

ENGS = ("sync", "gpsimd", "tensor", "vector", "scalar")


class Ctr:
    __slots__ = ("name", "n", "h")

    def __init__(self, name):
        self.name = name
        self.n = 0
        self.h = None


class Kb:
    def __init__(self):
        self.ctrs = []
        self.items = {e: [] for e in ENGS}
        self.ecnt = {e: self.ctr("c_" + e) for e in ("tensor", "vector", "scalar")}
        self._bar = {e: () for e in ENGS}

    def ctr(self, name):
        c = Ctr(name)
        self.ctrs.append(c)
        return c

    def op(self, eng, fn, waits=(), inc=None, incn=1):
        ws = tuple((c, v) for (c, v) in waits if v > 0)
        if self._bar[eng]:
            ws = self._bar[eng] + ws
            self._bar[eng] = ()
        self.items[eng].append((fn, ws, inc, incn))
        if inc is not None:
            inc.n += incn

    def cop(self, eng, fn, waits=(), count=True):
        c = self.ecnt[eng]
        self.op(eng, fn, waits, inc=c if count else None, incn=1)
        return c.n

    def barrier(self):
        snap = tuple((c, c.n) for c in self.ctrs if c.n > 0)
        for e in ENGS:
            self._bar[e] = snap

    def emit(self, nc, block, es):
        for c in self.ctrs:
            c.h = es.enter_context(nc.semaphore(c.name))
        for e in ENGS:
            items = self.items[e]

            def make(lst):
                def run(eng):
                    for fn, ws, inc, incn in lst:
                        for (c, v) in ws:
                            eng.wait_ge(c.h, v)
                        inst = fn(eng) if fn is not None else None
                        if inc is not None:
                            tgt = inst if inst is not None else eng.nop()
                            tgt.then_inc(inc.h, incn)
                return run

            getattr(block, e)(make(items))


class DStream:

    def __init__(self, kb, name, nbuf):
        self.kb = kb
        self.sems = [kb.ctr(f"{name}{i}") for i in range(nbuf)]

    def load(self, eng, par, fn, waits=()):
        s = self.sems[par]
        self.kb.op(eng, fn, waits, inc=s, incn=16)
        return (s, s.n)

    def totals(self):
        return tuple((s, s.n) for s in self.sems if s.n > 0)


class Bank:
    __slots__ = ("ap", "free")

    def __init__(self, ap):
        self.ap = ap
        self.free = ()


def build_program(stage=2):
    nc = bass.Bass("TRN2", target_bir_lowering=False)

    x_in = nc.dram_tensor("x", [SEQ, T, D], F32, kind="ExternalInput").ap()
    wqdT = nc.dram_tensor("wqdT", [128, NK, RQ], F32, kind="ExternalInput").ap()
    wkvdT = nc.dram_tensor("wkvdT", [128, NK, RK], F32, kind="ExternalInput").ap()
    woT = nc.dram_tensor("woT", [128, NK, D], F32, kind="ExternalInput").ap()
    wqu = nc.dram_tensor("wqu", [D, RQ], F32, kind="ExternalInput").ap()
    wku = nc.dram_tensor("wku", [D, RK], F32, kind="ExternalInput").ap()
    wvu = nc.dram_tensor("wvu", [D, RK], F32, kind="ExternalInput").ap()
    bo_p = nc.dram_tensor("bo_p", [128, NK], F32, kind="ExternalInput").ap()
    gateT_p = nc.dram_tensor("gateT_p", [128, NK, 8], F32, kind="ExternalInput").ap()
    eb_p = nc.dram_tensor("eb_p", [128, 8], F32, kind="ExternalInput").ap()
    w13p = nc.dram_tensor("w13p", [NE, 8, 128, 2, NK, 128], F32, kind="ExternalInput").ap()
    w2p = nc.dram_tensor("w2p", [4, 128, NE, 4, 8, 128], F32, kind="ExternalInput").ap()
    idf_p = nc.dram_tensor("idf", [128, 128], F32, kind="ExternalInput").ap()
    idb_p = nc.dram_tensor("idb", [128, 128], BF16, kind="ExternalInput").ap()
    maskb_p = nc.dram_tensor("maskb", [128, 128], BF16, kind="ExternalInput").ap()
    iota_p = nc.dram_tensor("iota_in", [128, 1], F32, kind="ExternalInput").ap()
    grprow_p = nc.dram_tensor("grprow", [1, 32], F32, kind="ExternalInput").ap()
    y_out = nc.dram_tensor("y", [SEQ, T, D], F32, kind="ExternalOutput").ap()

    w13s = nc.dram_tensor("w13s", [NE, 8, 128, 2, NK, 128], BF16, kind="Internal").ap()
    w2s = nc.dram_tensor("w2s", [4, 128, NE, 4, 8, 128], BF16, kind="Internal").ap()
    x1s = nc.dram_tensor("x1s", [SEQ, T, D], F32, kind="Internal").ap()
    xn2s_t = [nc.dram_tensor(f"xn2s{i}", [T, D], BF16, kind="Internal").ap()
              for i in range(SEQ)]
    rtg_t = [nc.dram_tensor(f"rtg{i}", [NTG, 2], F32, kind="Internal").ap()
             for i in range(SEQ)]

    kb = Kb()
    tc, vc, sc = kb.ecnt["tensor"], kb.ecnt["vector"], kb.ecnt["scalar"]

    es_outer = ExitStack()
    sb = lambda name, shape, dt: es_outer.enter_context(nc.sbuf_tensor(name, shape, dt))
    ps = lambda name, shape, dt: es_outer.enter_context(nc.psum_tensor(name, shape, dt))

    idf = sb("idf_sb", [128, 128], F32)
    idb = sb("idb_sb", [128, 128], BF16)
    maskb = sb("maskb_sb", [128, 128], BF16)
    bo_sb = sb("bo_sb", [128, NK], F32)
    eb_sb = sb("eb_sb", [128, 8], F32)
    gateT_sb = sb("gateT_sb", [128, NK, 8], BF16)
    ones_sb = sb("ones_sb", [1, 128], BF16)
    onesf_sb = sb("onesf_sb", [1, 128], F32)
    eps_sb = sb("eps_sb", [128, 1], F32)
    iota_sb = sb("iota_sb", [128, 1], F32)
    iota9_sb = sb("iota9_sb", [128, 1], F32)
    grprow_sb = sb("grprow_sb", [1, 32], F32)

    bkA = Bank(ps("psA", [128, 512], F32).ap())
    bkB = Bank(ps("psB", [128, 512], F32).ap())
    bkC = Bank(ps("psC", [128, 512], F32).ap())
    bkD = Bank(ps("psD", [128, 512], F32).ap())
    bkE = Bank(ps("psE", [128, 512], F32).ap())
    bkF = Bank(ps("psF", [128, 512], F32).ap())
    bkT0 = Bank(ps("psT0", [128, 1024], BF16).ap())
    bkT1 = Bank(ps("psT1", [128, 1024], BF16).ap())

    ld1 = DStream(kb, "ld1", 1)
    for dst, src in ((idf, idf_p), (idb, idb_p), (maskb, maskb_p),
                     (bo_sb, bo_p), (eb_sb, eb_p), (iota_sb, iota_p),
                     (grprow_sb, grprow_p)):
        ld1.load("sync", 0, lambda e, d=dst, s=src: e.dma_start(out=d[:], in_=s))
    ld1.load("gpsimd", 0, lambda e: e.dma_start(out=gateT_sb[:], in_=gateT_p))
    bregs = {}

    def _setup_regs(e):
        bregs["tok"] = e.alloc_register("bnd_tok")
        bregs["tab"] = e.alloc_register("bnd_tab")
        bregs["w13"] = e.alloc_register("bnd_w13")
        bregs["w2"] = e.alloc_register("bnd_w2")
        e.reg_alu(bregs["tok"], T - 1, 0, ALU.add)
        e.reg_alu(bregs["w13"], NE * 8 * 128 - 1, 0, ALU.add)
        e.reg_alu(bregs["w2"], 4 * 128 * NE - 1, 0, ALU.add)
        return e.reg_alu(bregs["tab"], NTG - 1, 0, ALU.add)

    kb.op("gpsimd", _setup_regs)
    kb.cop("vector", lambda e: e.memset(ones_sb[:], 1.0))
    kb.cop("vector", lambda e: e.memset(onesf_sb[:], 1.0))
    kb.cop("vector", lambda e: e.memset(eps_sb[:], EPS))
    kb.cop("vector", lambda e: e.tensor_scalar(
        iota9_sb[:], iota_sb[:], 9.0, None, op0=ALU.mult), list(ld1.totals()))
    kb.barrier()

    es_prep = ExitStack()
    prep_buf = es_prep.enter_context(nc.sbuf_tensor("prep_buf", [128, 2, 4608], BF16))
    prep_ld = DStream(kb, "pld", 2)
    prep_st = DStream(kb, "pst", 2)
    w13p_f = w13p.rearrange("e h p u k f -> (e h) p (u k f)")
    w13s_f = w13s.rearrange("e h p u k f -> (e h) p (u k f)")
    w2p_f = w2p.rearrange("g p e d h f -> (g p) (e d h f)")
    w2s_f = w2s.rearrange("g p e d h f -> (g p) (e d h f)")
    prep_jobs = [(w13p_f[i], w13s_f[i], 4096) for i in range(72)]
    for i in range(4):
        for hf in range(8):
            prep_jobs.append((w2p_f[128 * i:128 * (i + 1), 4608 * hf:4608 * (hf + 1)],
                              w2s_f[128 * i:128 * (i + 1), 4608 * hf:4608 * (hf + 1)],
                              4608))
    prep_state = {"i": 0}

    def pump(n=1):
        for _ in range(n):
            i = prep_state["i"]
            if i >= len(prep_jobs):
                return
            prep_state["i"] = i + 1
            par = i % 2
            src, dst, width = prep_jobs[i]
            waits = []
            if i >= 2:
                waits.append((prep_st.sems[par], prep_st.sems[par].n))
            ltok = prep_ld.load("gpsimd", par,
                                lambda e, s=src, p=par, w=width: e.dma_start(
                                    out=prep_buf[:, p, 0:w], in_=s), waits)
            prep_st.load("sync", par,
                         lambda e, d=dst, p=par, w=width: e.dma_start(
                             out=d, in_=prep_buf[:, p, 0:w]), [ltok])

    def phase_A(s, QlatT, CkvT):
        esA = ExitStack()
        sbA = lambda nm, shape, dt: esA.enter_context(nc.sbuf_tensor(f"{nm}_a{s}", shape, dt))
        xtile = sbA("xtile", [128, 2, D], F32)
        xn_bf = sbA("xn_bf", [128, 2, D], BF16)
        xnT = sbA("xnT", [128, NK, 512], BF16)
        stats = sbA("astat", [128, 2, 8], F32)
        wqd_sb = sbA("wqd_sb", [128, NK, RQ], BF16)
        wkvd_sb = sbA("wkvd_sb", [128, NK, RK], BF16)

        xA = DStream(kb, f"xA{s}", 2)
        wA = DStream(kb, f"wA{s}", 1)
        wA.load("gpsimd", 0, lambda e: e.dma_start(out=wqd_sb[:], in_=wqdT))
        tok_w = wA.load("gpsimd", 0, lambda e: e.dma_start(out=wkvd_sb[:], in_=wkvdT))
        xt_free = [(), ()]
        qkv_banks = [bkA, bkB]
        bkCb = Bank(bkC.ap.bitcast(BF16))
        bkCb.free = bkC.free
        bkDb = Bank(bkD.ap.bitcast(BF16))
        bkDb.free = bkD.free
        tp_banks = [bkT0, bkT1, bkCb, bkDb]
        rot = [0, 0]

        for cg in range(4):
            for t in range(4):
                i = 4 * cg + t
                par = i % 2
                tok = xA.load("sync", par, lambda e, p=par, r=512 * cg + 128 * t:
                              e.dma_start(out=xtile[:, p], in_=x_in[s, r:r + 128, :]),
                              xt_free[par])
                _ln(kb, xtile[:, par], xn_bf[:, par], stats[:, par], xn_bf[:, par], eps_sb[:, 0:1], tok,
                    [(tc, tc.n)])
                xt_free[par] = [(vc, vc.n)]
                xn_done = vc.n
                for k in range(NK):
                    bk = tp_banks[rot[0] % 4]
                    rot[0] += 1
                    cap = kb.cop("tensor", lambda e, b=bk, p=par, kk=k: e.transpose(
                        b.ap[:, 0:128], xn_bf[:, p, 128 * kk:128 * (kk + 1)], idb[:]),
                        [(vc, xn_done)] + list(bk.free))
                    kb.cop("vector", lambda e, b=bk, kk=k, tt=t: e.tensor_copy(
                        xnT[:, kk, 128 * tt:128 * (tt + 1)], b.ap[:, 0:128]), [(tc, cap)])
                    bk.free = ((vc, vc.n),)
            xnT_ready = vc.n
            pump(4)
            for rt in range(12):
                q_side = rt < 8
                wsel = wqd_sb if q_side else wkvd_sb
                col = rt if q_side else rt - 8
                bk = qkv_banks[rot[1] % 2]
                rot[1] += 1
                for k in range(NK):
                    waits = ([(vc, xnT_ready), tok_w] + list(bk.free)) if k == 0 else []
                    kb.cop("tensor", lambda e, b=bk, w=wsel, c=col, kk=k:
                           e.matmul(b.ap[:], w[:, kk, 128 * c:128 * (c + 1)],
                                    xnT[:, kk, :], start=(kk == 0), stop=(kk == NK - 1)),
                           waits, count=(k == NK - 1))
                cap = tc.n
                dst = QlatT if q_side else CkvT
                kb.cop("vector", lambda e, b=bk, d_=dst, c=col, g=cg: e.tensor_copy(
                    d_[:, c, 512 * g:512 * (g + 1)], b.ap[:]), [(tc, cap)])
                bk.free = ((vc, vc.n),)
        bkC.free = bkCb.free
        bkD.free = bkDb.free
        esA.close()

    def phase_B(s, QlatT, CkvT, ctxT):
        esB = ExitStack()
        sbB = lambda nm, shape, dt: esB.enter_context(nc.sbuf_tensor(f"{nm}_b{s}", shape, dt))
        wqh = sbB("wqh", [128, 2, RQ], BF16)
        wkh = sbB("wkh", [128, 2, RK], BF16)
        wvh = sbB("wvh", [128, 2, RK], BF16)
        uvT = sbB("uvT", [128, 4, 128], BF16)
        mm_sb = sbB("mm_sb", [128, 8, 512], BF16)
        qhT = sbB("qhT", [128, 4, T], BF16)
        vup = sbB("vup", [128, NK, 128], BF16)
        p_sb = sbB("p_sb", [128, 2, T], BF16)
        pt4 = sbB("pt4", [128, NK, 512], BF16)
        mst = sbB("mst", [128, 8], F32)
        sstat = sbB("sstat", [128, 4], F32)

        whl = DStream(kb, f"wh{s}", 2)
        wh_free = [(), ()]
        lg_banks = [bkA, bkB, bkC, bkD, bkE]
        bkFb = Bank(bkF.ap.bitcast(BF16))
        bkFb.free = bkF.free
        tp_banks = [bkT0, bkT1, bkFb]
        rot = {"lg": 0, "tp": 0}
        prev_pv = [0]

        for h in range(16):
            par = h % 2
            whl.load("gpsimd", par, lambda e, p=par, hh=h: e.dma_start(
                out=wqh[:, p], in_=wqu[128 * hh:128 * (hh + 1), :]), wh_free[par])
            whl.load("gpsimd", par, lambda e, p=par, hh=h: e.dma_start(
                out=wkh[:, p], in_=wku[128 * hh:128 * (hh + 1), :]))
            tok_h = whl.load("gpsimd", par, lambda e, p=par, hh=h: e.dma_start(
                out=wvh[:, p], in_=wvu[128 * hh:128 * (hh + 1), :]))
            pump(6)
            for c in range(4):
                bk = tp_banks[rot["tp"] % 3]
                rot["tp"] += 1
                cap = kb.cop("tensor", lambda e, b=bk, p=par, cc=c: e.transpose(
                    b.ap[:, 0:128], wvh[:, p, 128 * cc:128 * (cc + 1)], idb[:]),
                    [tok_h] + list(bk.free))
                kb.cop("vector", lambda e, b=bk, cc=c: e.tensor_copy(
                    uvT[:, cc, :], b.ap[:, 0:128]), [(tc, cap)])
                bk.free = ((vc, vc.n),)
            for j in range(8):
                bk = lg_banks[rot["lg"] % 5]
                rot["lg"] += 1
                cap = kb.cop("tensor", lambda e, b=bk, p=par, jj=j: e.matmul(
                    b.ap[:], wqh[:, p, 128 * jj:128 * (jj + 1)], wkh[:, p],
                    start=True, stop=True), [tok_h] + list(bk.free))
                kb.cop("vector", lambda e, b=bk, jj=j: e.tensor_scalar(
                    mm_sb[:, jj, :], b.ap[:], 1.0 / math.sqrt(128.0), None,
                    op0=ALU.mult), [(tc, cap)])
                bk.free = ((vc, vc.n),)
            mm_done = vc.n
            wh_free[par] = [(tc, tc.n)]
            for rt in range(4):
                for g4 in range(4):
                    bk = lg_banks[rot["lg"] % 5]
                    rot["lg"] += 1
                    for j in range(8):
                        waits = ([(vc, mm_done)] + list(bk.free)) if j == 0 else []
                        kb.cop("tensor", lambda e, b=bk, r=rt, g=g4, jj=j, Q=QlatT:
                               e.matmul(b.ap[:], mm_sb[:, jj, 128 * r:128 * (r + 1)],
                                        Q[:, jj, 512 * g:512 * (g + 1)],
                                        start=(jj == 0), stop=(jj == 7)),
                               waits, count=(j == 7))
                    cap = tc.n
                    kb.cop("vector", lambda e, b=bk, r=rt, g=g4: e.tensor_copy(
                        qhT[:, r, 512 * g:512 * (g + 1)], b.ap[:]), [(tc, cap)])
                    bk.free = ((vc, vc.n),)
            uv_done = vc.n
            for st in range(NK):
                bk = lg_banks[rot["lg"] % 5]
                rot["lg"] += 1
                for c in range(4):
                    waits = ([(vc, uv_done)] + list(bk.free)) if c == 0 else []
                    kb.cop("tensor", lambda e, b=bk, s_=st, cc=c, C=CkvT: e.matmul(
                        b.ap[:, 0:128], C[:, cc, 128 * s_:128 * (s_ + 1)],
                        uvT[:, cc, :], start=(cc == 0), stop=(cc == 3)),
                        waits, count=(c == 3))
                cap = tc.n
                kb.cop("vector", lambda e, b=bk, s_=st: e.tensor_copy(
                    vup[:, s_, :], b.ap[:, 0:128]), [(tc, cap)])
                bk.free = ((vc, vc.n),)
            qh_done = vc.n

            def emit_tr(tq, p_done):
                par = tq % 2
                g = tq // 4
                m = tq - 4 * g
                for st in range(tq + 1):
                    bk = tp_banks[rot["tp"] % 3]
                    rot["tp"] += 1
                    cap = kb.cop("tensor", lambda e, b=bk, s_=st, p=par: e.transpose(
                        b.ap[:, 0:128], p_sb[:, p, 128 * s_:128 * (s_ + 1)], idb[:]),
                        [(vc, p_done)] + list(bk.free))
                    kb.cop("vector", lambda e, b=bk, s_=st, mm=m: e.tensor_copy(
                        pt4[:, s_, 128 * mm:128 * (mm + 1)], b.ap[:, 0:128]),
                        [(tc, cap)])
                    bk.free = ((vc, vc.n),)
                for st in range(tq + 1, 4 * g + 4):
                    kb.cop("vector", lambda e, s_=st, mm=m: e.memset(
                        pt4[:, s_, 128 * mm:128 * (mm + 1)], 0.0),
                        [(tc, prev_pv[0])])
                return vc.n

            def emit_pv(g):
                bk = lg_banks[rot["lg"] % 5]
                rot["lg"] += 1
                nst = 4 * g + 4
                for st in range(nst):
                    waits = ([(vc, pt_done[0])] + list(bk.free)) if st == 0 else []
                    kb.cop("tensor", lambda e, b=bk, s_=st, n=nst: e.matmul(
                        b.ap[:], vup[:, s_, :], pt4[:, s_, :],
                        start=(s_ == 0), stop=(s_ == n - 1)),
                        waits, count=(st == nst - 1))
                prev_pv[0] = tc.n
                kb.cop("vector", lambda e, b=bk, hh=h, gg=g, X=ctxT: e.tensor_copy(
                    X[:, hh, 512 * gg:512 * (gg + 1)], b.ap[:]), [(tc, prev_pv[0])])
                bk.free = ((vc, vc.n),)

            pend = None
            pt_done = [0]
            for tq in range(16):
                g = tq // 4
                m = tq - 4 * g
                par = tq % 2
                slen = 128 * (tq + 1)
                ng = (slen + 511) // 512
                exp_banks = []
                for scg in range(ng):
                    w = min(512, slen - 512 * scg)
                    bk = lg_banks[rot["lg"] % 5]
                    rot["lg"] += 1
                    diag = (scg == ng - 1)
                    for c in range(4):
                        waits = ([(vc, qh_done)] + list(bk.free)) if c == 0 else []
                        kb.cop("tensor", lambda e, b=bk, cc=c, q=tq, sg=scg, ww=w,
                               C=CkvT, dg=diag:
                               e.matmul(b.ap[:, 0:ww],
                                        qhT[:, cc, 128 * q:128 * (q + 1)],
                                        C[:, cc, 512 * sg:512 * sg + ww],
                                        start=(cc == 0), stop=(cc == 3 and not dg)),
                               waits, count=(c == 3 and not diag))
                    if diag:
                        kb.cop("tensor", lambda e, b=bk, ww=w: e.matmul(
                            b.ap[:, ww - 128:ww], idb[:], maskb[:],
                            start=False, stop=True), [], count=True)
                    cap = tc.n
                    kb.cop("vector", lambda e, b=bk, sg=scg, ww=w: e.reduce_max(
                        mst[:, sg:sg + 1], b.ap[:, 0:ww], axis=AX.X), [(tc, cap)])
                    exp_banks.append((bk, w))
                kb.cop("vector", lambda e, n=ng: e.reduce_max(
                    sstat[:, 0:1], mst[:, 0:n], axis=AX.X, negate=True),
                    [(vc, vc.n), (sc, sc.n)])
                negm_done = vc.n
                if pend is not None:
                    pt_done[0] = emit_tr(*pend)
                if m == 0 and g > 0:
                    emit_pv(g - 1)
                for scg, (bk, w) in enumerate(exp_banks):
                    kb.cop("scalar", lambda e, b=bk, sg=scg, ww=w, p=par:
                           e.activation(
                               p_sb[:, p, 512 * sg:512 * sg + ww], b.ap[:, 0:ww],
                               AF.Exp, bias=sstat[:, 0:1],
                               accum_out=mst[:, 4 + sg:5 + sg]),
                           [(vc, negm_done)])
                    bk.free = ((sc, sc.n),)
                exp_done = sc.n
                kb.cop("vector", lambda e, n=ng: e.reduce_sum(
                    sstat[:, 1:2], mst[:, 4:4 + n], axis=AX.X), [(sc, exp_done)])
                kb.cop("vector", lambda e: e.reciprocal(sstat[:, 2:3], sstat[:, 1:2]),
                       [(vc, vc.n)])
                kb.cop("vector", lambda e, L=slen, p=par: e.tensor_scalar(
                    p_sb[:, p, 0:L], p_sb[:, p, 0:L], sstat[:, 2:3], None,
                    op0=ALU.mult), [(vc, vc.n)])
                pend = (tq, vc.n)
            pt_done[0] = emit_tr(*pend)
            emit_pv(3)
        bkF.free = bkFb.free
        esB.close()

    def phase_C(s, ctxT):
        esC = ExitStack()
        sbC = lambda nm, shape, dt: esC.enter_context(nc.sbuf_tensor(f"{nm}_c{s}", shape, dt))
        wo_sb = sbC("wo_sb", [128, NK, D], BF16)
        xcg = sbC("xcg", [128, 4, D], F32)
        wsb = sbC("wsbC", [128, 2, 512], F32)
        piece = sbC("pieceC", [128, 2, 128], F32)

        wC = DStream(kb, f"wC{s}", 2)
        tok_wo1 = wC.load("gpsimd", 0, lambda e: e.dma_start(
            out=wo_sb[:, :, 0:1024], in_=woT[:, :, 0:1024]))
        tok_wo2 = wC.load("gpsimd", 1, lambda e: e.dma_start(
            out=wo_sb[:, :, 1024:2048], in_=woT[:, :, 1024:2048]))
        xC = DStream(kb, f"xC{s}", 1)
        stC = DStream(kb, f"stC{s}", 2)
        acc_banks = [bkA, bkB]
        tr_banks = [bkC, bkD]
        xc_free = []
        piece_free = [(), ()]
        wsb_free = [(), ()]
        rot = {"a": 0, "t": 0, "p": 0}
        dest = x1s if stage >= 2 else y_out

        for cg in range(4):
            tokx = xC.load("sync", 0, lambda e, c=cg: e.dma_start(
                out=xcg[:],
                in_=x_in[s, 512 * c:512 * (c + 1), :].rearrange("(t p) d -> p t d", p=128)),
                list(xc_free))
            pump(2)
            for dc in range(NK):
                bk = acc_banks[rot["a"] % 2]
                rot["a"] += 1
                for k in range(NK):
                    tw = tok_wo1 if dc < 8 else tok_wo2
                    waits = ([tw] + list(bk.free)) if k == 0 else []
                    kb.cop("tensor", lambda e, b=bk, d_=dc, kk=k, c=cg, X=ctxT: e.matmul(
                        b.ap[:], wo_sb[:, kk, 128 * d_:128 * (d_ + 1)],
                        X[:, kk, 512 * c:512 * (c + 1)],
                        start=(kk == 0), stop=(kk == NK - 1)),
                        waits, count=(k == NK - 1))
                cap = tc.n
                wp = dc % 2
                kb.cop("vector", lambda e, b=bk, d_=dc, p=wp: e.tensor_scalar(
                    wsb[:, p], b.ap[:], bo_sb[:, d_:d_ + 1], None, op0=ALU.add),
                    [(tc, cap)] + list(wsb_free[wp]))
                bk.free = ((vc, vc.n),)
                wsb_done = vc.n
                for t in range(4):
                    tb = tr_banks[rot["t"] % 2]
                    rot["t"] += 1
                    capt = kb.cop("tensor", lambda e, b=tb, p=wp, tt=t: e.transpose(
                        b.ap[:, 0:128], wsb[:, p, 128 * tt:128 * (tt + 1)], idf[:]),
                        [(vc, wsb_done)] + list(tb.free))
                    pp = rot["p"] % 2
                    rot["p"] += 1
                    kb.cop("vector", lambda e, b=tb, tt=t, d_=dc, q=pp: e.tensor_add(
                        piece[:, q], b.ap[:, 0:128], xcg[:, tt, 128 * d_:128 * (d_ + 1)]),
                        [(tc, capt), tokx] + list(piece_free[pp]))
                    tb.free = ((vc, vc.n),)
                    addcap = vc.n
                    stC.load("gpsimd", pp, lambda e, q=pp, r=512 * cg + 128 * t, d_=dc:
                             e.dma_start(
                                 out=dest[s, r:r + 128, 128 * d_:128 * (d_ + 1)],
                                 in_=piece[:, q]), [(vc, addcap)])
                    piece_free[pp] = [(stC.sems[pp], stC.sems[pp].n)]
                wsb_free[wp] = [(tc, tc.n)]
            xc_free = [(vc, vc.n)]
        esC.close()
        return stC.totals()

    def phase_E(s, stC_tot, prep_tot):
        esE = ExitStack()
        sbE = lambda nm, shape, dt: esE.enter_context(nc.sbuf_tensor(f"{nm}_e{s}", shape, dt))
        xn2T = sbE("xn2T", [128, NK, 512], BF16)
        h_sb = sbE("h_sb", [128, 8, 512], BF16)
        w13buf = sbE("w13buf", [128, 3, 2 * NK * 128], BF16)
        w2buf = sbE("w2buf", [128, 2, 4096], BF16)
        scores = sbE("scores", [128, 4, 8], F32)
        gtok = sbE("gtok", [128, 8], F32)
        gtmp = sbE("gtmp", [128, 3, 8], F32)
        gm = sbE("gm", [128, 2], F32)
        xcg1 = sbE("xcg1", [128, 4, D], F32)
        xn2_bf = sbE("xn2_bf", [128, 2, D], BF16)
        stats = sbE("estat", [128, 8], F32)
        tmp1 = sbE("tmp1", [128, 2, 512], BF16)
        wsb = sbE("wsbE", [128, 2, 512], F32)
        piece = sbE("pieceE", [128, 2, 128], F32)
        crow = sbE("crow", [1, 16], F32)
        brow = sbE("brow", [1, 16], F32)
        etab = sbE("etab", [1, 64], F32)
        esT = ExitStack()
        sbT = lambda nm, shape, dt: esT.enter_context(nc.sbuf_tensor(f"{nm}_e{s}", shape, dt))
        grow = sbT("grow", [1, 8, T], BF16)
        mrow = sbT("mrow", [1, T], BF16)
        zrow = sbT("zrow", [1, T], BF16)
        prow = sbT("prow", [1, T], F32)
        fill2 = sbT("fill2", [128, NTG // 128, 2], F32)
        hugec = sbT("hugec", [128, 1], F32)
        sdata = sbT("sdata", [128, 2, 2], F32)
        scol = sbT("scol", [128, 2, 4], F32)
        sidx = sbT("sidx", [128, 2, 1], I32)
        e4b = sbT("e4b", [128, 1], F32)

        x1L = DStream(kb, f"x1L{s}", 1)
        xnS = DStream(kb, f"xnS{s}", 2)
        w13L = DStream(kb, f"w13L{s}", 2)
        w2L = DStream(kb, f"w2L{s}", 2)
        stE = DStream(kb, f"stE{s}", 2)
        tfill = DStream(kb, f"tf{s}", 1)
        tscat = DStream(kb, f"ts{s}", 2)
        rload = DStream(kb, f"rl{s}", 4)
        wg13 = DStream(kb, f"wg13{s}", 3)
        wg2 = DStream(kb, f"wg2{s}", 3)
        xgL = DStream(kb, f"xgL{s}", 4)
        stR = DStream(kb, f"stR{s}", 1)
        u1_banks = [bkA, bkC]
        u3_banks = [bkB, bkD]
        acc_banks = [bkA, bkB]
        tr_banks = [bkC, bkD]
        x1_free = []
        w13_free = [(), (), ()]
        w2_free = [(), ()]
        piece_free = [(), ()]
        wsb_free = [(), ()]
        xn_free = [(), ()]
        tscat_free = [(), ()]
        rot = {"u": 0, "a": 0, "t": 0, "p": 0, "ih": 0}
        bkFe = Bank(bkF.ap.bitcast(BF16))
        bkFe.free = bkF.free
        ln_pool = [bkT0, bkT1, bkFe]
        lnr = [0]
        w13s_rows = w13s.rearrange("e h p u k f -> (e h p) (u k f)")
        w2s_rows = w2s.rearrange("g p e d h f -> (g p e) (d h f)")
        xn2s_rows = xn2s_t[s]
        rtg_s = rtg_t[s]
        y_flat = y_out.rearrange("q t d -> (q t) d")

        e4_steps = []
        e4refs = {}

        def _e4_fill():
            bkF.free = bkFe.free
            kb.cop("vector", lambda e: e.memset(zrow[:], 0.0), [(vc, vc.n)])
            kb.cop("vector", lambda e: e.memset(fill2[:, :, 0:1], float(1 << 23)), [])
            kb.cop("vector", lambda e: e.memset(fill2[:, :, 1:2], 0.0), [])
            fill_done = vc.n
            e4refs["ftok"] = tfill.load("gpsimd", 0, lambda e: e.dma_start(
                out=rtg_s.rearrange("(a p) c -> p a c", p=128), in_=fill2[:]),
                [(vc, fill_done)])
        e4_steps.append(_e4_fill)

        def _e4_scanA(ee):
            def f():
                kb.cop("vector", lambda e, x=ee: e.tensor_scalar(
                    mrow[:], grow[0:1, x, :], 0.0, None, op0=ALU.is_gt), [(vc, vc.n)])
                kb.cop("vector", lambda e: e.tensor_tensor_scan(
                    prow[:], mrow[:], zrow[:], 0.0, op0=ALU.add, op1=ALU.add),
                    [(vc, vc.n)])
                kb.cop("vector", lambda e, x=ee: e.tensor_copy(
                    crow[0:1, x:x + 1], prow[0:1, T - 1:T]), [(vc, vc.n)])
            return f
        for _ee in range(8):
            e4_steps.append(_e4_scanA(_ee))

        def _e4_bases():
            kb.cop("vector", lambda e: e.tensor_scalar(
                crow[0:1, 8:16], crow[0:1, 0:8], float(GRP - 1), 1.0 / GRP,
                op0=ALU.add, op1=ALU.mult), [(vc, vc.n)])
            kb.cop("vector", lambda e: e.tensor_scalar(
                crow[0:1, 8:16], crow[0:1, 8:16], -0.498046875, 12582912.0,
                op0=ALU.add, op1=ALU.add), [(vc, vc.n)])
            kb.cop("vector", lambda e: e.tensor_scalar(
                crow[0:1, 8:16], crow[0:1, 8:16], -12582912.0, float(GRP),
                op0=ALU.add, op1=ALU.mult), [(vc, vc.n)])
            kb.cop("vector", lambda e: e.tensor_tensor_scan(
                brow[0:1, 1:9], crow[0:1, 8:16], zrow[0:1, 0:8], 0.0,
                op0=ALU.add, op1=ALU.add), [(vc, vc.n)])
            kb.cop("vector", lambda e: e.memset(brow[0:1, 0:1], 0.0), [(vc, vc.n)])
            kb.cop("vector", lambda e: e.memset(etab[:], 0.0), [(vc, vc.n)])
            for x in range(1, 9):
                kb.cop("vector", lambda e, x_=x: e.tensor_scalar(
                    etab[0:1, 32:32 + NT], grprow_sb[0:1, 0:NT], brow[0:1, x_:x_ + 1],
                    None, op0=ALU.is_ge), [(vc, vc.n)])
                kb.cop("vector", lambda e: e.tensor_add(
                    etab[0:1, 0:NT], etab[0:1, 0:NT], etab[0:1, 32:32 + NT]),
                    [(vc, vc.n)])
            e4refs["etab_done"] = vc.n
        e4_steps.append(_e4_bases)

        def _e4_scanB(ee):
            def f():
                kb.cop("vector", lambda e, x=ee: e.tensor_scalar(
                    mrow[:], grow[0:1, x, :], 0.0, None, op0=ALU.is_gt),
                    [(vc, vc.n)])
                kb.cop("vector", lambda e: e.tensor_tensor_scan(
                    prow[:], mrow[:], zrow[:], 0.0, op0=ALU.add, op1=ALU.add),
                    [(vc, vc.n)])
                kb.cop("vector", lambda e, x=ee: e.tensor_scalar(
                    brow[0:1, 9:10], brow[0:1, x:x + 1], -1.0, None, op0=ALU.add),
                    [(vc, vc.n)])
                capb = kb.cop("tensor", lambda e: e.matmul(
                    bkF.ap[:, 0:1], onesf_sb[:], brow[0:1, 9:10],
                    start=True, stop=True), [(vc, vc.n)] + list(bkF.free))
                kb.cop("vector", lambda e: e.tensor_copy(e4b[:], bkF.ap[:, 0:1]),
                       [(tc, capb)])
                bkF.free = ((vc, vc.n),)
            return f

        def _e4_tile(ee, t):
            def f():
                par = t % 2
                cap1 = kb.cop("tensor", lambda e, tt=t: e.transpose(
                    bkF.ap[:, 0:1], prow[0:1, 128 * tt:128 * (tt + 1)],
                    idf[0:1, 0:1]), [(vc, vc.n)] + list(bkF.free))
                kb.cop("vector", lambda e, p=par: e.tensor_copy(
                    scol[:, p, 0:1], bkF.ap[:, 0:1]),
                    [(tc, cap1)] + list(tscat_free[par]))
                bkF.free = ((vc, vc.n),)
                cap2 = kb.cop("tensor", lambda e, x=ee, tt=t: e.transpose(
                    bkT0.ap[:, 0:1], grow[0:1, x, 128 * tt:128 * (tt + 1)],
                    idb[0:1, 0:1]), list(bkT0.free))
                kb.cop("vector", lambda e, p=par: e.tensor_copy(
                    scol[:, p, 1:2], bkT0.ap[:, 0:1]), [(tc, cap2)])
                bkT0.free = ((vc, vc.n),)
                kb.cop("vector", lambda e, p=par, tt=t: e.tensor_scalar(
                    sdata[:, p, 0:1], iota_sb[:], float(128 * tt), None, op0=ALU.add),
                    [(vc, vc.n)])
                kb.cop("vector", lambda e, p=par: e.tensor_copy(
                    sdata[:, p, 1:2], scol[:, p, 1:2]), [(vc, vc.n)])
                kb.cop("vector", lambda e, p=par: e.tensor_add(
                    scol[:, p, 2:3], scol[:, p, 0:1], e4b[:]), [(vc, vc.n)])
                kb.cop("vector", lambda e, p=par: e.tensor_scalar(
                    scol[:, p, 3:4], scol[:, p, 1:2], 0.0, None, op0=ALU.is_gt),
                    [(vc, vc.n)])
                kb.cop("vector", lambda e, p=par: e.tensor_scalar(
                    scol[:, p, 2:3], scol[:, p, 2:3], -float(1 << 23), None,
                    op0=ALU.add), [(vc, vc.n)])
                kb.cop("vector", lambda e, p=par: e.scalar_tensor_tensor(
                    scol[:, p, 2:3], scol[:, p, 2:3], 1.0, scol[:, p, 3:4],
                    op0=ALU.mult, op1=ALU.mult), [(vc, vc.n)])
                kb.cop("vector", lambda e, p=par: e.tensor_scalar(
                    scol[:, p, 2:3], scol[:, p, 2:3], float(1 << 23), None,
                    op0=ALU.add), [(vc, vc.n)])
                kb.cop("vector", lambda e, p=par: e.tensor_copy(
                    sidx[:, p], scol[:, p, 2:3]), [(vc, vc.n)])
                dcap = vc.n
                stok = tscat.load("gpsimd", par, lambda e, p=par: e.indirect_dma_start(
                    out=rtg_s, out_offset=IndirectOffsetOnAxis(ap=sidx[:, p], axis=0),
                    in_=sdata[:, p], in_offset=None,
                    bounds_check=bregs["tab"], oob_is_err=False),
                    [(vc, dcap), e4refs["ftok"]])
                tscat_free[par] = [stok]
            return f
        for _ee in range(8):
            e4_steps.append(_e4_scanB(_ee))
            for _t in range(NK):
                e4_steps.append(_e4_tile(_ee, _t))
        e4_state = {"i": 0}

        def e4pump(n):
            for _ in range(n):
                i = e4_state["i"]
                if i >= len(e4_steps):
                    return
                e4_state["i"] = i + 1
                e4_steps[i]()

        for grp in range(4):
            tokx = x1L.load("sync", 0, lambda e, g=grp: e.dma_start(
                out=xcg1[:],
                in_=x1s[s, 512 * g:512 * (g + 1), :].rearrange("(t p) d -> p t d", p=128)),
                list(stC_tot) + list(x1_free))
            tile_caps = []
            for t in range(4):
                par = t % 2
                xnd = _ln(kb, xcg1[:, t, :], xn2_bf[:, par], stats, xn2_bf[:, par],
                          eps_sb[:, 0:1], tokx, [(tc, tc.n)] + list(xn_free[par]))
                xnS.load("gpsimd", par, lambda e, p=par, r=512 * grp + 128 * t:
                         e.dma_start(out=xn2s_rows[r:r + 128, :], in_=xn2_bf[:, p]),
                         [(vc, xnd)])
                xn_free[par] = [(xnS.sems[par], xnS.sems[par].n)]
                for k in range(NK):
                    bk = ln_pool[lnr[0] % 3]
                    lnr[0] += 1
                    cap = kb.cop("tensor", lambda e, b=bk, kk=k, p=par: e.transpose(
                        b.ap[:, 0:128], xn2_bf[:, p, 128 * kk:128 * (kk + 1)], idb[:]),
                        [(vc, xnd)] + list(bk.free))
                    kb.cop("vector", lambda e, b=bk, kk=k, tt=t: e.tensor_copy(
                        xn2T[:, kk, 128 * tt:128 * (tt + 1)], b.ap[:, 0:128]), [(tc, cap)])
                    bk.free = ((vc, vc.n),)
                tile_caps.append(vc.n)
            for t in range(4):
                for k in range(NK):
                    waits = ([(vc, tile_caps[t])] + list(bkE.free)) if k == 0 else []
                    kb.cop("tensor", lambda e, kk=k, tt=t: e.matmul(
                        bkE.ap[:, 0:8], xn2T[:, kk, 128 * tt:128 * (tt + 1)],
                        gateT_sb[:, kk, :], start=(kk == 0), stop=(kk == NK - 1)),
                        waits, count=(k == NK - 1))
                cap = tc.n
                kb.cop("scalar", lambda e, tt=t: e.activation(
                    scores[:, tt, :], bkE.ap[:, 0:8], AF.Sigmoid), [(tc, cap), (vc, vc.n)])
                bkE.free = ((sc, sc.n),)
                sgd = sc.n
                kb.cop("vector", lambda e, tt=t: e.tensor_add(
                    gtmp[:, 0, :], scores[:, tt, :], eb_sb[:]), [(sc, sgd), (vc, vc.n)])
                kb.cop("vector", lambda e: e.reduce_max(gm[:, 0:1], gtmp[:, 0, :],
                                                        axis=AX.X), [(vc, vc.n)])
                kb.cop("vector", lambda e: e.tensor_scalar(
                    gtmp[:, 1, :], gtmp[:, 0, :], gm[:, 0:1], None, op0=ALU.is_equal),
                    [(vc, vc.n)])
                kb.cop("vector", lambda e: e.scalar_tensor_tensor(
                    gtmp[:, 2, :], gtmp[:, 1, :], NEGM, gtmp[:, 0, :],
                    op0=ALU.mult, op1=ALU.add), [(vc, vc.n)])
                kb.cop("vector", lambda e: e.reduce_max(gm[:, 1:2], gtmp[:, 2, :],
                                                        axis=AX.X), [(vc, vc.n)])
                kb.cop("vector", lambda e: e.tensor_scalar(
                    gtmp[:, 2, :], gtmp[:, 2, :], gm[:, 1:2], None, op0=ALU.is_equal),
                    [(vc, vc.n)])
                kb.cop("vector", lambda e: e.tensor_add(
                    gtmp[:, 1, :], gtmp[:, 1, :], gtmp[:, 2, :]), [(vc, vc.n)])
                kb.cop("vector", lambda e, tt=t: e.tensor_mul(
                    gtok[:], scores[:, tt, :], gtmp[:, 1, :]), [(vc, vc.n), (tc, tc.n)])
                gdone = vc.n
                for ee in range(8):
                    capg = kb.cop("tensor", lambda e, x=ee: e.transpose(
                        bkF.ap[0:1, 0:128], gtok[:, x:x + 1], idf[:]),
                        [(vc, gdone)] + list(bkF.free))
                    kb.cop("vector", lambda e, x=ee, c0=512 * grp + 128 * t: e.tensor_copy(
                        grow[0:1, x, c0:c0 + 128], bkF.ap[0:1, 0:128]), [(tc, capg)])
                    bkF.free = ((vc, vc.n),)
            for hc in range(8):
                ih = rot["ih"]
                rot["ih"] += 1
                wp = ih % 2
                tokw = w13L.load("sync", wp, lambda e, p=wp, hh=hc: e.dma_start(
                    out=w13buf[:, p], in_=w13s[8, hh]),
                    list(prep_tot) + list(w13_free[wp]))
                b1 = u1_banks[rot["u"] % 2]
                b3 = u3_banks[rot["u"] % 2]
                rot["u"] += 1
                for k in range(NK):
                    waits = ([tokw, (vc, tile_caps[3])] + list(b1.free)) if k == 0 else []
                    kb.cop("tensor", lambda e, b=b1, p=wp, kk=k: e.matmul(
                        b.ap[:], w13buf[:, p, 128 * kk:128 * (kk + 1)], xn2T[:, kk, :],
                        start=(kk == 0), stop=(kk == NK - 1)),
                        waits, count=(k == NK - 1))
                cap1 = tc.n
                for k in range(NK):
                    waits = list(b3.free) if k == 0 else []
                    kb.cop("tensor", lambda e, b=b3, p=wp, kk=k: e.matmul(
                        b.ap[:], w13buf[:, p, 2048 + 128 * kk:2048 + 128 * (kk + 1)], xn2T[:, kk, :],
                        start=(kk == 0), stop=(kk == NK - 1)),
                        waits, count=(k == NK - 1))
                cap3 = tc.n
                w13_free[wp] = [(tc, cap3)]
                hp = hc % 2
                kb.cop("scalar", lambda e, b=b1, p=hp: e.activation(
                    tmp1[:, p], b.ap[:], AF.Silu), [(tc, cap1), (vc, vc.n), (sc, sc.n)])
                b1.free = ((sc, sc.n),)
                sil = sc.n
                kb.cop("vector", lambda e, b=b3, hh=hc, p=hp: e.tensor_mul(
                    h_sb[:, hh, :], b.ap[:], tmp1[:, p]),
                    [(tc, cap3), (sc, sil), (vc, vc.n)])
                b3.free = ((vc, vc.n),)
                if grp == 3:
                    e4pump(8)
            h_done = vc.n
            pendw = None

            def emit_epi(dc, wsb_done):
                sp = dc % 2
                for t in range(4):
                    tb = tr_banks[rot["t"] % 2]
                    rot["t"] += 1
                    capt = kb.cop("tensor", lambda e, b=tb, p=sp, tt=t: e.transpose(
                        b.ap[:, 0:128], wsb[:, p, 128 * tt:128 * (tt + 1)], idf[:]),
                        [(vc, wsb_done)] + list(tb.free))
                    pp = rot["p"] % 2
                    rot["p"] += 1
                    kb.cop("vector", lambda e, b=tb, tt=t, d_=dc, q=pp: e.tensor_add(
                        piece[:, q], b.ap[:, 0:128], xcg1[:, tt, 128 * d_:128 * (d_ + 1)]),
                        [(tc, capt), tokx] + list(piece_free[pp]))
                    tb.free = ((vc, vc.n),)
                    addcap = vc.n
                    stE.load("gpsimd", pp, lambda e, q=pp, r=512 * grp + 128 * t, d_=dc:
                             e.dma_start(
                                 out=y_out[s, r:r + 128, 128 * d_:128 * (d_ + 1)],
                                 in_=piece[:, q]), [(vc, addcap)])
                    piece_free[pp] = [(stE.sems[pp], stE.sems[pp].n)]
                wsb_free[sp] = [(tc, tc.n)]

            for dc in range(NK):
                wp = dc % 2
                tokw2 = w2L.load("sync", wp, lambda e, p=wp, d_=dc: e.dma_start(
                    out=w2buf[:, p, 0:1024], in_=w2s[d_ // 4, :, 8, d_ % 4]),
                    list(prep_tot) + list(w2_free[wp]))
                bk = acc_banks[rot["a"] % 2]
                rot["a"] += 1
                for hc in range(8):
                    waits = ([(vc, h_done), tokw2] + list(bk.free)) if hc == 0 else []
                    kb.cop("tensor", lambda e, b=bk, p=wp, hh=hc: e.matmul(
                        b.ap[:], w2buf[:, p, 128 * hh:128 * (hh + 1)], h_sb[:, hh, :],
                        start=(hh == 0), stop=(hh == 7)),
                        waits, count=(hc == 7))
                cap = tc.n
                w2_free[wp] = [(tc, cap)]
                sp = dc % 2
                kb.cop("vector", lambda e, b=bk, p=sp: e.tensor_copy(
                    wsb[:, p], b.ap[:]), [(tc, cap)] + list(wsb_free[sp]))
                bk.free = ((vc, vc.n),)
                if pendw is not None:
                    emit_epi(*pendw)
                pendw = (dc, vc.n)
                if grp == 3:
                    e4pump(12)
            emit_epi(*pendw)
            x1_free = [(vc, vc.n)]
        scores_done = vc.n
        e4pump(len(e4_steps))
        xn_store_tot = xnS.totals()
        shared_store_tot = stE.totals()
        tscat_tot = tscat.totals()
        esT.close()
        xg = sbE("xg", [128, 2, 2, D], BF16)
        xgT = sbE("xgT", [128, NK, 256], BF16)
        yr = sbE("yr", [128, 2, 2, D], F32)
        rtgt = sbE("rtgt", [128, 2, 2, 2], F32)
        tidx = sbE("tidx", [128, 2, 2, 1], I32)
        gvc = sbE("gvc", [128, 2, 2, 1], F32)
        ebtab = sbE("ebtab", [128, 32], F32)
        etab_bf = sbE("etab_bf", [1, 32], BF16)
        w13tf = sbE("w13tf", [128, 8, NT], F32)
        w2tf = sbE("w2tf", [128, 4, NT], F32)
        w13ti = sbE("w13ti", [128, 8, NT, 1], I32)
        w2ti = sbE("w2ti", [128, 4, NT, 1], I32)

        kb.cop("scalar", lambda e: e.copy(etab_bf[:], etab[0:1, 0:32]),
               [(vc, e4refs["etab_done"])])
        capb = kb.cop("tensor", lambda e: e.matmul(
            bkF.ap[:, 0:32], ones_sb[:], etab_bf[:], start=True, stop=True),
            [(sc, sc.n)] + list(bkF.free))
        kb.cop("vector", lambda e: e.tensor_copy(ebtab[:], bkF.ap[:, 0:32]),
               [(tc, capb), (vc, vc.n)])
        bkF.free = ((vc, vc.n),)
        kb.cop("vector", lambda e: e.tensor_scalar(
            w13tf[:, 0, :], ebtab[:, 0:NT], 1024.0, iota_sb[:, 0:1],
            op0=ALU.mult, op1=ALU.add), [(vc, vc.n)])
        for hh in range(1, 8):
            kb.cop("vector", lambda e, h_=hh: e.tensor_scalar(
                w13tf[:, h_, :], w13tf[:, 0, :], float(128 * h_), None,
                op0=ALU.add), [(vc, vc.n)])
        kb.cop("vector", lambda e: e.tensor_scalar(
            w2tf[:, 0, :], ebtab[:, 0:NT], 1.0, iota9_sb[:, 0:1],
            op0=ALU.mult, op1=ALU.add), [(vc, vc.n)])
        for d_ in range(1, 4):
            kb.cop("vector", lambda e, dd=d_: e.tensor_scalar(
                w2tf[:, dd, :], w2tf[:, 0, :], float(1152 * dd), None,
                op0=ALU.add), [(vc, vc.n)])
        kb.cop("vector", lambda e: e.tensor_copy(
            w13ti.rearrange("p h t o -> p (h t o)"),
            w13tf.rearrange("p h t -> p (h t)")), [(vc, vc.n)])
        kb.cop("vector", lambda e: e.tensor_copy(
            w2ti.rearrange("p d t o -> p (d t o)"),
            w2tf.rearrange("p d t -> p (d t)")), [(vc, vc.n)])
        tabs_done = vc.n

        bkEb = Bank(bkE.ap.bitcast(BF16))
        bkEb.free = bkE.free
        tpool = [bkT0, bkT1, bkEb]
        trot = [0]
        trpool = [bkC, bkD, bkF]
        kb.cop("vector", lambda e: e.memset(xg[:], 0.0), [(vc, vc.n)])
        xg_clear = vc.n
        prev_scat = []
        xg_free = {}
        rl_free = [(), ()]
        tok_free = [(), ()]
        yr_free = [(), ()]
        for gi in range(NT):
            gp = gi % 2
            rtoks = []
            for t in range(2):
                rtoks.append(rload.load("sync", 2 * gp + t,
                    lambda e, i=gi, tt=t, g=gp: e.dma_start(
                        out=rtgt[:, g, tt],
                        in_=rtg_s[GRP * i + 128 * tt:GRP * i + 128 * (tt + 1), :]),
                    (list(tscat_tot) + list(rl_free[gp])) if t == 0 else []))
            kb.cop("vector", lambda e, g=gp: e.tensor_copy(
                tidx[:, g, 0, :], rtgt[:, g, 0, 0:1]),
                [rtoks[0], rtoks[1], (vc, vc.n)] + list(tok_free[gp]))
            kb.cop("vector", lambda e, g=gp: e.tensor_copy(
                tidx[:, g, 1, :], rtgt[:, g, 1, 0:1]), [(vc, vc.n)])
            kb.cop("vector", lambda e, g=gp: e.tensor_copy(
                gvc[:, g, 0, :], rtgt[:, g, 0, 1:2]), [(vc, vc.n)])
            kb.cop("vector", lambda e, g=gp: e.tensor_copy(
                gvc[:, g, 1, :], rtgt[:, g, 1, 1:2]), [(vc, vc.n)])
            rl_free[gp] = [(vc, vc.n)]
            tcap = vc.n
            for t in range(2):
                gtokx = xgL.load("gpsimd", 2 * gp + t, lambda e, g=gp, tt=t: e.indirect_dma_start(
                    out=xg[:, g, tt], out_offset=None,
                    in_=xn2s_rows,
                    in_offset=IndirectOffsetOnAxis(ap=tidx[:, g, tt], axis=0),
                    bounds_check=bregs["tok"], oob_is_err=False),
                    ([(vc, tcap)] + list(xn_store_tot) + list(xg_free.get((gp, t), [(vc, xg_clear)]))))
                kb.cop("vector", lambda e, g=gp, tt=t: e.tensor_scalar(
                    xg[:, g, tt], xg[:, g, tt], gvc[:, g, tt, 0:1], None, op0=ALU.mult),
                    [gtokx, (vc, vc.n)])
                xsc = vc.n
                xg_free[(gp, t)] = [(vc, xsc)]
                for k in range(NK):
                    bk = tpool[trot[0] % 3]
                    trot[0] += 1
                    capk = kb.cop("tensor", lambda e, b=bk, kk=k, g=gp, tt=t: e.transpose(
                        b.ap[:, 0:128], xg[:, g, tt, 128 * kk:128 * (kk + 1)], idb[:]),
                        [(vc, xsc)] + list(bk.free))
                    kb.cop("vector", lambda e, b=bk, kk=k, tt=t: e.tensor_copy(
                        xgT[:, kk, 128 * tt:128 * (tt + 1)], b.ap[:, 0:128]),
                        [(tc, capk)])
                    bk.free = ((vc, vc.n),)
            xgT_done = vc.n
            for hc in range(8):
                ih = rot["ih"]
                rot["ih"] += 1
                wp = ih % 3
                tokw = wg13.load("gpsimd", wp, lambda e, p=wp, i=gi, hh=hc:
                                 e.indirect_dma_start(
                    out=w13buf[:, p], out_offset=None,
                    in_=w13s_rows,
                    in_offset=IndirectOffsetOnAxis(ap=w13ti[:, hh, i], axis=0),
                    bounds_check=bregs["w13"], oob_is_err=False),
                    [(vc, tabs_done)] + list(prep_tot) + list(w13_free[wp]))
                b1 = u1_banks[rot["u"] % 2]
                b3 = u3_banks[rot["u"] % 2]
                rot["u"] += 1
                for k in range(NK):
                    waits = ([tokw, (vc, xgT_done)] + list(b1.free)) if k == 0 else []
                    kb.cop("tensor", lambda e, b=b1, p=wp, kk=k: e.matmul(
                        b.ap[:, 0:GRP], w13buf[:, p, 128 * kk:128 * (kk + 1)],
                        xgT[:, kk, :],
                        start=(kk == 0), stop=(kk == NK - 1)),
                        waits, count=(k == NK - 1))
                cap1 = tc.n
                for k in range(NK):
                    waits = list(b3.free) if k == 0 else []
                    kb.cop("tensor", lambda e, b=b3, p=wp, kk=k: e.matmul(
                        b.ap[:, 0:GRP], w13buf[:, p, 2048 + 128 * kk:2048 + 128 * (kk + 1)],
                        xgT[:, kk, :],
                        start=(kk == 0), stop=(kk == NK - 1)),
                        waits, count=(k == NK - 1))
                cap3 = tc.n
                w13_free[wp] = [(tc, cap3)]
                hp = hc % 2
                kb.cop("scalar", lambda e, b=b1, p=hp: e.activation(
                    tmp1[:, p, 0:GRP], b.ap[:, 0:GRP], AF.Silu),
                    [(tc, cap1), (vc, vc.n), (sc, sc.n)])
                b1.free = ((sc, sc.n),)
                sil = sc.n
                kb.cop("vector", lambda e, b=b3, hh=hc, p=hp: e.tensor_mul(
                    h_sb[:, hh, 0:GRP], b.ap[:, 0:GRP], tmp1[:, p, 0:GRP]),
                    [(tc, cap3), (sc, sil), (vc, vc.n)])
                b3.free = ((vc, vc.n),)
            h_done = vc.n
            for dc in range(NK):
                wp = (dc // 4) % 2
                if dc % 4 == 0:
                    tokw2 = wg2.load("gpsimd", wp, lambda e, p=wp, i=gi, g_=dc // 4:
                                     e.indirect_dma_start(
                        out=w2buf[:, p], out_offset=None,
                        in_=w2s_rows,
                        in_offset=IndirectOffsetOnAxis(ap=w2ti[:, g_, i], axis=0),
                        bounds_check=bregs["w2"], oob_is_err=False),
                        [(vc, tabs_done)] + list(prep_tot) + list(w2_free[wp]))
                bk = acc_banks[rot["a"] % 2]
                rot["a"] += 1
                for hc in range(8):
                    waits = ([(vc, h_done), tokw2] + list(bk.free)) if hc == 0 else []
                    kb.cop("tensor", lambda e, b=bk, p=wp, hh=hc, d4=dc % 4: e.matmul(
                        b.ap[:, 0:GRP],
                        w2buf[:, p, 1024 * d4 + 128 * hh:1024 * d4 + 128 * (hh + 1)],
                        h_sb[:, hh, 0:GRP],
                        start=(hh == 0), stop=(hh == 7)),
                        waits, count=(hc == 7))
                cap = tc.n
                if dc % 4 == 3:
                    w2_free[wp] = [(tc, cap)]
                sp = dc % 2
                kb.cop("vector", lambda e, b=bk, p=sp: e.tensor_copy(
                    wsb[:, p, 0:GRP], b.ap[:, 0:GRP]), [(tc, cap)] + list(wsb_free[sp]))
                bk.free = ((vc, vc.n),)
                wsb_done = vc.n
                for t in range(2):
                    tb = trpool[rot["t"] % 3]
                    rot["t"] += 1
                    capt = kb.cop("tensor", lambda e, b=tb, p=sp, tt=t: e.transpose(
                        b.ap[:, 0:128], wsb[:, p, 128 * tt:128 * (tt + 1)], idf[:]),
                        [(vc, wsb_done)] + list(tb.free))
                    kb.cop("vector", lambda e, b=tb, tt=t, d_=dc, g=gp: e.tensor_copy(
                        yr[:, g, tt, 128 * d_:128 * (d_ + 1)], b.ap[:, 0:128]),
                        [(tc, capt)] + (list(yr_free[gp]) if dc == 0 else []))
                    tb.free = ((vc, vc.n),)
                wsb_free[sp] = [(tc, tc.n)]
            yr_done = vc.n
            for t in range(2):
                stok = stR.load("gpsimd", 0, lambda e, tt=t, g=gp: e.indirect_dma_start(
                    out=y_flat, out_offset=IndirectOffsetOnAxis(ap=tidx[:, g, tt], axis=0),
                    in_=yr[:, g, tt], in_offset=None, element_offset=s * T * D,
                    bounds_check=bregs["tok"], oob_is_err=False, compute_op=ALU.add),
                    [(vc, yr_done)] + list(shared_store_tot) + list(prev_scat))
                prev_scat = [stok]
            yr_free[gp] = [stok]
            tok_free[gp] = [stok]
        bkE.free = bkEb.free
        esE.close()
        return list(stE.totals()) + list(stR.totals())

    out_tots = []
    prep_closed = False
    for s in range(SEQ):
        esR = ExitStack()
        QlatT = esR.enter_context(nc.sbuf_tensor(f"QlatT{s}", [128, 8, T], BF16, side="right"))
        CkvT = esR.enter_context(nc.sbuf_tensor(f"CkvT{s}", [128, 4, T], BF16, side="right"))
        phase_A(s, QlatT, CkvT)
        kb.barrier()
        esCtx = ExitStack()
        ctxT = esCtx.enter_context(nc.sbuf_tensor(f"ctxT{s}", [128, NK, T], BF16))
        phase_B(s, QlatT, CkvT, ctxT)
        kb.barrier()
        esR.close()
        stC_tot = phase_C(s, ctxT)
        esCtx.close()
        if stage >= 2:
            if not prep_closed:
                pump(len(prep_jobs))
                es_prep.close()
                prep_closed = True
            prep_tot = prep_st.totals()
            kb.barrier()
            out_tots.append(phase_E(s, stC_tot, prep_tot))
            kb.barrier()
        else:
            out_tots.append(stC_tot)
            kb.barrier()
    if not prep_closed:
        pump(len(prep_jobs))
        es_prep.close()

    final_waits = [tok for tots in out_tots for tok in tots]
    kb.op("gpsimd", None, waits=final_waits)

    es_final = ExitStack()
    es_final.enter_context(es_outer)
    block = es_final.enter_context(nc.Block())
    kb.emit(nc, block, es_final)
    es_final.close()
    return nc


def _ln(kb, src_ap, xn_out, stat, sq, eps_ap, ld_tok, extra_waits=()):
    vc, scc = kb.ecnt["vector"], kb.ecnt["scalar"]
    w0 = [ld_tok, (vc, vc.n)] + list(extra_waits)
    kb.cop("vector", lambda e, a=src_ap, o=stat: e.reduce_sum(
        o[:, 0:1], a, axis=AX.X), w0)
    kb.cop("vector", lambda e, a=src_ap, o=stat, q=sq: e.scalar_tensor_tensor(
        q[:], a, 1.0, a, op0=ALU.mult, op1=ALU.mult, accum_out=o[:, 1:2]),
        [ld_tok])
    kb.cop("vector", lambda e, o=stat: e.tensor_scalar(
        o[:, 2:3], o[:, 0:1], -1.0 / D, None, op0=ALU.mult), [(vc, vc.n)])
    kb.cop("vector", lambda e, o=stat: e.tensor_mul(
        o[:, 3:4], o[:, 2:3], o[:, 2:3]), [(vc, vc.n)])
    kb.cop("vector", lambda e, o=stat: e.scalar_tensor_tensor(
        o[:, 4:5], o[:, 1:2], 1.0 / D, o[:, 3:4], op0=ALU.mult, op1=ALU.subtract),
        [(vc, vc.n)])
    cv = vc.n
    kb.cop("scalar", lambda e, o=stat, b_=eps_ap: e.activation(
        o[:, 5:6], o[:, 4:5], AF.Sqrt, bias=b_), [(vc, cv)])
    kb.cop("vector", lambda e, o=stat: e.reciprocal(o[:, 6:7], o[:, 5:6]),
           [(scc, scc.n)])
    kb.cop("vector", lambda e, a=src_ap, o=stat, x=xn_out: e.tensor_scalar(
        x[:], a, o[:, 2:3], o[:, 6:7], op0=ALU.add, op1=ALU.mult), [(vc, vc.n)])
    return vc.n


def _pack(inputs):
    f32 = np.float32
    asc = np.ascontiguousarray
    wqd = np.asarray(inputs["L0_attn_Wq_down"], f32)
    wkvd = np.asarray(inputs["L0_attn_Wkv_down"], f32)
    wo = np.asarray(inputs["L0_attn_Wo"], f32)
    gate = np.asarray(inputs["L0_ffn_gate"], f32)
    bo = np.asarray(inputs["L0_attn_bo"], f32)
    eb = np.asarray(inputs["L0_ffn_expert_bias"], f32)
    w1 = np.asarray(inputs["L0_ffn_w1"], f32)
    w2 = np.asarray(inputs["L0_ffn_w2"], f32)
    w3 = np.asarray(inputs["L0_ffn_w3"], f32)
    sw1 = np.asarray(inputs["L0_ffn_shared_w1"], f32)
    sw2 = np.asarray(inputs["L0_ffn_shared_w2"], f32)
    sw3 = np.asarray(inputs["L0_ffn_shared_w3"], f32)

    d = {}
    d["wqdT"] = asc(wqd.reshape(RQ, NK, 128).transpose(2, 1, 0))
    d["wkvdT"] = asc(wkvd.reshape(RK, NK, 128).transpose(2, 1, 0))
    d["woT"] = asc(wo.reshape(D, NK, 128).transpose(2, 1, 0))
    d["wqu"] = asc(np.asarray(inputs["L0_attn_Wq_up"], f32))
    d["wku"] = asc(np.asarray(inputs["L0_attn_Wk_up"], f32))
    d["wvu"] = asc(np.asarray(inputs["L0_attn_Wv_up"], f32))
    d["bo_p"] = asc(bo.reshape(NK, 128).T)
    d["gateT_p"] = asc(gate.T.reshape(NK, 128, 8).transpose(1, 0, 2))
    d["eb_p"] = asc(np.broadcast_to(eb, (128, 8)))
    w1all = np.concatenate([w1, sw1[None]], axis=0)
    w3all = np.concatenate([w3, sw3[None]], axis=0)
    w2all = np.concatenate([w2, sw2[None]], axis=0)
    w1p = w1all.reshape(NE, 8, 128, NK, 128).transpose(0, 1, 4, 3, 2)
    w3p = w3all.reshape(NE, 8, 128, NK, 128).transpose(0, 1, 4, 3, 2)
    d["w13p"] = asc(np.stack([w1p, w3p], axis=3))
    d["w2p"] = asc(w2all.reshape(NE, 4, 4, 128, 8, 128).transpose(1, 5, 0, 2, 4, 3))
    d["idf"] = np.eye(128, dtype=f32)
    d["idb"] = np.eye(128, dtype=f32).astype(BF)
    jj, ii = np.meshgrid(np.arange(128), np.arange(128))
    d["maskb"] = np.where(jj > ii, np.float32(NEGM), np.float32(0.0)).astype(BF)
    d["iota_in"] = np.arange(128, dtype=f32)[:, None]
    d["grprow"] = (np.arange(32, dtype=f32) * GRP)[None, :]
    return d


_NC_CACHE = {}


def _get_nc(stage=2):
    if stage not in _NC_CACHE:
        _NC_CACHE[stage] = build_program(stage)
    return _NC_CACHE[stage]


def _run(inputs, stage=2):
    shared = _pack(inputs)
    x = np.asarray(inputs["L0_attn_x"], np.float32)
    in_maps = []
    for c in range(NCORE):
        m = dict(shared)
        m["x"] = np.ascontiguousarray(x[SEQ * c:SEQ * (c + 1)])
        in_maps.append(m)
    nc = _get_nc(stage)
    res = run_bass_kernel_spmd(nc, in_maps, core_ids=list(range(NCORE)))
    out = np.concatenate([np.asarray(r["y"]) for r in res.results], axis=0)
    return out


def kernel(L0_attn_x, L0_attn_Wq_down, L0_attn_Wkv_down, L0_attn_Wq_up,
           L0_attn_Wk_up, L0_attn_Wv_up, L0_attn_Wo, L0_attn_bo,
           L0_ffn_gate, L0_ffn_expert_bias, L0_ffn_w1, L0_ffn_w2, L0_ffn_w3,
           L0_ffn_shared_w1, L0_ffn_shared_w2, L0_ffn_shared_w3):
    inputs = dict(
        L0_attn_x=L0_attn_x, L0_attn_Wq_down=L0_attn_Wq_down,
        L0_attn_Wkv_down=L0_attn_Wkv_down, L0_attn_Wq_up=L0_attn_Wq_up,
        L0_attn_Wk_up=L0_attn_Wk_up, L0_attn_Wv_up=L0_attn_Wv_up,
        L0_attn_Wo=L0_attn_Wo, L0_attn_bo=L0_attn_bo, L0_ffn_gate=L0_ffn_gate,
        L0_ffn_expert_bias=L0_ffn_expert_bias, L0_ffn_w1=L0_ffn_w1,
        L0_ffn_w2=L0_ffn_w2, L0_ffn_w3=L0_ffn_w3,
        L0_ffn_shared_w1=L0_ffn_shared_w1, L0_ffn_shared_w2=L0_ffn_shared_w2,
        L0_ffn_shared_w3=L0_ffn_shared_w3)
    return _run(inputs, stage=2)


# version 19 (final)
# speedup vs baseline: 1.0837x; 1.0143x over previous
import math
import numpy as np
import ml_dtypes
import concourse.bass as bass
import concourse.mybir as mybir
from concourse.bass_utils import run_bass_kernel_spmd
from concourse.bass_core_rust import IndirectOffsetOnAxis
from contextlib import ExitStack

F32 = mybir.dt.float32
I32 = mybir.dt.int32
BF16 = mybir.dt.bfloat16
AF = mybir.ActivationFunctionType
ALU = mybir.AluOpType
AX = mybir.AxisListType

BF = ml_dtypes.bfloat16
NEGM = -1.0e30
D = 2048
T = 2048
RQ = 1024
RK = 512
NE = 9
NCORE = 8
SEQ = 2
NK = 16
EPS = 1e-5
GRP = 256
NT = 23
NTG = NT * GRP

ENGS = ("sync", "gpsimd", "tensor", "vector", "scalar")


class Ctr:
    __slots__ = ("name", "n", "h")

    def __init__(self, name):
        self.name = name
        self.n = 0
        self.h = None


class Kb:
    def __init__(self):
        self.ctrs = []
        self.items = {e: [] for e in ENGS}
        self.ecnt = {e: self.ctr("c_" + e) for e in ("tensor", "vector", "scalar")}
        self._bar = {e: () for e in ENGS}

    def ctr(self, name):
        c = Ctr(name)
        self.ctrs.append(c)
        return c

    def op(self, eng, fn, waits=(), inc=None, incn=1):
        ws = tuple((c, v) for (c, v) in waits if v > 0)
        if self._bar[eng]:
            ws = self._bar[eng] + ws
            self._bar[eng] = ()
        self.items[eng].append((fn, ws, inc, incn))
        if inc is not None:
            inc.n += incn

    def cop(self, eng, fn, waits=(), count=True):
        c = self.ecnt[eng]
        self.op(eng, fn, waits, inc=c if count else None, incn=1)
        return c.n

    def barrier(self):
        snap = tuple((c, c.n) for c in self.ctrs if c.n > 0)
        for e in ENGS:
            self._bar[e] = snap

    def emit(self, nc, block, es):
        for c in self.ctrs:
            c.h = es.enter_context(nc.semaphore(c.name))
        for e in ENGS:
            items = self.items[e]

            def make(lst):
                def run(eng):
                    for fn, ws, inc, incn in lst:
                        for (c, v) in ws:
                            eng.wait_ge(c.h, v)
                        inst = fn(eng) if fn is not None else None
                        if inc is not None:
                            tgt = inst if inst is not None else eng.nop()
                            tgt.then_inc(inc.h, incn)
                return run

            getattr(block, e)(make(items))


class DStream:

    def __init__(self, kb, name, nbuf):
        self.kb = kb
        self.sems = [kb.ctr(f"{name}{i}") for i in range(nbuf)]

    def load(self, eng, par, fn, waits=()):
        s = self.sems[par]
        self.kb.op(eng, fn, waits, inc=s, incn=16)
        return (s, s.n)

    def totals(self):
        return tuple((s, s.n) for s in self.sems if s.n > 0)


class Bank:
    __slots__ = ("ap", "free")

    def __init__(self, ap):
        self.ap = ap
        self.free = ()


def build_program(stage=2):
    nc = bass.Bass("TRN2", target_bir_lowering=False)

    x_in = nc.dram_tensor("x", [SEQ, T, D], F32, kind="ExternalInput").ap()
    wqdT = nc.dram_tensor("wqdT", [128, NK, RQ], F32, kind="ExternalInput").ap()
    wkvdT = nc.dram_tensor("wkvdT", [128, NK, RK], F32, kind="ExternalInput").ap()
    woT = nc.dram_tensor("woT", [128, NK, D], F32, kind="ExternalInput").ap()
    wqu = nc.dram_tensor("wqu", [D, RQ], F32, kind="ExternalInput").ap()
    wku = nc.dram_tensor("wku", [D, RK], F32, kind="ExternalInput").ap()
    wvu = nc.dram_tensor("wvu", [D, RK], F32, kind="ExternalInput").ap()
    bo_p = nc.dram_tensor("bo_p", [128, NK], F32, kind="ExternalInput").ap()
    gateT_p = nc.dram_tensor("gateT_p", [128, NK, 8], F32, kind="ExternalInput").ap()
    eb_p = nc.dram_tensor("eb_p", [128, 8], F32, kind="ExternalInput").ap()
    w13p = nc.dram_tensor("w13p", [NE, 8, 128, 2, NK, 128], F32, kind="ExternalInput").ap()
    w2p = nc.dram_tensor("w2p", [4, 128, NE, 4, 8, 128], F32, kind="ExternalInput").ap()
    idf_p = nc.dram_tensor("idf", [128, 128], F32, kind="ExternalInput").ap()
    idb_p = nc.dram_tensor("idb", [128, 128], BF16, kind="ExternalInput").ap()
    maskb_p = nc.dram_tensor("maskb", [128, 128], BF16, kind="ExternalInput").ap()
    iota_p = nc.dram_tensor("iota_in", [128, 1], F32, kind="ExternalInput").ap()
    grprow_p = nc.dram_tensor("grprow", [1, 32], F32, kind="ExternalInput").ap()
    ltb_p = nc.dram_tensor("ltb", [128, 128], BF16, kind="ExternalInput").ap()
    onesm_p = nc.dram_tensor("onesm", [128, 128], BF16, kind="ExternalInput").ap()
    y_out = nc.dram_tensor("y", [SEQ, T, D], F32, kind="ExternalOutput").ap()

    w13s = nc.dram_tensor("w13s", [NE, 8, 128, 2, NK, 128], BF16, kind="Internal").ap()
    w2s = nc.dram_tensor("w2s", [4, 128, NE, 4, 8, 128], BF16, kind="Internal").ap()
    x1s = nc.dram_tensor("x1s", [SEQ, T, D], F32, kind="Internal").ap()
    xn2s_t = [nc.dram_tensor(f"xn2s{i}", [T, D], BF16, kind="Internal").ap()
              for i in range(SEQ)]
    rtg_t = [nc.dram_tensor(f"rtg{i}", [NTG, 2], F32, kind="Internal").ap()
             for i in range(SEQ)]

    kb = Kb()
    tc, vc, sc = kb.ecnt["tensor"], kb.ecnt["vector"], kb.ecnt["scalar"]

    es_outer = ExitStack()
    sb = lambda name, shape, dt: es_outer.enter_context(nc.sbuf_tensor(name, shape, dt))
    ps = lambda name, shape, dt: es_outer.enter_context(nc.psum_tensor(name, shape, dt))

    idf = sb("idf_sb", [128, 128], F32)
    idb = sb("idb_sb", [128, 128], BF16)
    maskb = sb("maskb_sb", [128, 128], BF16)
    bo_sb = sb("bo_sb", [128, NK], F32)
    eb_sb = sb("eb_sb", [128, 8], F32)
    gateT_sb = sb("gateT_sb", [128, NK, 8], BF16)
    ones_sb = sb("ones_sb", [1, 128], BF16)
    onesf_sb = sb("onesf_sb", [1, 128], F32)
    eps_sb = sb("eps_sb", [128, 1], F32)
    iota_sb = sb("iota_sb", [128, 1], F32)
    iota9_sb = sb("iota9_sb", [128, 1], F32)
    grprow_sb = sb("grprow_sb", [1, 32], F32)
    ltb = sb("ltb_sb", [128, 128], BF16)
    onesm = sb("onesm_sb", [128, 128], BF16)

    bkA = Bank(ps("psA", [128, 512], F32).ap())
    bkB = Bank(ps("psB", [128, 512], F32).ap())
    bkC = Bank(ps("psC", [128, 512], F32).ap())
    bkD = Bank(ps("psD", [128, 512], F32).ap())
    bkE = Bank(ps("psE", [128, 512], F32).ap())
    bkF = Bank(ps("psF", [128, 512], F32).ap())
    bkT0 = Bank(ps("psT0", [128, 1024], BF16).ap())
    bkT1 = Bank(ps("psT1", [128, 1024], BF16).ap())

    ld1 = DStream(kb, "ld1", 1)
    for dst, src in ((idf, idf_p), (idb, idb_p), (maskb, maskb_p),
                     (bo_sb, bo_p), (eb_sb, eb_p), (iota_sb, iota_p),
                     (grprow_sb, grprow_p), (ltb, ltb_p), (onesm, onesm_p)):
        ld1.load("sync", 0, lambda e, d=dst, s=src: e.dma_start(out=d[:], in_=s))
    ldg = DStream(kb, "ldg", 1)
    ldg.load("gpsimd", 0, lambda e: e.dma_start(out=gateT_sb[:], in_=gateT_p))
    bregs = {}

    def _setup_regs(e):
        bregs["tok"] = e.alloc_register("bnd_tok")
        bregs["tab"] = e.alloc_register("bnd_tab")
        bregs["w13"] = e.alloc_register("bnd_w13")
        bregs["w2"] = e.alloc_register("bnd_w2")
        e.reg_alu(bregs["tok"], T - 1, 0, ALU.add)
        e.reg_alu(bregs["w13"], NE * 8 * 128 - 1, 0, ALU.add)
        e.reg_alu(bregs["w2"], 4 * 128 * NE - 1, 0, ALU.add)
        return e.reg_alu(bregs["tab"], NTG - 1, 0, ALU.add)

    kb.op("gpsimd", _setup_regs)
    kb.cop("vector", lambda e: e.memset(ones_sb[:], 1.0))
    kb.cop("vector", lambda e: e.memset(onesf_sb[:], 1.0))
    kb.cop("vector", lambda e: e.memset(eps_sb[:], EPS))
    kb.cop("vector", lambda e: e.tensor_scalar(
        iota9_sb[:], iota_sb[:], 9.0, None, op0=ALU.mult),
        list(ld1.totals()) + list(ldg.totals()))
    kb.barrier()

    es_prep = ExitStack()
    prep_buf = es_prep.enter_context(nc.sbuf_tensor("prep_buf", [128, 2, 4608], BF16))
    prep_ld = DStream(kb, "pld", 2)
    prep_st = DStream(kb, "pst", 2)
    w13p_f = w13p.rearrange("e h p u k f -> (e h) p (u k f)")
    w13s_f = w13s.rearrange("e h p u k f -> (e h) p (u k f)")
    w2p_f = w2p.rearrange("g p e d h f -> (g p) (e d h f)")
    w2s_f = w2s.rearrange("g p e d h f -> (g p) (e d h f)")
    prep_jobs = [(w13p_f[i], w13s_f[i], 4096) for i in range(72)]
    for i in range(4):
        for hf in range(8):
            prep_jobs.append((w2p_f[128 * i:128 * (i + 1), 4608 * hf:4608 * (hf + 1)],
                              w2s_f[128 * i:128 * (i + 1), 4608 * hf:4608 * (hf + 1)],
                              4608))
    prep_state = {"i": 0}

    def pump(n=1):
        for _ in range(n):
            i = prep_state["i"]
            if i >= len(prep_jobs):
                return
            prep_state["i"] = i + 1
            par = i % 2
            src, dst, width = prep_jobs[i]
            waits = []
            if i >= 2:
                waits.append((prep_st.sems[par], prep_st.sems[par].n))
            ltok = prep_ld.load("gpsimd", par,
                                lambda e, s=src, p=par, w=width: e.dma_start(
                                    out=prep_buf[:, p, 0:w], in_=s), waits)
            prep_st.load("sync", par,
                         lambda e, d=dst, p=par, w=width: e.dma_start(
                             out=d, in_=prep_buf[:, p, 0:w]), [ltok])

    def phase_A(s, QlatT, CkvT):
        esA = ExitStack()
        sbA = lambda nm, shape, dt: esA.enter_context(nc.sbuf_tensor(f"{nm}_a{s}", shape, dt))
        xtile = sbA("xtile", [128, 2, D], F32)
        xn_bf = sbA("xn_bf", [128, 2, D], BF16)
        xnT = sbA("xnT", [128, NK, 512], BF16)
        stats = sbA("astat", [128, 2, 8], F32)
        wqd_sb = sbA("wqd_sb", [128, NK, RQ], BF16)
        wkvd_sb = sbA("wkvd_sb", [128, NK, RK], BF16)

        xA = DStream(kb, f"xA{s}", 2)
        wA = DStream(kb, f"wA{s}", 1)
        wA.load("gpsimd", 0, lambda e: e.dma_start(out=wqd_sb[:], in_=wqdT))
        tok_w = wA.load("gpsimd", 0, lambda e: e.dma_start(out=wkvd_sb[:], in_=wkvdT))
        xt_free = [(), ()]
        qkv_banks = [bkA, bkB]
        bkCb = Bank(bkC.ap.bitcast(BF16))
        bkCb.free = bkC.free
        bkDb = Bank(bkD.ap.bitcast(BF16))
        bkDb.free = bkD.free
        tp_banks = [bkT0, bkT1, bkCb, bkDb]
        rot = [0, 0]

        for cg in range(4):
            for t in range(4):
                i = 4 * cg + t
                par = i % 2
                tok = xA.load("sync", par, lambda e, p=par, r=512 * cg + 128 * t:
                              e.dma_start(out=xtile[:, p], in_=x_in[s, r:r + 128, :]),
                              xt_free[par])
                _ln(kb, xtile[:, par], xn_bf[:, par], stats[:, par], xn_bf[:, par], eps_sb[:, 0:1], tok,
                    [(tc, tc.n)])
                xt_free[par] = [(vc, vc.n)]
                xn_done = vc.n
                for k in range(NK):
                    bk = tp_banks[rot[0] % 4]
                    rot[0] += 1
                    cap = kb.cop("tensor", lambda e, b=bk, p=par, kk=k: e.transpose(
                        b.ap[:, 0:128], xn_bf[:, p, 128 * kk:128 * (kk + 1)], idb[:]),
                        [(vc, xn_done)] + list(bk.free))
                    kb.cop("vector", lambda e, b=bk, kk=k, tt=t: e.tensor_copy(
                        xnT[:, kk, 128 * tt:128 * (tt + 1)], b.ap[:, 0:128]), [(tc, cap)])
                    bk.free = ((vc, vc.n),)
            xnT_ready = vc.n
            pump(4)
            for rt in range(12):
                q_side = rt < 8
                wsel = wqd_sb if q_side else wkvd_sb
                col = rt if q_side else rt - 8
                bk = qkv_banks[rot[1] % 2]
                rot[1] += 1
                for k in range(NK):
                    waits = ([(vc, xnT_ready), tok_w] + list(bk.free)) if k == 0 else []
                    kb.cop("tensor", lambda e, b=bk, w=wsel, c=col, kk=k:
                           e.matmul(b.ap[:], w[:, kk, 128 * c:128 * (c + 1)],
                                    xnT[:, kk, :], start=(kk == 0), stop=(kk == NK - 1)),
                           waits, count=(k == NK - 1))
                cap = tc.n
                dst = QlatT if q_side else CkvT
                kb.cop("vector", lambda e, b=bk, d_=dst, c=col, g=cg: e.tensor_copy(
                    d_[:, c, 512 * g:512 * (g + 1)], b.ap[:]), [(tc, cap)])
                bk.free = ((vc, vc.n),)
        bkC.free = bkCb.free
        bkD.free = bkDb.free
        esA.close()

    def phase_B(s, QlatT, CkvT, ctxT):
        esB = ExitStack()
        sbB = lambda nm, shape, dt: esB.enter_context(nc.sbuf_tensor(f"{nm}_b{s}", shape, dt))
        wqh = sbB("wqh", [128, 2, RQ], BF16)
        wkh = sbB("wkh", [128, 2, RK], BF16)
        wvh = sbB("wvh", [128, 2, RK], BF16)
        uvT = sbB("uvT", [128, 4, 128], BF16)
        mm_sb = sbB("mm_sb", [128, 8, 512], BF16)
        qhT = sbB("qhT", [128, 4, T], BF16)
        vup = sbB("vup", [128, NK, 128], BF16)
        p_sb = sbB("p_sb", [128, 2, T], BF16)
        pt4 = sbB("pt4", [128, NK, 512], BF16)
        mst = sbB("mst", [128, 8], F32)
        sstat = sbB("sstat", [128, 4], F32)

        whl = DStream(kb, f"wh{s}", 2)
        wh_free = [(), ()]
        lg_banks = [bkA, bkB, bkC, bkD, bkE]
        bkFb = Bank(bkF.ap.bitcast(BF16))
        bkFb.free = bkF.free
        tp_banks = [bkT0, bkT1, bkFb]
        rot = {"lg": 0, "tp": 0}
        prev_pv = [0]

        for h in range(16):
            par = h % 2
            whl.load("gpsimd", par, lambda e, p=par, hh=h: e.dma_start(
                out=wqh[:, p], in_=wqu[128 * hh:128 * (hh + 1), :]), wh_free[par])
            whl.load("gpsimd", par, lambda e, p=par, hh=h: e.dma_start(
                out=wkh[:, p], in_=wku[128 * hh:128 * (hh + 1), :]))
            tok_h = whl.load("gpsimd", par, lambda e, p=par, hh=h: e.dma_start(
                out=wvh[:, p], in_=wvu[128 * hh:128 * (hh + 1), :]))
            pump(6)
            for c in range(4):
                bk = tp_banks[rot["tp"] % 3]
                rot["tp"] += 1
                cap = kb.cop("tensor", lambda e, b=bk, p=par, cc=c: e.transpose(
                    b.ap[:, 0:128], wvh[:, p, 128 * cc:128 * (cc + 1)], idb[:]),
                    [tok_h] + list(bk.free))
                kb.cop("vector", lambda e, b=bk, cc=c: e.tensor_copy(
                    uvT[:, cc, :], b.ap[:, 0:128]), [(tc, cap)])
                bk.free = ((vc, vc.n),)
            for j in range(8):
                bk = lg_banks[rot["lg"] % 5]
                rot["lg"] += 1
                cap = kb.cop("tensor", lambda e, b=bk, p=par, jj=j: e.matmul(
                    b.ap[:], wqh[:, p, 128 * jj:128 * (jj + 1)], wkh[:, p],
                    start=True, stop=True), [tok_h] + list(bk.free))
                kb.cop("vector", lambda e, b=bk, jj=j: e.tensor_scalar(
                    mm_sb[:, jj, :], b.ap[:], 1.0 / math.sqrt(128.0), None,
                    op0=ALU.mult), [(tc, cap)])
                bk.free = ((vc, vc.n),)
            mm_done = vc.n
            wh_free[par] = [(tc, tc.n)]
            for rt in range(4):
                for g4 in range(4):
                    bk = lg_banks[rot["lg"] % 5]
                    rot["lg"] += 1
                    for j in range(8):
                        waits = ([(vc, mm_done)] + list(bk.free)) if j == 0 else []
                        kb.cop("tensor", lambda e, b=bk, r=rt, g=g4, jj=j, Q=QlatT:
                               e.matmul(b.ap[:], mm_sb[:, jj, 128 * r:128 * (r + 1)],
                                        Q[:, jj, 512 * g:512 * (g + 1)],
                                        start=(jj == 0), stop=(jj == 7)),
                               waits, count=(j == 7))
                    cap = tc.n
                    kb.cop("vector", lambda e, b=bk, r=rt, g=g4: e.tensor_copy(
                        qhT[:, r, 512 * g:512 * (g + 1)], b.ap[:]), [(tc, cap)])
                    bk.free = ((vc, vc.n),)
            uv_done = vc.n
            for st in range(NK):
                bk = lg_banks[rot["lg"] % 5]
                rot["lg"] += 1
                for c in range(4):
                    waits = ([(vc, uv_done)] + list(bk.free)) if c == 0 else []
                    kb.cop("tensor", lambda e, b=bk, s_=st, cc=c, C=CkvT: e.matmul(
                        b.ap[:, 0:128], C[:, cc, 128 * s_:128 * (s_ + 1)],
                        uvT[:, cc, :], start=(cc == 0), stop=(cc == 3)),
                        waits, count=(c == 3))
                cap = tc.n
                kb.cop("vector", lambda e, b=bk, s_=st: e.tensor_copy(
                    vup[:, s_, :], b.ap[:, 0:128]), [(tc, cap)])
                bk.free = ((vc, vc.n),)
            qh_done = vc.n

            def emit_tr(tq, p_done):
                par = tq % 2
                g = tq // 4
                m = tq - 4 * g
                for st in range(tq + 1):
                    bk = tp_banks[rot["tp"] % 3]
                    rot["tp"] += 1
                    cap = kb.cop("tensor", lambda e, b=bk, s_=st, p=par: e.transpose(
                        b.ap[:, 0:128], p_sb[:, p, 128 * s_:128 * (s_ + 1)], idb[:]),
                        [(vc, p_done)] + list(bk.free))
                    kb.cop("vector", lambda e, b=bk, s_=st, mm=m: e.tensor_copy(
                        pt4[:, s_, 128 * mm:128 * (mm + 1)], b.ap[:, 0:128]),
                        [(tc, cap)])
                    bk.free = ((vc, vc.n),)
                for st in range(tq + 1, 4 * g + 4):
                    kb.cop("vector", lambda e, s_=st, mm=m: e.memset(
                        pt4[:, s_, 128 * mm:128 * (mm + 1)], 0.0),
                        [(tc, prev_pv[0])])
                return vc.n

            def emit_pv(g):
                bk = lg_banks[rot["lg"] % 5]
                rot["lg"] += 1
                nst = 4 * g + 4
                for st in range(nst):
                    waits = ([(vc, pt_done[0])] + list(bk.free)) if st == 0 else []
                    kb.cop("tensor", lambda e, b=bk, s_=st, n=nst: e.matmul(
                        b.ap[:], vup[:, s_, :], pt4[:, s_, :],
                        start=(s_ == 0), stop=(s_ == n - 1)),
                        waits, count=(st == nst - 1))
                prev_pv[0] = tc.n
                kb.cop("vector", lambda e, b=bk, hh=h, gg=g, X=ctxT: e.tensor_copy(
                    X[:, hh, 512 * gg:512 * (gg + 1)], b.ap[:]), [(tc, prev_pv[0])])
                bk.free = ((vc, vc.n),)

            pend = None
            pt_done = [0]
            for tq in range(16):
                g = tq // 4
                m = tq - 4 * g
                par = tq % 2
                slen = 128 * (tq + 1)
                ng = (slen + 511) // 512
                exp_banks = []
                for scg in range(ng):
                    w = min(512, slen - 512 * scg)
                    bk = lg_banks[rot["lg"] % 5]
                    rot["lg"] += 1
                    diag = (scg == ng - 1)
                    for c in range(4):
                        waits = ([(vc, qh_done)] + list(bk.free)) if c == 0 else []
                        kb.cop("tensor", lambda e, b=bk, cc=c, q=tq, sg=scg, ww=w,
                               C=CkvT, dg=diag:
                               e.matmul(b.ap[:, 0:ww],
                                        qhT[:, cc, 128 * q:128 * (q + 1)],
                                        C[:, cc, 512 * sg:512 * sg + ww],
                                        start=(cc == 0), stop=(cc == 3 and not dg)),
                               waits, count=(c == 3 and not diag))
                    if diag:
                        kb.cop("tensor", lambda e, b=bk, ww=w: e.matmul(
                            b.ap[:, ww - 128:ww], idb[:], maskb[:],
                            start=False, stop=True), [], count=True)
                    cap = tc.n
                    kb.cop("vector", lambda e, b=bk, sg=scg, ww=w: e.reduce_max(
                        mst[:, sg:sg + 1], b.ap[:, 0:ww], axis=AX.X), [(tc, cap)])
                    exp_banks.append((bk, w))
                kb.cop("vector", lambda e, n=ng: e.reduce_max(
                    sstat[:, 0:1], mst[:, 0:n], axis=AX.X, negate=True),
                    [(vc, vc.n), (sc, sc.n)])
                negm_done = vc.n
                if pend is not None:
                    pt_done[0] = emit_tr(*pend)
                if m == 0 and g > 0:
                    emit_pv(g - 1)
                for scg, (bk, w) in enumerate(exp_banks):
                    kb.cop("scalar", lambda e, b=bk, sg=scg, ww=w, p=par:
                           e.activation(
                               p_sb[:, p, 512 * sg:512 * sg + ww], b.ap[:, 0:ww],
                               AF.Exp, bias=sstat[:, 0:1],
                               accum_out=mst[:, 4 + sg:5 + sg]),
                           [(vc, negm_done)])
                    bk.free = ((sc, sc.n),)
                exp_done = sc.n
                kb.cop("vector", lambda e, n=ng: e.reduce_sum(
                    sstat[:, 1:2], mst[:, 4:4 + n], axis=AX.X), [(sc, exp_done)])
                kb.cop("vector", lambda e: e.reciprocal(sstat[:, 2:3], sstat[:, 1:2]),
                       [(vc, vc.n)])
                kb.cop("vector", lambda e, L=slen, p=par: e.tensor_scalar(
                    p_sb[:, p, 0:L], p_sb[:, p, 0:L], sstat[:, 2:3], None,
                    op0=ALU.mult), [(vc, vc.n)])
                pend = (tq, vc.n)
            pt_done[0] = emit_tr(*pend)
            emit_pv(3)
        bkF.free = bkFb.free
        esB.close()

    def phase_C(s, ctxT):
        esC = ExitStack()
        sbC = lambda nm, shape, dt: esC.enter_context(nc.sbuf_tensor(f"{nm}_c{s}", shape, dt))
        wo_sb = sbC("wo_sb", [128, NK, D], BF16)
        xcg = sbC("xcg", [128, 4, D], F32)
        wsb = sbC("wsbC", [128, 2, 512], F32)
        piece = sbC("pieceC", [128, 2, 128], F32)

        wC = DStream(kb, f"wC{s}", 2)
        tok_wo1 = wC.load("gpsimd", 0, lambda e: e.dma_start(
            out=wo_sb[:, :, 0:1024], in_=woT[:, :, 0:1024]))
        tok_wo2 = wC.load("gpsimd", 1, lambda e: e.dma_start(
            out=wo_sb[:, :, 1024:2048], in_=woT[:, :, 1024:2048]))
        xC = DStream(kb, f"xC{s}", 1)
        stC = DStream(kb, f"stC{s}", 2)
        acc_banks = [bkA, bkB]
        tr_banks = [bkC, bkD]
        xc_free = []
        piece_free = [(), ()]
        wsb_free = [(), ()]
        rot = {"a": 0, "t": 0, "p": 0}
        dest = x1s if stage >= 2 else y_out

        for cg in range(4):
            tokx = xC.load("sync", 0, lambda e, c=cg: e.dma_start(
                out=xcg[:],
                in_=x_in[s, 512 * c:512 * (c + 1), :].rearrange("(t p) d -> p t d", p=128)),
                list(xc_free))
            pump(2)
            for dc in range(NK):
                bk = acc_banks[rot["a"] % 2]
                rot["a"] += 1
                for k in range(NK):
                    tw = tok_wo1 if dc < 8 else tok_wo2
                    waits = ([tw] + list(bk.free)) if k == 0 else []
                    kb.cop("tensor", lambda e, b=bk, d_=dc, kk=k, c=cg, X=ctxT: e.matmul(
                        b.ap[:], wo_sb[:, kk, 128 * d_:128 * (d_ + 1)],
                        X[:, kk, 512 * c:512 * (c + 1)],
                        start=(kk == 0), stop=(kk == NK - 1)),
                        waits, count=(k == NK - 1))
                cap = tc.n
                wp = dc % 2
                kb.cop("vector", lambda e, b=bk, d_=dc, p=wp: e.tensor_scalar(
                    wsb[:, p], b.ap[:], bo_sb[:, d_:d_ + 1], None, op0=ALU.add),
                    [(tc, cap)] + list(wsb_free[wp]))
                bk.free = ((vc, vc.n),)
                wsb_done = vc.n
                for t in range(4):
                    tb = tr_banks[rot["t"] % 2]
                    rot["t"] += 1
                    capt = kb.cop("tensor", lambda e, b=tb, p=wp, tt=t: e.transpose(
                        b.ap[:, 0:128], wsb[:, p, 128 * tt:128 * (tt + 1)], idf[:]),
                        [(vc, wsb_done)] + list(tb.free))
                    pp = rot["p"] % 2
                    rot["p"] += 1
                    kb.cop("vector", lambda e, b=tb, tt=t, d_=dc, q=pp: e.tensor_add(
                        piece[:, q], b.ap[:, 0:128], xcg[:, tt, 128 * d_:128 * (d_ + 1)]),
                        [(tc, capt), tokx] + list(piece_free[pp]))
                    tb.free = ((vc, vc.n),)
                    addcap = vc.n
                    stC.load("gpsimd", pp, lambda e, q=pp, r=512 * cg + 128 * t, d_=dc:
                             e.dma_start(
                                 out=dest[s, r:r + 128, 128 * d_:128 * (d_ + 1)],
                                 in_=piece[:, q]), [(vc, addcap)])
                    piece_free[pp] = [(stC.sems[pp], stC.sems[pp].n)]
                wsb_free[wp] = [(tc, tc.n)]
            xc_free = [(vc, vc.n)]
        esC.close()
        return stC.totals()

    def phase_E(s, stC_tot, prep_tot):
        esE = ExitStack()
        sbE = lambda nm, shape, dt: esE.enter_context(nc.sbuf_tensor(f"{nm}_e{s}", shape, dt))
        xn2T = sbE("xn2T", [128, NK, 512], BF16)
        h_sb = sbE("h_sb", [128, 8, 512], BF16)
        w13buf = sbE("w13buf", [128, 3, 2 * NK * 128], BF16)
        w2buf = sbE("w2buf", [128, 2, 4096], BF16)
        scores = sbE("scores", [128, 4, 8], F32)
        gtok = sbE("gtok", [128, 8], F32)
        gtmp = sbE("gtmp", [128, 3, 8], F32)
        gm = sbE("gm", [128, 2], F32)
        xcg1 = sbE("xcg1", [128, 4, D], F32)
        xn2_bf = sbE("xn2_bf", [128, 2, D], BF16)
        stats = sbE("estat", [128, 8], F32)
        tmp1 = sbE("tmp1", [128, 2, 512], BF16)
        wsb = sbE("wsbE", [128, 2, 512], F32)
        piece = sbE("pieceE", [128, 2, 128], F32)
        crow = sbE("crow", [1, 16], F32)
        brow = sbE("brow", [1, 16], F32)
        etab = sbE("etab", [1, 64], F32)
        esT = ExitStack()
        sbT = lambda nm, shape, dt: esT.enter_context(nc.sbuf_tensor(f"{nm}_e{s}", shape, dt))
        gcols = sbT("gcols", [128, 16, 8], F32)
        mcols = sbT("mcols", [128, 16, 8], BF16)
        scols = sbT("scols", [128, 16, 8], F32)
        basesb = sbT("basesb", [128, 8], F32)
        ptile = sbT("ptile", [128, 8], F32)
        fill2 = sbT("fill2", [128, NTG // 128, 2], F32)
        hugec = sbT("hugec", [128, 1], F32)
        sdata = sbT("sdata", [128, 2, 2], F32)
        scol = sbT("scol", [128, 2, 4], F32)
        sidx = sbT("sidx", [128, 2, 1], I32)
        e4b = sbT("e4b", [128, 1], F32)

        x1L = DStream(kb, f"x1L{s}", 1)
        xnS = DStream(kb, f"xnS{s}", 2)
        w13L = DStream(kb, f"w13L{s}", 2)
        w2L = DStream(kb, f"w2L{s}", 2)
        stE = DStream(kb, f"stE{s}", 2)
        tfill = DStream(kb, f"tf{s}", 1)
        tscat = DStream(kb, f"ts{s}", 2)
        rload = DStream(kb, f"rl{s}", 4)
        wg13 = DStream(kb, f"wg13{s}", 3)
        wg2 = DStream(kb, f"wg2{s}", 3)
        xgL = DStream(kb, f"xgL{s}", 4)
        stR = DStream(kb, f"stR{s}", 1)
        u1_banks = [bkA, bkC]
        u3_banks = [bkB, bkD]
        acc_banks = [bkA, bkB]
        tr_banks = [bkC, bkD]
        x1_free = []
        w13_free = [(), (), ()]
        w2_free = [(), ()]
        piece_free = [(), ()]
        wsb_free = [(), ()]
        xn_free = [(), ()]
        tscat_free = [(), ()]
        rot = {"u": 0, "a": 0, "t": 0, "p": 0, "ih": 0}
        bkFe = Bank(bkF.ap.bitcast(BF16))
        bkFe.free = bkF.free
        ln_pool = [bkT0, bkT1, bkFe]
        lnr = [0]
        w13s_rows = w13s.rearrange("e h p u k f -> (e h p) (u k f)")
        w2s_rows = w2s.rearrange("g p e d h f -> (g p e) (d h f)")
        xn2s_rows = xn2s_t[s]
        rtg_s = rtg_t[s]
        y_flat = y_out.rearrange("q t d -> (q t) d")

        e4_steps = []
        e4refs = {}

        def _e4_fill():
            bkF.free = bkFe.free
            kb.cop("vector", lambda e: e.memset(fill2[:, :, 0:1], float(1 << 23)),
                   [(vc, vc.n)])
            kb.cop("vector", lambda e: e.memset(fill2[:, :, 1:2], 0.0), [])
            fill_done = vc.n
            e4refs["ftok"] = tfill.load("gpsimd", 0, lambda e: e.dma_start(
                out=rtg_s.rearrange("(a p) c -> p a c", p=128), in_=fill2[:]),
                [(vc, fill_done)])
            kb.cop("vector", lambda e: e.tensor_scalar(
                mcols.rearrange("p a b -> p (a b)"),
                gcols.rearrange("p a b -> p (a b)"), 0.0, None, op0=ALU.is_gt),
                [(vc, vc.n)])
            kb.cop("vector", lambda e: e.tensor_scalar(
                scols.rearrange("p a b -> p (a b)"),
                gcols.rearrange("p a b -> p (a b)"), 0.0, None, op0=ALU.is_gt),
                [(vc, vc.n)])
        e4_steps.append(_e4_fill)

        def _e4_counts():
            mdone = vc.n
            for t in range(NK):
                kb.cop("tensor", lambda e, tt=t: e.matmul(
                    bkE.ap[:, 0:8], onesm[:], mcols[:, tt, :],
                    start=(tt == 0), stop=(tt == NK - 1)),
                    ([(vc, mdone)] + list(bkE.free)) if t == 0 else [],
                    count=(t == NK - 1))
            cap = tc.n
            kb.cop("vector", lambda e: e.tensor_copy(
                crow[0:1, 0:8], bkE.ap[0:1, 0:8]), [(tc, cap), (vc, vc.n)])
            bkE.free = ((vc, vc.n),)
        e4_steps.append(_e4_counts)

        def _e4_bases():
            kb.cop("vector", lambda e: e.tensor_scalar(
                crow[0:1, 8:16], crow[0:1, 0:8], float(GRP - 1), 1.0 / GRP,
                op0=ALU.add, op1=ALU.mult), [(vc, vc.n)])
            kb.cop("vector", lambda e: e.tensor_scalar(
                crow[0:1, 8:16], crow[0:1, 8:16], -0.498046875, 12582912.0,
                op0=ALU.add, op1=ALU.add), [(vc, vc.n)])
            kb.cop("vector", lambda e: e.tensor_scalar(
                crow[0:1, 8:16], crow[0:1, 8:16], -12582912.0, float(GRP),
                op0=ALU.add, op1=ALU.mult), [(vc, vc.n)])
            kb.cop("vector", lambda e: e.memset(etab[:], 0.0), [(vc, vc.n)])
            kb.cop("vector", lambda e: e.tensor_tensor_scan(
                brow[0:1, 1:9], crow[0:1, 8:16], etab[0:1, 0:8], 0.0,
                op0=ALU.add, op1=ALU.add), [(vc, vc.n)])
            kb.cop("vector", lambda e: e.memset(brow[0:1, 0:1], 0.0), [(vc, vc.n)])
            for x in range(1, 9):
                kb.cop("vector", lambda e, x_=x: e.tensor_scalar(
                    etab[0:1, 32:32 + NT], grprow_sb[0:1, 0:NT], brow[0:1, x_:x_ + 1],
                    None, op0=ALU.is_ge), [(vc, vc.n)])
                kb.cop("vector", lambda e: e.tensor_add(
                    etab[0:1, 0:NT], etab[0:1, 0:NT], etab[0:1, 32:32 + NT]),
                    [(vc, vc.n)])
            e4refs["etab_done"] = vc.n
            kb.cop("vector", lambda e: e.tensor_scalar(
                crow[0:1, 8:16], brow[0:1, 0:8], -1.0,
                None, op0=ALU.add), [(vc, vc.n)])
            capb = kb.cop("tensor", lambda e: e.matmul(
                bkF.ap[:, 0:8], onesf_sb[:], crow[0:1, 8:16], start=True, stop=True),
                [(vc, vc.n)] + list(bkF.free))
            kb.cop("vector", lambda e: e.tensor_copy(basesb[:], bkF.ap[:, 0:8]),
                   [(tc, capb)])
            bkF.free = ((vc, vc.n),)
        e4_steps.append(_e4_bases)

        def _e4_tile(t):
            def f():
                par = t % 2
                bk = bkE if t % 2 == 0 else bkF
                for tp in range(t + 1):
                    last = (tp == t)
                    kb.cop("tensor", lambda e, b=bk, q=tp, lt=last: e.matmul(
                        b.ap[:, 0:8], ltb[:] if lt else onesm[:], mcols[:, q, :],
                        start=(q == 0), stop=lt),
                        list(bk.free) if tp == 0 else [], count=last)
                cap = tc.n
                kb.cop("vector", lambda e, b=bk: e.tensor_copy(ptile[:], b.ap[:, 0:8]),
                       [(tc, cap), (vc, vc.n)])
                bk.free = ((vc, vc.n),)
                kb.cop("vector", lambda e: e.tensor_add(ptile[:], ptile[:], basesb[:]),
                       [(vc, vc.n)])
                kb.cop("vector", lambda e: e.tensor_scalar(
                    ptile[:], ptile[:], -float(1 << 23), None, op0=ALU.add),
                    [(vc, vc.n)])
                kb.cop("vector", lambda e, tt=t: e.tensor_mul(
                    ptile[:], ptile[:], scols[:, tt, :]), [(vc, vc.n)])
                kb.cop("vector", lambda e: e.tensor_scalar(
                    ptile[:], ptile[:], float(1 << 23), None, op0=ALU.add),
                    [(vc, vc.n)])
                kb.cop("vector", lambda e, p=par, tt=t: e.tensor_scalar(
                    sdata[:, p, 0:1], iota_sb[:], float(128 * tt), None, op0=ALU.add),
                    [(vc, vc.n)] + list(tscat_free[par]))
                pdone = vc.n
                for ee in range(8):
                    kb.cop("vector", lambda e, p=par, tt=t, x=ee: e.tensor_copy(
                        sdata[:, p, 1:2], gcols[:, tt, x:x + 1]),
                        [(vc, vc.n)] + (list(tscat_free[par]) if ee > 0 else []))
                    kb.cop("vector", lambda e, p=par, x=ee: e.tensor_copy(
                        sidx[:, p], ptile[:, x:x + 1]), [(vc, vc.n)])
                    dcap = vc.n
                    stok = tscat.load("gpsimd", par,
                                      lambda e, p=par: e.indirect_dma_start(
                        out=rtg_s,
                        out_offset=IndirectOffsetOnAxis(ap=sidx[:, p], axis=0),
                        in_=sdata[:, p], in_offset=None,
                        bounds_check=bregs["tab"], oob_is_err=False),
                        [(vc, dcap), e4refs["ftok"]])
                    tscat_free[par] = [stok]
            return f
        for _t in range(NK):
            e4_steps.append(_e4_tile(_t))
        e4_state = {"i": 0}

        def e4pump(n):
            for _ in range(n):
                i = e4_state["i"]
                if i >= len(e4_steps):
                    return
                e4_state["i"] = i + 1
                e4_steps[i]()

        for grp in range(4):
            tokx = x1L.load("sync", 0, lambda e, g=grp: e.dma_start(
                out=xcg1[:],
                in_=x1s[s, 512 * g:512 * (g + 1), :].rearrange("(t p) d -> p t d", p=128)),
                list(stC_tot) + list(x1_free))
            tile_caps = []
            for t in range(4):
                par = t % 2
                xnd = _ln(kb, xcg1[:, t, :], xn2_bf[:, par], stats, xn2_bf[:, par],
                          eps_sb[:, 0:1], tokx, [(tc, tc.n)] + list(xn_free[par]))
                xnS.load("gpsimd", par, lambda e, p=par, r=512 * grp + 128 * t:
                         e.dma_start(out=xn2s_rows[r:r + 128, :], in_=xn2_bf[:, p]),
                         [(vc, xnd)])
                xn_free[par] = [(xnS.sems[par], xnS.sems[par].n)]
                for k in range(NK):
                    bk = ln_pool[lnr[0] % 3]
                    lnr[0] += 1
                    cap = kb.cop("tensor", lambda e, b=bk, kk=k, p=par: e.transpose(
                        b.ap[:, 0:128], xn2_bf[:, p, 128 * kk:128 * (kk + 1)], idb[:]),
                        [(vc, xnd)] + list(bk.free))
                    kb.cop("vector", lambda e, b=bk, kk=k, tt=t: e.tensor_copy(
                        xn2T[:, kk, 128 * tt:128 * (tt + 1)], b.ap[:, 0:128]), [(tc, cap)])
                    bk.free = ((vc, vc.n),)
                tile_caps.append(vc.n)
            for t in range(4):
                for k in range(NK):
                    waits = ([(vc, tile_caps[t])] + list(bkE.free)) if k == 0 else []
                    kb.cop("tensor", lambda e, kk=k, tt=t: e.matmul(
                        bkE.ap[:, 0:8], xn2T[:, kk, 128 * tt:128 * (tt + 1)],
                        gateT_sb[:, kk, :], start=(kk == 0), stop=(kk == NK - 1)),
                        waits, count=(k == NK - 1))
                cap = tc.n
                kb.cop("scalar", lambda e, tt=t: e.activation(
                    scores[:, tt, :], bkE.ap[:, 0:8], AF.Sigmoid), [(tc, cap), (vc, vc.n)])
                bkE.free = ((sc, sc.n),)
                sgd = sc.n
                kb.cop("vector", lambda e, tt=t: e.tensor_add(
                    gtmp[:, 0, :], scores[:, tt, :], eb_sb[:]), [(sc, sgd), (vc, vc.n)])
                kb.cop("vector", lambda e: e.reduce_max(gm[:, 0:1], gtmp[:, 0, :],
                                                        axis=AX.X), [(vc, vc.n)])
                kb.cop("vector", lambda e: e.tensor_scalar(
                    gtmp[:, 1, :], gtmp[:, 0, :], gm[:, 0:1], None, op0=ALU.is_equal),
                    [(vc, vc.n)])
                kb.cop("vector", lambda e: e.scalar_tensor_tensor(
                    gtmp[:, 2, :], gtmp[:, 1, :], NEGM, gtmp[:, 0, :],
                    op0=ALU.mult, op1=ALU.add), [(vc, vc.n)])
                kb.cop("vector", lambda e: e.reduce_max(gm[:, 1:2], gtmp[:, 2, :],
                                                        axis=AX.X), [(vc, vc.n)])
                kb.cop("vector", lambda e: e.tensor_scalar(
                    gtmp[:, 2, :], gtmp[:, 2, :], gm[:, 1:2], None, op0=ALU.is_equal),
                    [(vc, vc.n)])
                kb.cop("vector", lambda e: e.tensor_add(
                    gtmp[:, 1, :], gtmp[:, 1, :], gtmp[:, 2, :]), [(vc, vc.n)])
                kb.cop("vector", lambda e, tt=t: e.tensor_mul(
                    gtok[:], scores[:, tt, :], gtmp[:, 1, :]), [(vc, vc.n), (tc, tc.n)])
                kb.cop("vector", lambda e, ti=4 * grp + t: e.tensor_copy(
                    gcols[:, ti, :], gtok[:]), [(vc, vc.n)])
            for hc in range(8):
                ih = rot["ih"]
                rot["ih"] += 1
                wp = ih % 2
                tokw = w13L.load("sync", wp, lambda e, p=wp, hh=hc: e.dma_start(
                    out=w13buf[:, p], in_=w13s[8, hh]),
                    list(prep_tot) + list(w13_free[wp]))
                b1 = u1_banks[rot["u"] % 2]
                b3 = u3_banks[rot["u"] % 2]
                rot["u"] += 1
                for k in range(NK):
                    waits = ([tokw, (vc, tile_caps[3])] + list(b1.free)) if k == 0 else []
                    kb.cop("tensor", lambda e, b=b1, p=wp, kk=k: e.matmul(
                        b.ap[:], w13buf[:, p, 128 * kk:128 * (kk + 1)], xn2T[:, kk, :],
                        start=(kk == 0), stop=(kk == NK - 1)),
                        waits, count=(k == NK - 1))
                cap1 = tc.n
                for k in range(NK):
                    waits = list(b3.free) if k == 0 else []
                    kb.cop("tensor", lambda e, b=b3, p=wp, kk=k: e.matmul(
                        b.ap[:], w13buf[:, p, 2048 + 128 * kk:2048 + 128 * (kk + 1)], xn2T[:, kk, :],
                        start=(kk == 0), stop=(kk == NK - 1)),
                        waits, count=(k == NK - 1))
                cap3 = tc.n
                w13_free[wp] = [(tc, cap3)]
                hp = hc % 2
                kb.cop("scalar", lambda e, b=b1, p=hp: e.activation(
                    tmp1[:, p], b.ap[:], AF.Silu), [(tc, cap1), (vc, vc.n), (sc, sc.n)])
                b1.free = ((sc, sc.n),)
                sil = sc.n
                kb.cop("vector", lambda e, b=b3, hh=hc, p=hp: e.tensor_mul(
                    h_sb[:, hh, :], b.ap[:], tmp1[:, p]),
                    [(tc, cap3), (sc, sil), (vc, vc.n)])
                b3.free = ((vc, vc.n),)
                if grp == 3:
                    e4pump(8)
            h_done = vc.n
            pendw = None

            def emit_epi(dc, wsb_done):
                sp = dc % 2
                for t in range(4):
                    tb = tr_banks[rot["t"] % 2]
                    rot["t"] += 1
                    capt = kb.cop("tensor", lambda e, b=tb, p=sp, tt=t: e.transpose(
                        b.ap[:, 0:128], wsb[:, p, 128 * tt:128 * (tt + 1)], idf[:]),
                        [(vc, wsb_done)] + list(tb.free))
                    pp = rot["p"] % 2
                    rot["p"] += 1
                    kb.cop("vector", lambda e, b=tb, tt=t, d_=dc, q=pp: e.tensor_add(
                        piece[:, q], b.ap[:, 0:128], xcg1[:, tt, 128 * d_:128 * (d_ + 1)]),
                        [(tc, capt), tokx] + list(piece_free[pp]))
                    tb.free = ((vc, vc.n),)
                    addcap = vc.n
                    stE.load("gpsimd", pp, lambda e, q=pp, r=512 * grp + 128 * t, d_=dc:
                             e.dma_start(
                                 out=y_out[s, r:r + 128, 128 * d_:128 * (d_ + 1)],
                                 in_=piece[:, q]), [(vc, addcap)])
                    piece_free[pp] = [(stE.sems[pp], stE.sems[pp].n)]
                wsb_free[sp] = [(tc, tc.n)]

            for dc in range(NK):
                wp = dc % 2
                tokw2 = w2L.load("sync", wp, lambda e, p=wp, d_=dc: e.dma_start(
                    out=w2buf[:, p, 0:1024], in_=w2s[d_ // 4, :, 8, d_ % 4]),
                    list(prep_tot) + list(w2_free[wp]))
                bk = acc_banks[rot["a"] % 2]
                rot["a"] += 1
                for hc in range(8):
                    waits = ([(vc, h_done), tokw2] + list(bk.free)) if hc == 0 else []
                    kb.cop("tensor", lambda e, b=bk, p=wp, hh=hc: e.matmul(
                        b.ap[:], w2buf[:, p, 128 * hh:128 * (hh + 1)], h_sb[:, hh, :],
                        start=(hh == 0), stop=(hh == 7)),
                        waits, count=(hc == 7))
                cap = tc.n
                w2_free[wp] = [(tc, cap)]
                sp = dc % 2
                kb.cop("vector", lambda e, b=bk, p=sp: e.tensor_copy(
                    wsb[:, p], b.ap[:]), [(tc, cap)] + list(wsb_free[sp]))
                bk.free = ((vc, vc.n),)
                if pendw is not None:
                    emit_epi(*pendw)
                pendw = (dc, vc.n)
                if grp == 3:
                    e4pump(12)
            emit_epi(*pendw)
            x1_free = [(vc, vc.n)]
        scores_done = vc.n
        e4pump(len(e4_steps))
        xn_store_tot = xnS.totals()
        shared_store_tot = stE.totals()
        tscat_tot = tscat.totals()
        esT.close()
        xg = sbE("xg", [128, 2, 2, D], BF16)
        xgT = sbE("xgT", [128, NK, 256], BF16)
        yr = sbE("yr", [128, 2, 2, D], F32)
        rtgt = sbE("rtgt", [128, 2, 2, 2], F32)
        tidx = sbE("tidx", [128, 2, 2, 1], I32)
        gvc = sbE("gvc", [128, 2, 2, 1], F32)
        ebtab = sbE("ebtab", [128, 32], F32)
        etab_bf = sbE("etab_bf", [1, 32], BF16)
        w13tf = sbE("w13tf", [128, 8, NT], F32)
        w2tf = sbE("w2tf", [128, 4, NT], F32)
        w13ti = sbE("w13ti", [128, 8, NT, 1], I32)
        w2ti = sbE("w2ti", [128, 4, NT, 1], I32)

        kb.cop("scalar", lambda e: e.copy(etab_bf[:], etab[0:1, 0:32]),
               [(vc, e4refs["etab_done"])])
        capb = kb.cop("tensor", lambda e: e.matmul(
            bkF.ap[:, 0:32], ones_sb[:], etab_bf[:], start=True, stop=True),
            [(sc, sc.n)] + list(bkF.free))
        kb.cop("vector", lambda e: e.tensor_copy(ebtab[:], bkF.ap[:, 0:32]),
               [(tc, capb), (vc, vc.n)])
        bkF.free = ((vc, vc.n),)
        kb.cop("vector", lambda e: e.tensor_scalar(
            w13tf[:, 0, :], ebtab[:, 0:NT], 1024.0, iota_sb[:, 0:1],
            op0=ALU.mult, op1=ALU.add), [(vc, vc.n)])
        for hh in range(1, 8):
            kb.cop("vector", lambda e, h_=hh: e.tensor_scalar(
                w13tf[:, h_, :], w13tf[:, 0, :], float(128 * h_), None,
                op0=ALU.add), [(vc, vc.n)])
        kb.cop("vector", lambda e: e.tensor_scalar(
            w2tf[:, 0, :], ebtab[:, 0:NT], 1.0, iota9_sb[:, 0:1],
            op0=ALU.mult, op1=ALU.add), [(vc, vc.n)])
        for d_ in range(1, 4):
            kb.cop("vector", lambda e, dd=d_: e.tensor_scalar(
                w2tf[:, dd, :], w2tf[:, 0, :], float(1152 * dd), None,
                op0=ALU.add), [(vc, vc.n)])
        kb.cop("vector", lambda e: e.tensor_copy(
            w13ti.rearrange("p h t o -> p (h t o)"),
            w13tf.rearrange("p h t -> p (h t)")), [(vc, vc.n)])
        kb.cop("vector", lambda e: e.tensor_copy(
            w2ti.rearrange("p d t o -> p (d t o)"),
            w2tf.rearrange("p d t -> p (d t)")), [(vc, vc.n)])
        tabs_done = vc.n

        bkEb = Bank(bkE.ap.bitcast(BF16))
        bkEb.free = bkE.free
        tpool = [bkT0, bkT1, bkEb]
        trot = [0]
        trpool = [bkC, bkD, bkF]
        kb.cop("vector", lambda e: e.memset(xg[:], 0.0), [(vc, vc.n)])
        xg_clear = vc.n
        prev_scat = []
        xg_free = {}
        rl_free = [(), ()]
        tok_free = [(), ()]
        yr_free = [(), ()]
        for gi in range(NT):
            gp = gi % 2
            rtoks = []
            for t in range(2):
                rtoks.append(rload.load("sync", 2 * gp + t,
                    lambda e, i=gi, tt=t, g=gp: e.dma_start(
                        out=rtgt[:, g, tt],
                        in_=rtg_s[GRP * i + 128 * tt:GRP * i + 128 * (tt + 1), :]),
                    (list(tscat_tot) + list(rl_free[gp])) if t == 0 else []))
            kb.cop("vector", lambda e, g=gp: e.tensor_copy(
                tidx[:, g, 0, :], rtgt[:, g, 0, 0:1]),
                [rtoks[0], rtoks[1], (vc, vc.n)] + list(tok_free[gp]))
            kb.cop("vector", lambda e, g=gp: e.tensor_copy(
                tidx[:, g, 1, :], rtgt[:, g, 1, 0:1]), [(vc, vc.n)])
            kb.cop("vector", lambda e, g=gp: e.tensor_copy(
                gvc[:, g, 0, :], rtgt[:, g, 0, 1:2]), [(vc, vc.n)])
            kb.cop("vector", lambda e, g=gp: e.tensor_copy(
                gvc[:, g, 1, :], rtgt[:, g, 1, 1:2]), [(vc, vc.n)])
            rl_free[gp] = [(vc, vc.n)]
            tcap = vc.n
            for t in range(2):
                gtokx = xgL.load("gpsimd", 2 * gp + t, lambda e, g=gp, tt=t: e.indirect_dma_start(
                    out=xg[:, g, tt], out_offset=None,
                    in_=xn2s_rows,
                    in_offset=IndirectOffsetOnAxis(ap=tidx[:, g, tt], axis=0),
                    bounds_check=bregs["tok"], oob_is_err=False),
                    ([(vc, tcap)] + list(xn_store_tot) + list(xg_free.get((gp, t), [(vc, xg_clear)]))))
                kb.cop("vector", lambda e, g=gp, tt=t: e.tensor_scalar(
                    xg[:, g, tt], xg[:, g, tt], gvc[:, g, tt, 0:1], None, op0=ALU.mult),
                    [gtokx, (vc, vc.n)])
                xsc = vc.n
                xg_free[(gp, t)] = [(vc, xsc)]
                for k in range(NK):
                    bk = tpool[trot[0] % 3]
                    trot[0] += 1
                    capk = kb.cop("tensor", lambda e, b=bk, kk=k, g=gp, tt=t: e.transpose(
                        b.ap[:, 0:128], xg[:, g, tt, 128 * kk:128 * (kk + 1)], idb[:]),
                        [(vc, xsc)] + list(bk.free))
                    kb.cop("vector", lambda e, b=bk, kk=k, tt=t: e.tensor_copy(
                        xgT[:, kk, 128 * tt:128 * (tt + 1)], b.ap[:, 0:128]),
                        [(tc, capk)])
                    bk.free = ((vc, vc.n),)
            xgT_done = vc.n
            for hc in range(8):
                ih = rot["ih"]
                rot["ih"] += 1
                wp = ih % 3
                tokw = wg13.load("gpsimd", wp, lambda e, p=wp, i=gi, hh=hc:
                                 e.indirect_dma_start(
                    out=w13buf[:, p], out_offset=None,
                    in_=w13s_rows,
                    in_offset=IndirectOffsetOnAxis(ap=w13ti[:, hh, i], axis=0),
                    bounds_check=bregs["w13"], oob_is_err=False),
                    [(vc, tabs_done)] + list(prep_tot) + list(w13_free[wp]))
                b1 = u1_banks[rot["u"] % 2]
                b3 = u3_banks[rot["u"] % 2]
                rot["u"] += 1
                for k in range(NK):
                    waits = ([tokw, (vc, xgT_done)] + list(b1.free)) if k == 0 else []
                    kb.cop("tensor", lambda e, b=b1, p=wp, kk=k: e.matmul(
                        b.ap[:, 0:GRP], w13buf[:, p, 128 * kk:128 * (kk + 1)],
                        xgT[:, kk, :],
                        start=(kk == 0), stop=(kk == NK - 1)),
                        waits, count=(k == NK - 1))
                cap1 = tc.n
                for k in range(NK):
                    waits = list(b3.free) if k == 0 else []
                    kb.cop("tensor", lambda e, b=b3, p=wp, kk=k: e.matmul(
                        b.ap[:, 0:GRP], w13buf[:, p, 2048 + 128 * kk:2048 + 128 * (kk + 1)],
                        xgT[:, kk, :],
                        start=(kk == 0), stop=(kk == NK - 1)),
                        waits, count=(k == NK - 1))
                cap3 = tc.n
                w13_free[wp] = [(tc, cap3)]
                hp = hc % 2
                kb.cop("scalar", lambda e, b=b1, p=hp: e.activation(
                    tmp1[:, p, 0:GRP], b.ap[:, 0:GRP], AF.Silu),
                    [(tc, cap1), (vc, vc.n), (sc, sc.n)])
                b1.free = ((sc, sc.n),)
                sil = sc.n
                kb.cop("vector", lambda e, b=b3, hh=hc, p=hp: e.tensor_mul(
                    h_sb[:, hh, 0:GRP], b.ap[:, 0:GRP], tmp1[:, p, 0:GRP]),
                    [(tc, cap3), (sc, sil), (vc, vc.n)])
                b3.free = ((vc, vc.n),)
            h_done = vc.n
            for dc in range(NK):
                wp = (dc // 4) % 2
                if dc % 4 == 0:
                    tokw2 = wg2.load("gpsimd", wp, lambda e, p=wp, i=gi, g_=dc // 4:
                                     e.indirect_dma_start(
                        out=w2buf[:, p], out_offset=None,
                        in_=w2s_rows,
                        in_offset=IndirectOffsetOnAxis(ap=w2ti[:, g_, i], axis=0),
                        bounds_check=bregs["w2"], oob_is_err=False),
                        [(vc, tabs_done)] + list(prep_tot) + list(w2_free[wp]))
                bk = acc_banks[rot["a"] % 2]
                rot["a"] += 1
                for hc in range(8):
                    waits = ([(vc, h_done), tokw2] + list(bk.free)) if hc == 0 else []
                    kb.cop("tensor", lambda e, b=bk, p=wp, hh=hc, d4=dc % 4: e.matmul(
                        b.ap[:, 0:GRP],
                        w2buf[:, p, 1024 * d4 + 128 * hh:1024 * d4 + 128 * (hh + 1)],
                        h_sb[:, hh, 0:GRP],
                        start=(hh == 0), stop=(hh == 7)),
                        waits, count=(hc == 7))
                cap = tc.n
                if dc % 4 == 3:
                    w2_free[wp] = [(tc, cap)]
                sp = dc % 2
                kb.cop("vector", lambda e, b=bk, p=sp: e.tensor_copy(
                    wsb[:, p, 0:GRP], b.ap[:, 0:GRP]), [(tc, cap)] + list(wsb_free[sp]))
                bk.free = ((vc, vc.n),)
                wsb_done = vc.n
                for t in range(2):
                    tb = trpool[rot["t"] % 3]
                    rot["t"] += 1
                    capt = kb.cop("tensor", lambda e, b=tb, p=sp, tt=t: e.transpose(
                        b.ap[:, 0:128], wsb[:, p, 128 * tt:128 * (tt + 1)], idf[:]),
                        [(vc, wsb_done)] + list(tb.free))
                    kb.cop("vector", lambda e, b=tb, tt=t, d_=dc, g=gp: e.tensor_copy(
                        yr[:, g, tt, 128 * d_:128 * (d_ + 1)], b.ap[:, 0:128]),
                        [(tc, capt)] + (list(yr_free[gp]) if dc == 0 else []))
                    tb.free = ((vc, vc.n),)
                wsb_free[sp] = [(tc, tc.n)]
            yr_done = vc.n
            for t in range(2):
                stok = stR.load("gpsimd", 0, lambda e, tt=t, g=gp: e.indirect_dma_start(
                    out=y_flat, out_offset=IndirectOffsetOnAxis(ap=tidx[:, g, tt], axis=0),
                    in_=yr[:, g, tt], in_offset=None, element_offset=s * T * D,
                    bounds_check=bregs["tok"], oob_is_err=False, compute_op=ALU.add),
                    [(vc, yr_done)] + list(shared_store_tot) + list(prev_scat))
                prev_scat = [stok]
            yr_free[gp] = [stok]
            tok_free[gp] = [stok]
        bkE.free = bkEb.free
        esE.close()
        return list(stE.totals()) + list(stR.totals())

    out_tots = []
    prep_closed = False
    for s in range(SEQ):
        esR = ExitStack()
        QlatT = esR.enter_context(nc.sbuf_tensor(f"QlatT{s}", [128, 8, T], BF16, side="right"))
        CkvT = esR.enter_context(nc.sbuf_tensor(f"CkvT{s}", [128, 4, T], BF16, side="right"))
        phase_A(s, QlatT, CkvT)
        kb.barrier()
        esCtx = ExitStack()
        ctxT = esCtx.enter_context(nc.sbuf_tensor(f"ctxT{s}", [128, NK, T], BF16))
        phase_B(s, QlatT, CkvT, ctxT)
        kb.barrier()
        esR.close()
        stC_tot = phase_C(s, ctxT)
        esCtx.close()
        if stage >= 2:
            if not prep_closed:
                pump(len(prep_jobs))
                es_prep.close()
                prep_closed = True
            prep_tot = prep_st.totals()
            kb.barrier()
            out_tots.append(phase_E(s, stC_tot, prep_tot))
            kb.barrier()
        else:
            out_tots.append(stC_tot)
            kb.barrier()
    if not prep_closed:
        pump(len(prep_jobs))
        es_prep.close()

    final_waits = [tok for tots in out_tots for tok in tots]
    kb.op("gpsimd", None, waits=final_waits)

    es_final = ExitStack()
    es_final.enter_context(es_outer)
    block = es_final.enter_context(nc.Block())
    kb.emit(nc, block, es_final)
    es_final.close()
    return nc


def _ln(kb, src_ap, xn_out, stat, sq, eps_ap, ld_tok, extra_waits=()):
    vc, scc = kb.ecnt["vector"], kb.ecnt["scalar"]
    w0 = [ld_tok, (vc, vc.n)] + list(extra_waits)
    kb.cop("vector", lambda e, a=src_ap, o=stat: e.reduce_sum(
        o[:, 0:1], a, axis=AX.X), w0)
    kb.cop("vector", lambda e, a=src_ap, o=stat, q=sq: e.scalar_tensor_tensor(
        q[:], a, 1.0, a, op0=ALU.mult, op1=ALU.mult, accum_out=o[:, 1:2]),
        [ld_tok])
    kb.cop("vector", lambda e, o=stat: e.tensor_scalar(
        o[:, 2:3], o[:, 0:1], -1.0 / D, None, op0=ALU.mult), [(vc, vc.n)])
    kb.cop("vector", lambda e, o=stat: e.tensor_mul(
        o[:, 3:4], o[:, 2:3], o[:, 2:3]), [(vc, vc.n)])
    kb.cop("vector", lambda e, o=stat: e.scalar_tensor_tensor(
        o[:, 4:5], o[:, 1:2], 1.0 / D, o[:, 3:4], op0=ALU.mult, op1=ALU.subtract),
        [(vc, vc.n)])
    cv = vc.n
    kb.cop("scalar", lambda e, o=stat, b_=eps_ap: e.activation(
        o[:, 5:6], o[:, 4:5], AF.Sqrt, bias=b_), [(vc, cv)])
    kb.cop("vector", lambda e, o=stat: e.reciprocal(o[:, 6:7], o[:, 5:6]),
           [(scc, scc.n)])
    kb.cop("vector", lambda e, a=src_ap, o=stat, x=xn_out: e.tensor_scalar(
        x[:], a, o[:, 2:3], o[:, 6:7], op0=ALU.add, op1=ALU.mult), [(vc, vc.n)])
    return vc.n


def _pack(inputs):
    f32 = np.float32
    asc = np.ascontiguousarray
    wqd = np.asarray(inputs["L0_attn_Wq_down"], f32)
    wkvd = np.asarray(inputs["L0_attn_Wkv_down"], f32)
    wo = np.asarray(inputs["L0_attn_Wo"], f32)
    gate = np.asarray(inputs["L0_ffn_gate"], f32)
    bo = np.asarray(inputs["L0_attn_bo"], f32)
    eb = np.asarray(inputs["L0_ffn_expert_bias"], f32)
    w1 = np.asarray(inputs["L0_ffn_w1"], f32)
    w2 = np.asarray(inputs["L0_ffn_w2"], f32)
    w3 = np.asarray(inputs["L0_ffn_w3"], f32)
    sw1 = np.asarray(inputs["L0_ffn_shared_w1"], f32)
    sw2 = np.asarray(inputs["L0_ffn_shared_w2"], f32)
    sw3 = np.asarray(inputs["L0_ffn_shared_w3"], f32)

    d = {}
    d["wqdT"] = asc(wqd.reshape(RQ, NK, 128).transpose(2, 1, 0))
    d["wkvdT"] = asc(wkvd.reshape(RK, NK, 128).transpose(2, 1, 0))
    d["woT"] = asc(wo.reshape(D, NK, 128).transpose(2, 1, 0))
    d["wqu"] = asc(np.asarray(inputs["L0_attn_Wq_up"], f32))
    d["wku"] = asc(np.asarray(inputs["L0_attn_Wk_up"], f32))
    d["wvu"] = asc(np.asarray(inputs["L0_attn_Wv_up"], f32))
    d["bo_p"] = asc(bo.reshape(NK, 128).T)
    d["gateT_p"] = asc(gate.T.reshape(NK, 128, 8).transpose(1, 0, 2))
    d["eb_p"] = asc(np.broadcast_to(eb, (128, 8)))
    w1all = np.concatenate([w1, sw1[None]], axis=0)
    w3all = np.concatenate([w3, sw3[None]], axis=0)
    w2all = np.concatenate([w2, sw2[None]], axis=0)
    w1p = w1all.reshape(NE, 8, 128, NK, 128).transpose(0, 1, 4, 3, 2)
    w3p = w3all.reshape(NE, 8, 128, NK, 128).transpose(0, 1, 4, 3, 2)
    d["w13p"] = asc(np.stack([w1p, w3p], axis=3))
    d["w2p"] = asc(w2all.reshape(NE, 4, 4, 128, 8, 128).transpose(1, 5, 0, 2, 4, 3))
    d["idf"] = np.eye(128, dtype=f32)
    d["idb"] = np.eye(128, dtype=f32).astype(BF)
    jj, ii = np.meshgrid(np.arange(128), np.arange(128))
    d["maskb"] = np.where(jj > ii, np.float32(NEGM), np.float32(0.0)).astype(BF)
    d["iota_in"] = np.arange(128, dtype=f32)[:, None]
    d["grprow"] = (np.arange(32, dtype=f32) * GRP)[None, :]
    ii2, jj2 = np.meshgrid(np.arange(128), np.arange(128), indexing="ij")
    d["ltb"] = (ii2 <= jj2).astype(f32).astype(BF)
    d["onesm"] = np.ones((128, 128), f32).astype(BF)
    return d


_NC_CACHE = {}


def _get_nc(stage=2):
    if stage not in _NC_CACHE:
        _NC_CACHE[stage] = build_program(stage)
    return _NC_CACHE[stage]


def _run(inputs, stage=2):
    shared = _pack(inputs)
    x = np.asarray(inputs["L0_attn_x"], np.float32)
    in_maps = []
    for c in range(NCORE):
        m = dict(shared)
        m["x"] = np.ascontiguousarray(x[SEQ * c:SEQ * (c + 1)])
        in_maps.append(m)
    nc = _get_nc(stage)
    res = run_bass_kernel_spmd(nc, in_maps, core_ids=list(range(NCORE)))
    out = np.concatenate([np.asarray(r["y"]) for r in res.results], axis=0)
    return out


def kernel(L0_attn_x, L0_attn_Wq_down, L0_attn_Wkv_down, L0_attn_Wq_up,
           L0_attn_Wk_up, L0_attn_Wv_up, L0_attn_Wo, L0_attn_bo,
           L0_ffn_gate, L0_ffn_expert_bias, L0_ffn_w1, L0_ffn_w2, L0_ffn_w3,
           L0_ffn_shared_w1, L0_ffn_shared_w2, L0_ffn_shared_w3):
    inputs = dict(
        L0_attn_x=L0_attn_x, L0_attn_Wq_down=L0_attn_Wq_down,
        L0_attn_Wkv_down=L0_attn_Wkv_down, L0_attn_Wq_up=L0_attn_Wq_up,
        L0_attn_Wk_up=L0_attn_Wk_up, L0_attn_Wv_up=L0_attn_Wv_up,
        L0_attn_Wo=L0_attn_Wo, L0_attn_bo=L0_attn_bo, L0_ffn_gate=L0_ffn_gate,
        L0_ffn_expert_bias=L0_ffn_expert_bias, L0_ffn_w1=L0_ffn_w1,
        L0_ffn_w2=L0_ffn_w2, L0_ffn_w3=L0_ffn_w3,
        L0_ffn_shared_w1=L0_ffn_shared_w1, L0_ffn_shared_w2=L0_ffn_shared_w2,
        L0_ffn_shared_w3=L0_ffn_shared_w3)
    return _run(inputs, stage=2)
